# version 14
# baseline (speedup 1.0000x reference)
import math
from contextlib import ExitStack

import numpy as np
import ml_dtypes

import concourse.bass as bass
import concourse.mybir as mybir
from concourse.bass_utils import run_bass_kernel_spmd

F32 = mybir.dt.float32
BF16 = mybir.dt.bfloat16
AF = mybir.ActivationFunctionType
ALU = mybir.AluOpType
AX = mybir.AxisListType

T = 4096
D = 1024
NSLAB = 8
NTILE = 32
PI = math.pi

BF = np.dtype(ml_dtypes.bfloat16)


CHAIN_SAME_ENGINE = True


class _EngProxy:

    def __init__(self, kb, eng, sem_name):
        self._kb = kb
        self._eng = eng
        self._sem = sem_name

    def wait_ge(self, semh, val):
        return self._eng.wait_ge(semh, val)

    def __getattr__(self, m):
        f = getattr(self._eng, m)
        kb = self._kb
        sem = self._sem

        def call(*a, **kw):
            ent = kb.sem(sem)
            if CHAIN_SAME_ENGINE and ent[1] > 0:
                self._eng.wait_ge(ent[0], ent[1])
            ins = f(*a, **kw)
            kb.inc(sem, ins)
            kb._last_instr = ins
            kb._last_sem = sem
            return ins
        return call


class KB:
    def __init__(self, nc, ctx):
        self.nc = nc
        self.ctx = ctx
        self.t = nc.tensor
        self.v = nc.vector
        self.s = nc.scalar
        self.g = nc.gpsimd
        self.sp = nc.sync
        self._sems = {}
        self.buf = {}
        self.V = _EngProxy(self, nc.vector, "cv")
        self.S = _EngProxy(self, nc.scalar, "cs")
        self.G = _EngProxy(self, nc.gpsimd, "cg")

    def sem(self, name):
        if name not in self._sems:
            h = self.ctx.enter_context(self.nc.semaphore(name))
            self._sems[name] = [h, 0]
        return self._sems[name]

    _last_instr = None
    _last_sem = None

    def inc(self, name, instr, k=1):
        ent = self.sem(name)
        if instr is self._last_instr and name == self._last_sem:
            return ent[1]
        instr.then_inc(ent[0], k)
        ent[1] += k
        return ent[1]

    def val(self, name):
        return self.sem(name)[1]

    def wait(self, eng, name, val=None):
        ent = self.sem(name)
        eng.wait_ge(ent[0], ent[1] if val is None else val)

    def barrier(self):
        for eng in (self.t, self.v, self.s, self.g, self.sp):
            for snm in ("ct", "cv", "cs", "cg"):
                if snm in self._sems and self._sems[snm][1] > 0:
                    eng.wait_ge(self._sems[snm][0], self._sems[snm][1])

    def sbuf(self, name, shape, dtype):
        h = self.ctx.enter_context(self.nc.sbuf_tensor(name, shape, dtype))
        self.buf[name] = h
        return h

    def psum(self, name, shape, dtype=F32):
        h = self.ctx.enter_context(self.nc.psum_tensor(name, shape, dtype))
        self.buf[name] = h
        return h



def emit_load_xt(k, src_ap):
    nc = k.nc
    XT, xld, xbf, id_bf = k.buf["XT"], k.buf["xld"], k.buf["xbf"], k.buf["id_bf"]
    pT = k.buf["pT"]
    k.wait(k.t, "d_cst", k.val("d_cst"))
    for i in range(NTILE):
        ld = k.inc("d_x", k.sp.dma_start(out=xld[:], in_=src_ap[i * 128:(i + 1) * 128, :]), 16)
        k.wait(k.S, "d_x", ld)
        k.wait(k.S, "ct")
        cs = k.inc("cs", k.S.copy(out=xbf[:], in_=xld[:]))
        k.wait(k.t, "cs", cs)
        tseq(k, [(xbf[:, s * 128:(s + 1) * 128], XT[:, s * T + i * 128: s * T + (i + 1) * 128])
                 for s in range(NSLAB)])
        k.wait(k.sp, "cs", cs)


def emit_ln_tile(k, x_f32_ap, out_f32_ap, eps=1e-5, gain_ap=None, bias_ap=None):
    v, s = k.V, k.S
    W = x_f32_ap.shape[-1]
    sm = k.buf["sm"]
    sq = k.buf["sq"]
    cv = k.inc("cv", v.reduce_sum(out=sm[:, 0:1], in_=x_f32_ap, axis=AX.X))
    k.wait(s, "cv", cv)
    cs = k.inc("cs", s.activation(out=sq[:, 0:W], in_=x_f32_ap, func=AF.Square, accum_out=sm[:, 1:2]))
    k.wait(v, "cs", cs)
    v.tensor_scalar(out=sm[:, 2:3], in0=sm[:, 0:1], scalar1=-1.0 / W, scalar2=None, op0=ALU.mult)
    v.tensor_tensor(out=sm[:, 3:4], in0=sm[:, 2:3], in1=sm[:, 2:3], op=ALU.mult)
    v.tensor_scalar(out=sm[:, 4:5], in0=sm[:, 1:2], scalar1=1.0 / W, scalar2=float(eps), op0=ALU.mult, op1=ALU.add)
    cv = k.inc("cv", v.tensor_tensor(out=sm[:, 5:6], in0=sm[:, 4:5], in1=sm[:, 3:4], op=ALU.subtract))
    k.wait(s, "cv", cv)
    s.activation(out=sm[:, 6:7], in_=sm[:, 5:6], func=AF.Ln)
    cs = k.inc("cs", s.activation(out=sm[:, 7:8], in_=sm[:, 6:7], func=AF.Exp, scale=-0.5))
    k.wait(v, "cs", cs)
    cv = k.inc("cv", v.tensor_scalar(out=out_f32_ap, in0=x_f32_ap, scalar1=sm[:, 2:3],
                                     scalar2=sm[:, 7:8], op0=ALU.add, op1=ALU.mult))
    return cv


def tseq(k, jobs, f32=False):
    t, v = k.t, k.V
    hist = []
    for i, (sap, dap) in enumerate(jobs):
        sp_, sf = sap.shape[-2], sap.shape[-1]
        bank = k.buf["pTB"] if i % 2 == 0 else k.buf["pTB2"]
        ps = bank[0:sf, 0:sp_]
        idn = k.buf["id_bf"][0:sp_, 0:sp_]
        if len(hist) >= 2:
            k.wait(t, "cv", hist[-2])
        ct = k.inc("ct", t.transpose(ps, sap, idn))
        k.wait(v, "ct", ct)
        cv = k.inc("cv", v.tensor_copy(out=dap, in_=ps))
        hist.append(cv)
    k.wait(t, "cv", hist[-1])


def emit_w_bf16_stage(k, W, blocks):
    g, sp = k.g, k.sp
    AA = k.buf["AA"]
    stages = [AA[:, 0:8192], AA[:, 8192:16384]]
    units = []
    if 1 in blocks:
        for e in range(8):
            for grp in range(4):
                units.append(("w1t", "w1bf", e * 4 + grp))
        for e in range(8):
            for grp in range(4):
                units.append(("w2T", "w2bf", e * 4 + grp))
    if 2 in blocks:
        for nm in ("wqt", "wkt", "wvt", "w1at", "wot"):
            units.append((nm, nm + "bf", 0))
    if 3 in blocks:
        units.append(("enct", "encbf", 0))
        units.append(("enct", "encbf", 1))
        units.append(("dec_W", "decbf", 0))
        units.append(("dec_W", "decbf", 1))
    out_done = [0, 0]
    for i, (kind, dstname, grp) in enumerate(units):
        sl_ = i % 2
        sem_in = f"d_cvi{sl_}"
        if out_done[sl_]:
            k.wait(g, f"d_cvo{sl_}", out_done[sl_])
        if kind == "w1t":
            srcap = W["w1t"][grp // 4, (grp % 4) * 8:(grp % 4) * 8 + 8].rearrange("f p s c -> p f (s c)")
            dstap = W[dstname][grp // 4, (grp % 4) * 8:(grp % 4) * 8 + 8].rearrange("f p c -> p f c")
        elif kind == "w2T":
            srcap = W["w2T"][grp // 4, (grp % 4) * 1024:(grp % 4) * 1024 + 1024, :].rearrange("(f p) c -> p f c", p=128)
            dstap = W[dstname][grp // 4, (grp % 4) * 8:(grp % 4) * 8 + 8].rearrange("f p c -> p f c")
        elif kind == "enct":
            srcap = W["enct"][grp * 8:(grp + 1) * 8].rearrange("f p s c -> p f (s c)")
            dstap = W[dstname][grp * 8:(grp + 1) * 8].rearrange("f p c -> p f c")
        elif kind == "dec_W":
            srcap = W["dec_W"][grp * 1024:(grp + 1) * 1024, :].rearrange("(f p) c -> p f c", p=128)
            dstap = W[dstname][grp * 8:(grp + 1) * 8].rearrange("f p c -> p f c")
        else:
            srcap = W[kind].rearrange("f p s c -> p f (s c)")
            dstap = W[dstname].rearrange("f p c -> p f c")
        st3 = stages[sl_].rearrange("p (f c) -> p f c", f=8)
        di = k.inc(sem_in, g.dma_start(out=st3, in_=srcap), 16)
        k.wait(sp, sem_in, di)
        out_done[sl_] = k.inc(f"d_cvo{sl_}", sp.dma_start(out=dstap, in_=st3), 16)
    k._wconv_done = (k.val("d_cvo0"), k.val("d_cvo1"))



def emit_l0_attn(k, src_ap, dst_ap, W):
    nc = k.nc
    t, v, s, g, sp = k.t, k.V, k.S, k.g, k.sp
    XT = k.buf["XT"]
    BIG = k.buf["BIG"]
    id_bf = k.buf["id_bf"]
    pT = k.buf["pT"]
    pT2 = k.buf["pT2"]
    pA, pB, pC, pD = k.buf["pA"], k.buf["pB"], k.buf["pC"], k.buf["pD"]
    pS = k.buf["pS"]
    tmp = k.buf["sq"][:, 0:512]
    tmp2 = k.buf["sq"][:, 512:1024]
    tfull = k.buf["tfull"]
    xld = k.buf["xld"]
    xbf = k.buf["xbf"]
    sm = k.buf["sm"]
    sm2 = k.buf["sm2"]
    rowbuf = k.buf["rowbuf"]
    Pbf = k.buf["Pbf"]
    PT = Pbf
    wbig = k.buf["wbig"]
    wsmall = k.buf["wsmall"]
    omg_bf = k.buf["omg_bf"]
    biasb = k.buf["biasb"]
    cst = k.buf["cst"]
    B4 = k.buf["B4"]

    wsm_f = k.buf["wsmall_f32"]
    if getattr(k, "_wconv_done", None):
        for eng in (g, sp, v, t, s):
            k.wait(eng, "d_cvo0", k._wconv_done[0])
            k.wait(eng, "d_cvo1", k._wconv_done[1])
        k._wconv_done = None
    for sl in range(NSLAB):
        k.inc("d_wg", g.dma_start(out=wbig[:, sl * 1024:(sl + 1) * 1024], in_=W["wvT"][sl * 128:(sl + 1) * 128, :]), 16)
        k.inc("d_wg", g.dma_start(out=wsmall[:, sl * 64:(sl + 1) * 64], in_=W["wshT"][sl * 128:(sl + 1) * 128, :]), 16)
        k.inc("d_wg", g.dma_start(out=wsmall[:, 512 + sl:513 + sl], in_=W["wchT"][sl * 128:(sl + 1) * 128, :]), 16)
    k.inc("d_wg", g.dma_start(out=omg_bf[0:64, :], in_=W["omgT"]), 16)
    k.inc("d_cst", sp.dma_start(out=wsm_f[:, 0:64], in_=W["wcmT"][0:128, :]), 16)
    k.inc("d_cst", sp.dma_start(out=wsm_f[:, 64:128], in_=W["wcmT"][128:256, :]), 16)
    k.inc("d_cst", sp.dma_start(out=wsm_f[0:64, 128:192], in_=W["wcmT"][256:320, :]), 16)
    k.inc("d_cst", sp.dma_start(out=biasb[:, 0:1024], in_=W["r_bv"]), 16)
    k.inc("d_cst", sp.dma_start(out=biasb[:, 1024:2048], in_=W["r_bout"]), 16)
    k.inc("d_cst", sp.dma_start(out=biasb[:, 2048:2112], in_=W["r_bsh"]), 16)
    k.inc("d_cst", sp.dma_start(out=biasb[:, 2112:2144], in_=W["r_rffb"]), 16)
    k.inc("d_cst", sp.dma_start(out=biasb[:, 2144:2145], in_=W["r_bch"]), 16)
    k.inc("d_cst", sp.dma_start(out=biasb[:, 2176:2177], in_=W["r_lt"]), 16)
    k.inc("d_cst", sp.dma_start(out=cst[:, 0:512], in_=W["c_brow"]), 16)
    k.inc("d_cst", sp.dma_start(out=cst[:, 512:513], in_=W["c_colb"]), 16)
    k.inc("d_cst", sp.dma_start(out=cst[:, 513:516], in_=W["c_mags"]), 16)
    k.inc("d_wg", g.dma_start(out=B4[:], in_=W["c_B4"]), 16)
    dg0, dsp0 = k.val("d_wg"), k.val("d_cst")

    k.wait(v, "d_wg", dg0)
    k.wait(v, "d_cst", dsp0)
    v.memset(wsm_f[64:128, 128:192], 0.0)
    v.memset(k.buf["tp_bf"][:, 320:384], 0.0)
    for c3 in range(3):
        k.inc("cv", v.tensor_scalar(out=k.buf["wcm_bf"][:, c3 * 64:(c3 + 1) * 64],
                                    in0=wsm_f[:, c3 * 64:(c3 + 1) * 64],
                                    scalar1=cst[:, 513 + c3:514 + c3], scalar2=None, op0=ALU.mult))
    k.wait(s, "d_cst", dsp0)
    k.wait(s, "d_wg", dg0)
    lt = biasb[:, 2176:2177]
    cv0 = k.inc("cv", v.scalar_tensor_tensor(out=sm2[:, 0:1], in0=lt, scalar=-1.0, in1=lt,
                                             op0=ALU.mult, op1=ALU.max))
    k.wait(s, "cv", cv0)
    s.activation(out=sm2[:, 1:2], in_=sm2[:, 0:1], func=AF.Exp, scale=-1.0)
    cs = k.inc("cs", s.activation(out=sm2[:, 2:3], in_=sm2[:, 1:2], func=AF.Ln, bias=1.0))
    k.wait(v, "cs", cs)
    v.tensor_scalar(out=sm2[:, 3:4], in0=lt, scalar1=0.0, scalar2=None, op0=ALU.max)
    v.tensor_tensor(out=sm2[:, 4:5], in0=sm2[:, 3:4], in1=sm2[:, 2:3], op=ALU.add)
    v.tensor_scalar(out=sm2[:, 5:6], in0=sm2[:, 4:5], scalar1=0.1, scalar2=None, op0=ALU.add)
    v.reciprocal(out=sm2[:, 6:7], in_=sm2[:, 5:6])
    cv = k.inc("cv", v.tensor_tensor(out=sm2[:, 7:8], in0=cst[:, 512:513], in1=sm2[:, 6:7], op=ALU.mult))

    wv = wbig
    k.wait(t, "cv", cv)
    for i in range(NTILE):
        for half in range(2):
            pp = pA if half == 0 else pB
            for sl in range(NSLAB):
                mm = t.matmul(pp[:], XT[:, sl * T + i * 128:sl * T + (i + 1) * 128],
                              wv[:, sl * 1024 + half * 512: sl * 1024 + (half + 1) * 512],
                              start=(sl == 0), stop=(sl == NSLAB - 1))
            ct = k.inc("ct", mm)
            k.wait(v, "ct", ct)
            cv = k.inc("cv", v.tensor_tensor(out=BIG[:, i * 1024 + half * 512: i * 1024 + (half + 1) * 512],
                                             in0=pp[:], in1=biasb[:, half * 512:(half + 1) * 512], op=ALU.add))
        k.wait(t, "cv", cv)

    wsh = _bfview(k, "wsh")
    wch = _bfview(k, "wch")
    wcm = k.buf["wcm_bf"]
    for i in range(NTILE):
        for sl in range(NSLAB):
            mmq = t.matmul(pD[:, 0:1], XT[:, sl * T + i * 128:sl * T + (i + 1) * 128],
                           wch[:, sl:sl + 1], start=(sl == 0), stop=(sl == NSLAB - 1))
        ct_q = k.inc("ct", mmq)
        for sl in range(NSLAB):
            mms = t.matmul(pS[:, 1:65], XT[:, sl * T + i * 128:sl * T + (i + 1) * 128],
                           wsh[:, sl * 64:(sl + 1) * 64], start=(sl == 0), stop=(sl == NSLAB - 1))
        ct_sh = k.inc("ct", mms)
        k.wait(s, "ct", ct_q)
        cs = k.inc("cs", s.activation(out=sm2[:, 8:9], in_=pD[:, 0:1], func=AF.Tanh, bias=biasb[:, 2144:2145]))
        k.wait(v, "cs", cs)
        q = sm2[:, 8:9]
        v.memset(sm2[:, 9:10], 1.0)
        v.tensor_tensor(out=sm2[:, 10:11], in0=q, in1=q, op=ALU.mult)
        v.tensor_tensor(out=sm2[:, 11:12], in0=sm2[:, 10:11], in1=q, op=ALU.mult)
        cv = k.inc("cv", v.tensor_tensor(out=sm2[:, 12:13], in0=sm2[:, 10:11], in1=sm2[:, 10:11], op=ALU.mult))
        k.wait(v, "ct", ct_sh)
        sh = k.buf["sh64"]
        cv = k.inc("cv", v.tensor_tensor(out=sh[:, 0:64], in0=pS[:, 1:65],
                                         in1=biasb[:, 2048:2112], op=ALU.add))
        k.wait(s, "cv", cv)
        cs = k.inc("cs", s.activation(out=sh[:, 128:192], in_=sh[:, 0:64], func=AF.Square, accum_out=sm2[:, 14:15]))
        k.wait(v, "cs", cs)
        cv = k.inc("cv", v.tensor_scalar(out=sm2[:, 15:16], in0=sm2[:, 14:15], scalar1=1e-24, scalar2=None, op0=ALU.max))
        k.wait(s, "cv", cv)
        s.activation(out=sm2[:, 17:18], in_=sm2[:, 15:16], func=AF.Ln)
        cs = k.inc("cs", s.activation(out=sm2[:, 16:17], in_=sm2[:, 17:18], func=AF.Exp, scale=-0.5))
        k.wait(v, "cs", cs)
        shn = k.buf["shn_bf"]
        cv = k.inc("cv", v.tensor_scalar(out=shn[:], in0=sh[:, 0:64], scalar1=sm2[:, 16:17],
                                         scalar2=None, op0=ALU.mult))
        k.wait(t, "cv", cv)
        ct = k.inc("ct", t.transpose(pT[0:64, 0:128], shn[:], id_bf[:]))
        k.wait(v, "ct", ct)
        shnT = k.buf["shnT_bf"]
        cv = k.inc("cv", v.tensor_copy(out=shnT[0:64, :], in_=pT[0:64, 0:128]))
        k.wait(t, "cv", cv)
        ct = k.inc("ct", t.matmul(pS[:, 1:33], shnT[0:64, :], omg_bf[0:64, :], start=True, stop=True))
        k.wait(v, "ct", ct)
        pj = k.buf["pj"]
        v.tensor_tensor(out=pj[:, 0:32], in0=pS[:, 1:33], in1=biasb[:, 2112:2144], op=ALU.add)
        cv = k.inc("cv", v.tensor_scalar(out=pj[:, 32:64], in0=pj[:, 0:32], scalar1=PI / 2, scalar2=None, op0=ALU.add))
        for it in range(2):
            for colofs in (0, 32):
                w_ap = pj[:, colofs:colofs + 32]
                v.tensor_scalar(out=pj[:, 64 + colofs:96 + colofs], in0=w_ap, scalar1=PI, scalar2=-2 * PI,
                                op0=ALU.is_gt, op1=ALU.mult)
                v.tensor_tensor(out=w_ap, in0=w_ap, in1=pj[:, 64 + colofs:96 + colofs], op=ALU.add)
                v.tensor_scalar(out=pj[:, 64 + colofs:96 + colofs], in0=w_ap, scalar1=-PI, scalar2=2 * PI,
                                op0=ALU.is_lt, op1=ALU.mult)
                cv = k.inc("cv", v.tensor_tensor(out=w_ap, in0=w_ap, in1=pj[:, 64 + colofs:96 + colofs], op=ALU.add))
        k.wait(s, "cv", cv)
        phis = k.buf["phis_bf"]
        s.activation(out=phis[:, 32:64], in_=pj[:, 0:32], func=AF.Sin)
        cs = k.inc("cs", s.activation(out=phis[:, 0:32], in_=pj[:, 32:64], func=AF.Sin))
        k.wait(v, "cs", cs)
        tp_bf = k.buf["tp_bf"]
        phic_cols = [sm2[:, 9:10], sm2[:, 8:9], sm2[:, 10:11], sm2[:, 11:12], sm2[:, 12:13]]
        for c in range(5):
            cv = k.inc("cv", v.tensor_scalar(out=tp_bf[:, c * 64:(c + 1) * 64], in0=phis[:],
                                             scalar1=phic_cols[c], scalar2=None, op0=ALU.mult))
        k.wait(t, "cv", cv)
        tpT = k.buf["tpT_bf"]
        for c3 in range(3):
            ct = k.inc("ct", t.transpose(pT[:], tp_bf[:, c3 * 128:(c3 + 1) * 128], id_bf[:]))
            k.wait(v, "ct", ct)
            cv = k.inc("cv", v.tensor_copy(out=tpT[:, c3 * 128:(c3 + 1) * 128], in_=pT[:]))
            k.wait(t, "cv", cv)
        k.wait(t, "cv", cv)
        for c3 in range(3):
            kk = 128 if c3 < 2 else 64
            mmp = t.matmul(k.buf["pPhi"][0:64, 0:128],
                           wcm[0:kk, c3 * 64:(c3 + 1) * 64],
                           tpT[0:kk, c3 * 128:(c3 + 1) * 128],
                           start=(c3 == 0), stop=(c3 == 2))
        ct = k.inc("ct", mmp)
        k.wait(v, "ct", ct)
        cv = k.inc("cv", v.tensor_copy(out=phiap(k, i * 128, 128), in_=k.buf["pPhi"][0:64, 0:128]))
        k.wait(t, "cv", cv)

    k.wait(g, "ct", k.val("ct"))
    k.wait(g, "cv", k.val("cv"))
    for sl in range(NSLAB):
        dg = k.inc("d_wo", g.dma_start(out=wbig[:, sl * 1024:(sl + 1) * 1024], in_=W["woT"][sl * 128:(sl + 1) * 128, :]), 16)

    brow = cst[:, 0:512]
    for i in range(NTILE):
        jmax = (i * 128 + 127) // 512
        pa_cv = getattr(k, "_pa_cv", {0: 0, 1: 0})
        k._pa_cv = pa_cv
        for j in range(jmax + 1):
            pSb = pC if j % 2 == 0 else pD
            if pa_cv[j % 2]:
                k.wait(t, "cv", pa_cv[j % 2])
            ct = k.inc("ct", t.matmul(pSb[:], phiap(k, i * 128, 128),
                                      phiap(k, j * 512, 512), start=True, stop=True))
            k.wait(v, "ct", ct)
            v.tensor_tensor(out=tmp[:], in0=pSb[:], in1=brow, op=ALU.mult)
            delta = i * 128 - j * 512
            sc = math.exp(-0.03 * delta) if delta < 500 else 0.0
            v.tensor_scalar(out=sm2[:, 20:21], in0=sm2[:, 7:8], scalar1=float(sc), scalar2=None, op0=ALU.mult)
            if j == jmax and delta < 512:
                v.tensor_scalar(out=tmp2[:], in0=tmp[:], scalar1=sm2[:, 20:21], scalar2=None, op0=ALU.mult)
                cv = k.inc("cv", v.tensor_tensor(out=rowbuf[:, j * 512:(j + 1) * 512], in0=tmp2[:],
                                                 in1=B4[:, (delta // 128) * 512:(delta // 128 + 1) * 512], op=ALU.add))
            else:
                cv = k.inc("cv", v.tensor_scalar(out=rowbuf[:, j * 512:(j + 1) * 512], in0=tmp[:],
                                                 scalar1=sm2[:, 20:21], scalar2=None, op0=ALU.mult))
            pa_cv[j % 2] = k.val("cv")
        wrow = (jmax + 1) * 512
        v.reduce_max(out=sm2[:, 22:23], in_=rowbuf[:, 0:wrow], axis=AX.X)
        cv = k.inc("cv", v.tensor_scalar(out=sm2[:, 23:24], in0=sm2[:, 22:23], scalar1=-1.0, scalar2=None, op0=ALU.mult))
        k.wait(s, "cv", cv)
        cs = k.inc("cs", s.activation(out=Pbf[:, 0:wrow], in_=rowbuf[:, 0:wrow],
                                      func=AF.Exp, bias=sm2[:, 23:24], accum_out=sm2[:, 33:34]))
        k.wait(v, "cs", cs)
        cv = k.inc("cv", v.reciprocal(out=sm2[:, 34:35], in_=sm2[:, 33:34]))
        k.wait(t, "cv", k.val("cv"))
        k.wait(t, "cs", k.val("cs"))
        tseq(k, [(Pbf[:, mb * 128:(mb + 1) * 128], PT[:, mb * 128:(mb + 1) * 128])
                 for mb in range(i + 1)])
        for half in range(2):
            pp = pA if half == 0 else pB
            for mb in range(i + 1):
                mm = t.matmul(pp[:], PT[:, mb * 128:(mb + 1) * 128],
                              BIG[:, mb * 1024 + half * 512: mb * 1024 + (half + 1) * 512],
                              start=(mb == 0), stop=(mb == i))
            k.inc("ct", mm)
        ct = k.val("ct")
        ao = xbf
        k.wait(v, "ct", ct)
        for half in range(2):
            pp = pA if half == 0 else pB
            cv = k.inc("cv", v.tensor_scalar(out=ao[:, half * 512:(half + 1) * 512], in0=pp[:],
                                             scalar1=sm2[:, 34:35], scalar2=None, op0=ALU.mult))
        if i == 0:
            k.wait(t, "d_wo", dg)
        aofm = k.buf["aofm"]
        k.wait(t, "cv", cv)
        tseq(k, [(ao[:, sl * 128:(sl + 1) * 128], aofm[:, sl * 128:(sl + 1) * 128])
                 for sl in range(NSLAB)])
        ld = k.inc("d_x", sp.dma_start(out=xld[:], in_=src_ap[i * 128:(i + 1) * 128, :]), 16)
        for half in range(2):
            pp = pC if half == 0 else pD
            for sl in range(NSLAB):
                mm = t.matmul(pp[:], aofm[:, sl * 128:(sl + 1) * 128],
                              wbig[:, sl * 1024 + half * 512: sl * 1024 + (half + 1) * 512],
                              start=(sl == 0), stop=(sl == NSLAB - 1))
            k.inc("ct", mm)
        ct = k.val("ct")
        k.wait(v, "ct", ct)
        k.wait(v, "d_x", k.val("d_x"))
        for half in range(2):
            pp = pC if half == 0 else pD
            v.tensor_tensor(out=tmp[:], in0=pp[:], in1=biasb[:, 1024 + half * 512:1024 + (half + 1) * 512], op=ALU.add)
            cv = k.inc("cv", v.tensor_tensor(out=tfull[:, half * 512:(half + 1) * 512], in0=tmp[:],
                                             in1=xld[:, half * 512:(half + 1) * 512], op=ALU.add))
        cv = emit_ln_tile(k, tfull[:, 0:1024], xld[:, 0:1024])
        k.wait(g, "cv", cv)
        k.inc("d_st", g.dma_start(out=dst_ap[i * 128:(i + 1) * 128, :], in_=xld[:]), 16)
        k.wait(s, "cv", cv)
        cs = k.inc("cs", s.copy(out=xbf[:], in_=xld[:]))
        k.wait(t, "cs", cs)
        tseq(k, [(xbf[:, sl * 128:(sl + 1) * 128], XT[:, sl * T + i * 128:sl * T + (i + 1) * 128])
                 for sl in range(NSLAB)])
        k.wait(sp, "d_st", k.val("d_st"))
        k.wait(sp, "cv", k.val("cv"))
        k.wait(sp, "cs", k.val("cs"))


def phiap(k, col, width):
    return k.buf["phi"][0:64, col:col + width]


def _wcm_f32_view(k, c3):
    wsm = k.buf["wsmall_f32"]
    return wsm[:, c3 * 64:(c3 + 1) * 64]


def _bfview(k, which):
    if which == "wsh":
        return k.buf["wsmall"][:, 0:512]
    if which == "wch":
        return k.buf["wsmall"][:, 512:520]
    raise KeyError(which)



def emit_l0_ffn(k, src_ap, dst_ap, W):
    t, v, s, g, sp = k.t, k.V, k.S, k.g, k.sp
    b = k.buf
    XT = k.buf["XT"]
    w2buf, h1buf, w1buf = k.buf["w2buf"], k.buf["h1buf"], k.buf["w1buf"]
    b2buf, b1c = k.buf["b2buf"], k.buf["b1c"]
    acc = k.buf["acc"]
    smA, smB, w_tm, rwT = k.buf["sm3"], k.buf["sm4"], k.buf["w_tm"], k.buf["rwT_f32"]
    xT32 = k.buf["xT32"]
    xbf2 = k.buf["xbf2"]
    idf = k.buf["id_f32"]
    id_bf = k.buf["id_bf"]
    pA, pB, pC, pD = k.buf["pA"], k.buf["pB"], k.buf["pC"], k.buf["pD"]
    pT32, pT, pT2 = k.buf["pT32"], k.buf["pT"], k.buf["pT2"]

    for sl in range(NSLAB):
        k.inc("d_cst", sp.dma_start(out=rwT[:, sl * 8:(sl + 1) * 8],
                                    in_=W["rwT32"][sl * 128:(sl + 1) * 128, :]), 16)
    k.inc("d_wg", g.dma_start(out=b1c[:], in_=W["b1c"]), 16)
    k.inc("d_wg", g.dma_start(out=b["b2all"][0:8, :], in_=W["b2r"]), 16)
    dcst0 = k.val("d_cst")
    dwg0 = k.val("d_wg")

    w1slot_ct = [0, 0, 0, 0]
    ga_cs = {0: 0, 1: 0}
    oe_cv = {0: 0, 1: 0}
    for tc in range(4):
        k.wait(t, "d_cst", dcst0)
        for tt in range(8):
            gt_ = tc * 8 + tt
            accap = acc[:, tt * 1024:(tt + 1) * 1024]
            ldx = k.inc("d_x", sp.dma_start(out=accap, in_=src_ap[gt_ * 128:(gt_ + 1) * 128, :]), 16)
            k.wait(t, "d_x", ldx)
            rl8 = smB[:, 0:8]
            for sl in range(NSLAB):
                ct = k.inc("ct", t.transpose(pT32[:], accap[:, sl * 128:(sl + 1) * 128], idf[:]))
                k.wait(v, "ct", ct)
                v.tensor_copy(out=xT32[:], in_=pT32[:])
                k.wait(t, "cv", k.val("cv"))
                mmr = t.matmul(pD[:, 0:8], xT32[:], rwT[:, sl * 8:(sl + 1) * 8],
                               start=True, stop=True)
                ct = k.inc("ct", mmr)
                k.wait(v, "ct", ct)
                if sl == 0:
                    v.tensor_copy(out=rl8, in_=pD[:, 0:8])
                else:
                    v.tensor_tensor(out=rl8, in0=rl8, in1=pD[:, 0:8], op=ALU.add)
                k.wait(t, "cv", k.val("cv"))
            v.reduce_max(out=smA[:, 0:1], in_=rl8, axis=AX.X)
            v.tensor_scalar(out=smB[:, 8:16], in0=rl8, scalar1=smA[:, 0:1], scalar2=None, op0=ALU.is_ge)
            v.scalar_tensor_tensor(out=smB[:, 16:24], in0=smB[:, 8:16], scalar=-1e30, in1=rl8,
                                   op0=ALU.mult, op1=ALU.add)
            v.reduce_max(out=smA[:, 1:2], in_=smB[:, 16:24], axis=AX.X)
            v.tensor_scalar(out=smB[:, 8:16], in0=rl8, scalar1=smA[:, 1:2], scalar2=None, op0=ALU.is_ge)
            cv = k.inc("cv", v.tensor_scalar(out=smA[:, 2:3], in0=smA[:, 0:1], scalar1=-1.0, scalar2=None, op0=ALU.mult))
            k.wait(s, "cv", cv)
            cs = k.inc("cs", s.activation(out=smB[:, 16:24], in_=rl8, func=AF.Exp, bias=smA[:, 2:3]))
            k.wait(v, "cs", cs)
            v.tensor_tensor(out=smB[:, 24:32], in0=smB[:, 16:24], in1=smB[:, 8:16], op=ALU.mult)
            v.reduce_sum(out=smA[:, 3:4], in_=smB[:, 24:32], axis=AX.X)
            v.reciprocal(out=smA[:, 4:5], in_=smA[:, 3:4])
            cv = k.inc("cv", v.tensor_scalar(out=w_tm[:, tt * 8:(tt + 1) * 8], in0=smB[:, 24:32],
                                             scalar1=smA[:, 4:5], scalar2=None, op0=ALU.mult))
            k.wait(sp, "cv", cv)
        k.wait(s, "d_wg", dwg0)
        k.wait(v, "d_x", k.val("d_x"))
        for e in range(8):
            k.wait(sp, "ct", k.val("ct"))
            k.wait(g, "ct", k.val("ct"))
            k.wait(g, "cv", k.val("cv"))
            for grp in range(4):
                k.inc("d_w2", sp.dma_start(
                    out=w2buf[:, grp * 8192:(grp + 1) * 8192].rearrange("p (f c) -> p f c", f=8),
                    in_=W["w2bf"][e, grp * 8:(grp + 1) * 8].rearrange("f p c -> p f c")), 16)
            dw2 = k.val("d_w2")
            for half in range(2):
                for fc in range(32):
                    pslot = (fc // 2) % 2
                    sname = f"d_w1s{pslot}"
                    if fc % 2 == 0:
                        k.wait(sp, "ct", w1slot_ct[pslot])
                        k.inc(sname, sp.dma_start(
                            out=w1buf[:, pslot * 2048:(pslot + 1) * 2048].rearrange("p (f c) -> p f c", f=2),
                            in_=W["w1bf"][e, fc:fc + 2].rearrange("f p c -> p f c")), 16)
                    k.wait(t, sname, k.val(sname))
                    slot = fc % 4
                    pp = (pA, pB, pC, pD)[fc % 4]
                    if ga_cs.get(fc % 4):
                        k.wait(t, "cs", ga_cs[fc % 4])
                    if oe_cv.get(fc % 4):
                        k.wait(t, "cv", oe_cv[fc % 4])
                    for sl in range(NSLAB):
                        mmh = t.matmul(pp[:, 0:512],
                                       w1buf[:, slot * 1024 + sl * 128: slot * 1024 + (sl + 1) * 128],
                                       XT[:, sl * T + tc * 1024 + half * 512: sl * T + tc * 1024 + half * 512 + 512],
                                       start=(sl == 0), stop=(sl == NSLAB - 1))
                    ct_h = k.inc("ct", mmh)
                    if fc % 2 == 1:
                        w1slot_ct[pslot] = ct_h
                    k.wait(s, "ct", ct_h)
                    cs = k.inc("cs", s.activation(out=h1buf[:, fc * 512:(fc + 1) * 512], in_=pp[:, 0:512],
                                                  func=AF.Gelu, bias=b1c[:, e * 32 + fc:e * 32 + fc + 1]))
                    ga_cs[fc % 4] = cs
                k.wait(t, "d_w2", dw2)
                k.wait(t, "cs", k.val("cs"))
                for tt in range(4):
                    for dh in range(2):
                        u = tt * 2 + dh
                        pp = (pA, pB, pC, pD)[u % 4]
                        if oe_cv.get(u % 4):
                            k.wait(t, "cv", oe_cv[u % 4])
                        if ga_cs.get(u % 4):
                            k.wait(t, "cs", ga_cs[u % 4])
                        for fs in range(32):
                            mmo = t.matmul(pp[:, 0:512],
                                           h1buf[:, fs * 512 + tt * 128: fs * 512 + (tt + 1) * 128],
                                           w2buf[:, fs * 1024 + dh * 512: fs * 1024 + dh * 512 + 512],
                                           start=(fs == 0), stop=(fs == 31))
                        ct_o = k.inc("ct", mmo)
                        k.wait(v, "ct", ct_o)
                        k.wait(v, "d_w2", dw2)
                        gtt = half * 4 + tt
                        wcol = w_tm[:, gtt * 8 + e: gtt * 8 + e + 1]
                        accap = acc[:, gtt * 1024 + dh * 512: gtt * 1024 + dh * 512 + 512]
                        cv = k.inc("cv", v.scalar_tensor_tensor(out=accap, in0=pp[:, 0:512], scalar=wcol,
                                                                in1=accap, op0=ALU.mult, op1=ALU.add))
                        oe_cv[u % 4] = cv
        for tt8 in range(8):
            k.wait(s, "ct", k.val("ct"))
            cs = k.inc("cs", k.S.copy(out=b["w8bf"][:, 0:8], in_=w_tm[:, tt8 * 8:(tt8 + 1) * 8]))
            k.wait(t, "cs", cs)
            ct = k.inc("ct", t.transpose(k.buf["pTB"][0:8, 0:128], b["w8bf"][:, 0:8], id_bf[:]))
            k.wait(v, "ct", ct)
            cv = k.inc("cv", v.tensor_copy(out=b["w8bf"][0:8, 8:136], in_=k.buf["pTB"][0:8, 0:128]))
            k.wait(t, "cv", cv)
            for dh in range(2):
                pp = pC if dh == 0 else pD
                ct = k.inc("ct", t.matmul(pp[:, 0:512], b["w8bf"][0:8, 8:136],
                                          b["b2all"][0:8, dh * 512:(dh + 1) * 512], start=True, stop=True))
                k.wait(v, "ct", ct)
                accap = acc[:, tt8 * 1024 + dh * 512: tt8 * 1024 + dh * 512 + 512]
                cv = k.inc("cv", v.tensor_tensor(out=accap, in0=accap, in1=pp[:, 0:512], op=ALU.add))
            k.wait(t, "cv", cv)
        for tt8 in range(8):
            gt_ = tc * 8 + tt8
            accap = acc[:, tt8 * 1024:(tt8 + 1) * 1024]
            cv = emit_ln_tile(k, accap, accap)
            k.wait(g, "cv", cv)
            k.inc("d_st", g.dma_start(out=dst_ap[gt_ * 128:(gt_ + 1) * 128, :], in_=accap), 16)
            k.wait(s, "cv", cv)
            cs = k.inc("cs", k.S.copy(out=xbf2[:], in_=accap))
            k.wait(t, "cs", cs)
            tseq(k, [(xbf2[:, sl * 128:(sl + 1) * 128], XT[:, sl * T + gt_ * 128:sl * T + (gt_ + 1) * 128])
                     for sl in range(NSLAB)])
        k.wait(sp, "cv", k.val("cv"))
        k.wait(sp, "cs", k.val("cs"))
        k.wait(sp, "d_st", k.val("d_st"))



WIN = 512
NWIN = T // WIN
NCH = WIN // 32
NTL = WIN // 128


def emit_l1_attn(k, src_ap, dst_ap, W):
    t, v, s, g, sp = k.t, k.V, k.S, k.g, k.sp
    b = k.buf
    XT = b["XT"]
    id_bf = b["id_bf"]
    pA, pB, pC, pD = b["pA"], b["pB"], b["pC"], b["pD"]
    pM = b["pMISC"]
    pT, pT2 = b["pT"], b["pT2"]
    v_fm = b["v_fm"]; fs_tm = b["fs_tm"]; fl_tm = b["fl_tm"]; v_tm = b["v_tm"]
    od_tm = b["od_tm"]; u4 = b["u4"]; w4 = b["w4"]; xw_bf = b["xw_bf"]
    q_fm = b["q_fm"]; k_fm = b["k_fm"]; kn_tm = b["kn_tm"]; w_fm2 = b["w_fm2"]
    wst = b["wst"]; h1b = b["h1b"]
    Sbf = b["Sbf"]
    ABS = b["ABS"]
    atn = ABS[:, 0:64]
    hist = ABS[:, 64:136]
    vhist = ABS[:, 136:640]
    btm = b["btm"]
    w2gT = ABS[:, 640:672]
    ones1 = ABS[0:1, 672:800]
    onesc = ABS[:, 800:801]
    wbT = ABS[:, 801:833]
    c_triu = ABS[:, 833:897]

    b["c_triu"] = c_triu
    A4 = b["A4"]; Pc = b["Pc"]; PTc = b["PTc"]
    mixs = b["mixs"]
    w1bT = b["w1bT"]
    rstd_bf = b["rstd_bf"]
    stats_bf = b["stats_bf"]
    Sf = b["Sf"]
    beta_f = b["beta_f"]
    strow = b["strow"]
    sq = b["sq"]
    lnin = b["lnin"]
    sm = b["sm"]; sm2 = b["sm2"]
    onwb = b["onwb"]
    convw = b["convw"]
    firws = b["firws"]; firwl = b["firwl"]
    gsm = b["gsm"]
    statsb = W["statsb"]

    if getattr(k, "_wconv_done", None):
        for eng in (g, sp, v, t, s):
            k.wait(eng, "d_cvo0", k._wconv_done[0])
            k.wait(eng, "d_cvo1", k._wconv_done[1])
        k._wconv_done = None
    k.inc("d_cst", sp.dma_start(out=convw[:], in_=W["c_conv"]), 16)
    k.inc("d_cst", sp.dma_start(out=firws[:], in_=W["c_firs"]), 16)
    k.inc("d_cst", sp.dma_start(out=firwl[:], in_=W["c_firl"]), 16)
    k.inc("d_cst", sp.dma_start(out=onwb[:], in_=W["r_onw"]), 16)
    k.inc("d_cst", sp.dma_start(out=gsm[:, 8:12], in_=W["r_glt"]), 16)
    k.inc("d_cst", sp.dma_start(out=gsm[:, 12:16], in_=W["r_gcb"]), 16)
    k.inc("d_cst", sp.dma_start(out=gsm[:, 6:7], in_=W["gb2col"]), 16)
    k.inc("d_cst", sp.dma_start(out=b["gb1c"][:], in_=W["gb1c"]), 16)
    dcst1 = k.val("d_cst")
    k.inc("d_wg", g.dma_start(out=w2gT[:], in_=W["w2gT"]), 16)
    k.inc("d_wg", g.dma_start(out=wbT[:], in_=W["wbTs"]), 16)
    k.inc("d_wg", g.dma_start(out=c_triu[:], in_=W["c_triu"]), 16)
    dwg1 = k.val("d_wg")
    DECAY = 1.0 - 1.0 / 3000.0
    EPSF = 0.08 * DECAY
    k.wait(s, "d_cst", dcst1)
    s.activation(out=gsm[:, 0:4], in_=gsm[:, 8:12], func=AF.Exp)
    cs = k.val("cs")
    k.wait(v, "cs", cs)
    v.reciprocal(out=gsm[:, 4:5], in_=gsm[:, 0:1])
    v.reciprocal(out=gsm[:, 5:6], in_=gsm[:, 1:2])
    v.reciprocal(out=gsm[:, 7:8], in_=gsm[:, 2:3])
    cv = k.inc("cv", v.reciprocal(out=gsm[:, 0:1], in_=gsm[:, 3:4]))
    itcol = {0: 4, 1: 5, 2: 7, 3: 0}
    v.tensor_scalar(out=gsm[:, 1:2], in0=gsm[:, 12:13], scalar1=DECAY, scalar2=None, op0=ALU.mult)
    v.tensor_scalar(out=gsm[:, 2:3], in0=gsm[:, 13:14], scalar1=DECAY, scalar2=None, op0=ALU.mult)
    v.tensor_scalar(out=gsm[:, 3:4], in0=gsm[:, 14:15], scalar1=DECAY, scalar2=None, op0=ALU.mult)
    cv = k.inc("cv", v.tensor_scalar(out=gsm[:, 8:9], in0=gsm[:, 15:16], scalar1=DECAY, scalar2=None, op0=ALU.mult))
    cbcol = {0: 1, 1: 2, 2: 3, 3: 8}
    v.memset(ones1[:], 1.0)
    v.memset(onesc[:], 1.0)
    v.memset(hist[:], 0.0)
    v.memset(vhist[:], 0.0)
    for h in range(4):
        v.memset(Sf[:, h * 512:(h + 1) * 512], 0.0)
    cv = k.inc("cv", v.memset(Sbf[:], 0.0))
    k.wait(t, "cv", cv)
    k.wait(t, "d_wg", dwg1)

    wslot_ct = [0, 0]

    def stream_w(name, c, eng_gate_ct=None):
        slot = c % 2
        k.wait(sp, "ct", wslot_ct[slot])
        sem = f"d_ws{slot}"
        k.inc(sem, sp.dma_start(out=wst[:, slot * 1024:(slot + 1) * 1024],
                                in_=W[name + "bf"][c]), 16)
        k.wait(t, sem, k.val(sem))
        return slot

    for w in range(NWIN):
        tok0 = w * WIN
        k.barrier()
        for ti, (wname, fmbuf, cwofs) in enumerate(
                (("wqt", q_fm, 0), ("wkt", k_fm, 32), ("wvt", v_fm, 64))):
            for c in range(NSLAB):
                slot = stream_w(wname, c)
                pp = pA if c % 2 == 0 else pB
                for sl in range(NSLAB):
                    mmp = t.matmul(pp[:, 0:512], wst[:, slot * 1024 + sl * 128: slot * 1024 + (sl + 1) * 128],
                                   XT[:, sl * T + tok0: sl * T + tok0 + WIN],
                                   start=(sl == 0), stop=(sl == NSLAB - 1))
                ct = k.inc("ct", mmp)
                wslot_ct[slot] = ct
                k.wait(v, "ct", ct)
                st_ = b["convst"]
                G = k.G
                v.tensor_copy(out=st_[:, 0:3], in_=hist[:, (ti * 8 + c) * 3:(ti * 8 + c) * 3 + 3])
                cv = k.inc("cv", v.tensor_copy(out=st_[:, 3:515], in_=pp[:, 0:512]))
                k.wait(t, "cv", cv)
                v.tensor_copy(out=hist[:, (ti * 8 + c) * 3:(ti * 8 + c) * 3 + 3], in_=st_[:, 512:515])
                v.tensor_scalar(out=fmbuf[:, c * 512:(c + 1) * 512], in0=st_[:, 0:512],
                                scalar1=convw[:, cwofs + c * 4:cwofs + c * 4 + 1], scalar2=None, op0=ALU.mult)
                for j in range(1, 4):
                    v.scalar_tensor_tensor(out=fmbuf[:, c * 512:(c + 1) * 512], in0=st_[:, j:512 + j],
                                           scalar=convw[:, cwofs + c * 4 + j:cwofs + c * 4 + j + 1],
                                           in1=fmbuf[:, c * 512:(c + 1) * 512], op0=ALU.mult, op1=ALU.add)
                cg = k.val("cv")
                k.wait(s, "cv", cg)
                cs = k.inc("cs", s.activation(out=b["siltmp"][:], in_=fmbuf[:, c * 512:(c + 1) * 512],
                                              func=AF.Sigmoid))
                k.wait(v, "cs", cs)
                cv = k.inc("cv", v.tensor_tensor(out=fmbuf[:, c * 512:(c + 1) * 512],
                                                 in0=fmbuf[:, c * 512:(c + 1) * 512],
                                                 in1=b["siltmp"][:], op=ALU.mult))
        for sl in range(NSLAB):
            mmb = t.matmul(pM[0:4, 0:512], wbT[:, sl * 4:(sl + 1) * 4],
                           XT[:, sl * T + tok0: sl * T + tok0 + WIN],
                           start=(sl == 0), stop=(sl == NSLAB - 1))
        ct = k.inc("ct", mmb)
        k.wait(s, "ct", ct)
        cs = k.inc("cs", s.activation(out=b["beta_bf"][0:4, :], in_=pM[0:4, 0:512], func=AF.Sigmoid))
        k.wait(t, "cs", cs)
        for tl in range(NTL):
            ct = k.inc("ct", t.transpose(pT[0:128, 0:4], b["beta_bf"][0:4, tl * 128:(tl + 1) * 128], id_bf[0:4, 0:4]))
            k.wait(v, "ct", ct)
            cv = k.inc("cv", v.tensor_copy(out=btm[:, tl * 4:(tl + 1) * 4], in_=pT[0:128, 0:4]))
            k.wait(t, "cv", cv)
        cv = k.inc("cv", v.tensor_scalar(out=b["nbt"][:], in0=btm[:], scalar1=-1.0, scalar2=None, op0=ALU.mult))
        k.barrier()
        for fmbuf in (q_fm, k_fm):
            for h in range(4):
                for s2 in range(2):
                    sl = 2 * h + s2
                    cs = k.inc("cs", s.activation(out=b["sqst"][:, s2 * 512:(s2 + 1) * 512],
                                                  in_=fmbuf[:, sl * 512:(sl + 1) * 512], func=AF.Square))
                    k.wait(t, "cs", cs)
                    mmn = t.matmul(pM[0:1, 0:512], onesc[:], b["sqst"][:, s2 * 512:(s2 + 1) * 512],
                                   start=(s2 == 0), stop=(s2 == 1))
                ct = k.inc("ct", mmn)
                k.wait(v, "ct", ct)
                v.tensor_scalar(out=strow[0:1, 0:512], in0=pM[0:1, 0:512], scalar1=1e-6, scalar2=None, op0=ALU.add)
                cv = k.val("cv")
                k.wait(s, "cv", cv)
                s.activation(out=strow[0:1, 512:1024], in_=strow[0:1, 0:512], func=AF.Ln)
                cs = k.inc("cs", s.activation(out=strow[0:1, 0:512], in_=strow[0:1, 512:1024], func=AF.Exp, scale=-0.5))
                k.wait(v, "cs", cs)
                cv = k.inc("cv", v.tensor_copy(out=rstd_bf[:], in_=strow[0:1, 0:512]))
                k.wait(t, "cv", cv)
                ct = k.inc("ct", t.matmul(pC[:, 0:512], ones1[:], rstd_bf[:], start=True, stop=True))
                k.wait(v, "ct", ct)
                for s2 in range(2):
                    sl = 2 * h + s2
                    cv = k.inc("cv", v.tensor_tensor(out=fmbuf[:, sl * 512:(sl + 1) * 512],
                                                     in0=fmbuf[:, sl * 512:(sl + 1) * 512],
                                                     in1=pC[:, 0:512], op=ALU.mult))
                k.wait(t, "cv", cv)
        k.barrier()
        for (srcb, dstb, perhead) in ((k_fm, kn_tm, True), (v_fm, v_tm, False)):
            jobs = []
            for sl in range(NSLAB):
                for tl in range(NTL):
                    if perhead:
                        h = sl // 2
                        dcol = h * 1024 + tl * 256 + (sl % 2) * 128
                    else:
                        dcol = tl * 1024 + sl * 128
                    jobs.append((srcb[:, sl * 512 + tl * 128: sl * 512 + (tl + 1) * 128],
                                 dstb[:, dcol:dcol + 128]))
            tseq(k, jobs)
        for h in range(4):
            for tl in range(NTL):
                bc = btm[:, tl * 4 + h:tl * 4 + h + 1]
                v.tensor_scalar(out=u4[:, h * 1024 + tl * 256: h * 1024 + (tl + 1) * 256],
                                in0=v_tm[:, tl * 1024 + h * 256: tl * 1024 + (h + 1) * 256],
                                scalar1=bc, scalar2=None, op0=ALU.mult)
                cv = k.inc("cv", v.tensor_scalar(out=w4[:, h * 1024 + tl * 256: h * 1024 + (tl + 1) * 256],
                                                 in0=kn_tm[:, h * 1024 + tl * 256: h * 1024 + (tl + 1) * 256],
                                                 scalar1=bc, scalar2=None, op0=ALU.mult))
        k.wait(t, "cv", cv)
        k.barrier()
        k.wait(g, "ct", k.val("ct"))
        k.wait(g, "cv", k.val("cv"))
        dbd = k.inc("d_wg", g.dma_start(out=b["c_bd"][:], in_=W["c_bd"]), 16)
        k.wait(v, "d_wg", dbd)
        k.wait(t, "d_wg", dbd)
        p4cv = getattr(k, "_p4cv", {0: 0, 1: 0})
        k._p4cv = p4cv
        ui = 0
        for h in range(4):
            for gr in range(NTL):
                par = ui % 2
                ui += 1
                pap = pA if par == 0 else pC
                psq = pB if par == 0 else pD
                A4p = A4 if par == 0 else b["A4b"]
                Pcp = Pc if par == 0 else b["Pc2"]
                PTp = PTc if par == 0 else b["PTc2"]
                if p4cv[par]:
                    k.wait(t, "cv", p4cv[par])
                for s2 in range(2):
                    sl = 2 * h + s2
                    mma = t.matmul(psq[:, 0:128], k_fm[:, sl * 512 + gr * 128: sl * 512 + (gr + 1) * 128],
                                   k_fm[:, sl * 512 + gr * 128: sl * 512 + (gr + 1) * 128],
                                   start=(s2 == 0), stop=(s2 == 1))
                ct = k.inc("ct", mma)
                k.wait(v, "ct", ct)
                v.tensor_scalar(out=b["a4f"][:], in0=psq[:, 0:128],
                                scalar1=b["nbt"][:, gr * 4 + h:gr * 4 + h + 1], scalar2=None, op0=ALU.mult)
                cv = k.inc("cv", v.tensor_tensor(out=A4p[:], in0=b["a4f"][:], in1=b["c_bd"][:], op=ALU.mult))
                k.wait(t, "cv", cv)
                cv = k.inc("cv", v.tensor_copy(out=Pcp[:], in_=A4p[:]))
                k.wait(t, "cv", cv)
                ct = k.inc("ct", t.transpose(psq.bitcast(BF16)[:, 0:128] if False else k.buf["pTB" if par == 0 else "pTB2"][0:128, 0:128], Pcp[:], id_bf[:]))
                k.wait(v, "ct", ct)
                cv = k.inc("cv", v.tensor_copy(out=PTp[:], in_=k.buf["pTB" if par == 0 else "pTB2"][0:128, 0:128]))
                for lev in range(6):
                    k.wait(t, "cv", cv)
                    mmu = t.matmul(pap[:, 0:256], PTp[:], u4[:, h * 1024 + gr * 256: h * 1024 + (gr + 1) * 256],
                                   start=True, stop=True)
                    k.inc("ct", mmu)
                    mmw = t.matmul(pap[:, 256:512], PTp[:], w4[:, h * 1024 + gr * 256: h * 1024 + (gr + 1) * 256],
                                   start=True, stop=True)
                    k.inc("ct", mmw)
                    if lev < 5:
                        mmp2 = t.matmul(psq[:, 0:128], PTp[:], Pcp[:], start=True, stop=True)
                        k.inc("ct", mmp2)
                        mmpT = t.matmul(psq[:, 128:256], Pcp[:], PTp[:], start=True, stop=True)
                        ct = k.inc("ct", mmpT)
                    else:
                        ct = k.val("ct")
                    k.wait(v, "ct", ct)
                    v.tensor_tensor(out=u4[:, h * 1024 + gr * 256: h * 1024 + (gr + 1) * 256],
                                    in0=u4[:, h * 1024 + gr * 256: h * 1024 + (gr + 1) * 256],
                                    in1=pap[:, 0:256], op=ALU.add)
                    cv = k.inc("cv", v.tensor_tensor(out=w4[:, h * 1024 + gr * 256: h * 1024 + (gr + 1) * 256],
                                                     in0=w4[:, h * 1024 + gr * 256: h * 1024 + (gr + 1) * 256],
                                                     in1=pap[:, 256:512], op=ALU.add))
                    if lev < 5:
                        v.tensor_copy(out=Pcp[:], in_=psq[:, 0:128])
                        cv = k.inc("cv", v.tensor_copy(out=PTp[:], in_=psq[:, 128:256]))
                p4cv[par] = cv
        k.wait(t, "cv", k.val("cv"))
        jobs = []
        for h in range(4):
            for tl in range(NTL):
                for s2 in range(2):
                    jobs.append((w4[:, h * 1024 + tl * 256 + s2 * 128: h * 1024 + tl * 256 + (s2 + 1) * 128],
                                 w_fm2[:, h * 1024 + s2 * 512 + tl * 128: h * 1024 + s2 * 512 + (tl + 1) * 128]))
        tseq(k, jobs)
        for h in range(4):
            cv = k.inc("cv", v.tensor_copy(out=Sbf[:, h * 512:(h + 1) * 512], in_=Sf[:, h * 512:(h + 1) * 512]))
        k.wait(t, "cv", cv)
        k.barrier()
        for ci in range(NCH // 2):
            tl = ci // 2
            bo = (ci % 2) * 64
            for h in range(4):
                hu = h * 1024 + tl * 256
                for s2 in range(2):
                    mmws = t.matmul(pA[bo:bo + 64, 0:256],
                                    w_fm2[:, h * 1024 + s2 * 512 + tl * 128 + bo: h * 1024 + s2 * 512 + tl * 128 + bo + 64],
                                    Sbf[:, h * 512 + s2 * 256: h * 512 + (s2 + 1) * 256],
                                    start=(s2 == 0), stop=(s2 == 1), tile_position=(0, bo))
                ct = k.inc("ct", mmws)
                k.wait(v, "ct", ct)
                cv = k.inc("cv", v.tensor_tensor(out=u4[bo:bo + 64, hu:hu + 256],
                                                 in0=u4[bo:bo + 64, hu:hu + 256],
                                                 in1=pA[bo:bo + 64, 0:256], op=ALU.subtract))
                k.wait(t, "cv", cv)
                for s2 in range(2):
                    sl = 2 * h + s2
                    mat = t.matmul(pM[bo:bo + 64, 0:64],
                                   k_fm[:, sl * 512 + tl * 128 + bo: sl * 512 + tl * 128 + bo + 64],
                                   q_fm[:, sl * 512 + tl * 128 + bo: sl * 512 + tl * 128 + bo + 64],
                                   start=(s2 == 0), stop=(s2 == 1), tile_position=(0, bo))
                ct = k.inc("ct", mat)
                k.wait(v, "ct", ct)
                cv = k.inc("cv", v.tensor_tensor(out=atn[bo:bo + 64, 0:64], in0=pM[bo:bo + 64, 0:64],
                                                 in1=b["c_triu"][bo:bo + 64, 0:64], op=ALU.mult))
                k.wait(t, "cv", cv)
                for s2 in range(2):
                    sl = 2 * h + s2
                    t.matmul(pB[bo:bo + 64, 0:256],
                             q_fm[:, sl * 512 + tl * 128 + bo: sl * 512 + tl * 128 + bo + 64],
                             Sbf[:, h * 512 + s2 * 256: h * 512 + (s2 + 1) * 256],
                             start=(s2 == 0), stop=False, tile_position=(0, bo))
                mo = t.matmul(pB[bo:bo + 64, 0:256], atn[bo:bo + 64, 0:64], u4[bo:bo + 64, hu:hu + 256],
                              start=False, stop=True, tile_position=(bo, bo))
                ct = k.inc("ct", mo)
                k.wait(v, "ct", ct)
                cv = k.inc("cv", v.tensor_copy(out=od_tm[bo:bo + 64, tl * 1024 + h * 256: tl * 1024 + (h + 1) * 256],
                                               in_=pB[bo:bo + 64, 0:256]))
                k.wait(t, "cv", cv)
                for s2 in range(2):
                    msu = t.matmul(pC[:, s2 * 256:(s2 + 1) * 256],
                                   kn_tm[bo:bo + 64, hu + s2 * 128: hu + (s2 + 1) * 128],
                                   u4[bo:bo + 64, hu:hu + 256],
                                   start=True, stop=True, tile_position=(bo, 0))
                ct = k.inc("ct", msu)
                k.wait(v, "ct", ct)
                v.tensor_tensor(out=Sf[:, h * 512: h * 512 + 512],
                                in0=Sf[:, h * 512: h * 512 + 512],
                                in1=pC[:, 0:512], op=ALU.add)
                cv = k.inc("cv", v.tensor_copy(out=Sbf[:, h * 512:(h + 1) * 512],
                                               in_=Sf[:, h * 512:(h + 1) * 512]))
                k.wait(t, "cv", cv)
        k.barrier()
        k._fd_ct = getattr(k, "_fd_ct", [0, 0])
        vxb = b["vx"]
        for sl in range(NSLAB):
            v.tensor_copy(out=vxb[:, sl * 575: sl * 575 + 63], in_=vhist[:, sl * 63:(sl + 1) * 63])
            cv = k.inc("cv", v.tensor_copy(out=vxb[:, sl * 575 + 63: sl * 575 + 575],
                                           in_=v_fm[:, sl * 512:(sl + 1) * 512]))
        k.wait(t, "cv", cv)
        for h in range(4):
            for s2 in range(2):
                sl = 2 * h + s2
                sts = wst[:, s2 * 512: s2 * 512 + 512]
                stl = wst[:, 1024 + s2 * 512: 1024 + (s2 + 1) * 512]
                vx = b["vx"]
                dstage = b["dstage"]
                for (stg, ntap, fdname, pbank) in ((sts, 5, "c_fds", pC), (stl, 64, "c_fdl", pD)):
                    slot = 0
                    for j in range(ntap):
                        if j % 8 == 0:
                            slot = (j // 8) % 2
                            sname = f"d_fd{slot}"
                            k.wait(sp, "ct", k._fd_ct[slot])
                            ng = min(8, ntap - j)
                            k.inc(sname, sp.dma_start(
                                out=dstage[:, slot * 1024:slot * 1024 + ng * 128].rearrange(
                                    "p (f c) -> p f c", f=ng),
                                in_=W[fdname][sl, j:j + ng].rearrange("f p c -> p f c")), 16)
                            k.wait(t, sname, k.val(sname))
                        sh = ntap - 1 - j
                        mmf = t.matmul(pbank[:, 0:512],
                                       dstage[:, slot * 1024 + (j % 8) * 128: slot * 1024 + (j % 8 + 1) * 128],
                                       vx[:, sl * 575 + 63 - sh: sl * 575 + 63 - sh + 512],
                                       start=(j == 0), stop=(j == ntap - 1))
                        if j % 8 == 7 or j == ntap - 1:
                            ct = k.inc("ct", mmf)
                            k._fd_ct[slot] = ct
                    k.wait(v, "ct", ct)
                    cv = k.inc("cv", v.tensor_copy(out=stg[:], in_=pbank[:, 0:512]))
                    k.wait(t, "cv", cv)
            k.wait(t, "cv", k.val("cv"))
            k.wait(s, "cv", k.val("cv"))
            jobs = []
            for (stgbase, dstb2) in ((0, fs_tm), (1024, fl_tm)):
                for s2 in range(2):
                    for tl in range(NTL):
                        jobs.append((wst[:, stgbase + s2 * 512 + tl * 128: stgbase + s2 * 512 + (tl + 1) * 128],
                                     dstb2[:, tl * 1024 + (2 * h + s2) * 128: tl * 1024 + (2 * h + s2 + 1) * 128]))
            tseq(k, jobs)
            k.wait(t, "cv", cv)
            jobs = []
            for s2 in range(2):
                for tl in range(NTL):
                    jobs.append((od_tm[:, tl * 1024 + h * 256 + s2 * 128: tl * 1024 + h * 256 + (s2 + 1) * 128],
                                 b["odst"][:, s2 * 512 + tl * 128: s2 * 512 + (tl + 1) * 128]))
            tseq(k, jobs)
            srcs = [wst[:, 0:1024], wst[:, 1024:2048], b["odst"][:, 0:1024], None]
            for ri, sb_ in enumerate(srcs):
                def chunkap(s2):
                    if sb_ is None:
                        return v_fm[:, (2 * h + s2) * 512:(2 * h + s2 + 1) * 512]
                    return sb_[:, s2 * 512:(s2 + 1) * 512]
                for kind in range(3):
                    for s2 in range(2):
                        xap = chunkap(s2)
                        if kind == 0:
                            rh = xap
                        elif kind == 1:
                            k.wait(s, "cv", k.val("cv"))
                            cs = k.inc("cs", s.activation(out=b["kst"][:, s2 * 512:(s2 + 1) * 512], in_=xap, func=AF.Square))
                            k.wait(t, "cs", cs)
                            rh = b["kst"][:, s2 * 512:(s2 + 1) * 512]
                        else:
                            cv = k.inc("cv", v.scalar_tensor_tensor(out=b["kst"][:, s2 * 512:(s2 + 1) * 512],
                                                                    in0=xap, scalar=-1.0, in1=xap,
                                                                    op0=ALU.mult, op1=ALU.max))
                            k.wait(t, "cv", cv)
                            rh = b["kst"][:, s2 * 512:(s2 + 1) * 512]
                        mst = t.matmul(pM[0:1, 0:512], onesc[:], rh, start=(s2 == 0), stop=(s2 == 1))
                    ct = k.inc("ct", mst)
                    k.wait(v, "ct", ct)
                    cv = k.inc("cv", v.tensor_copy(out=strow[0:1, kind * 512:(kind + 1) * 512], in_=pM[0:1, 0:512]))
                    k.wait(t, "cv", cv)
                k.wait(v, "d_st2", k.val("d_st2"))
                k.wait(v, "d_st", k.val("d_st"))
                v.tensor_scalar(out=b["fin"][0:1, 0:512], in0=strow[0:1, 0:512], scalar1=1.0 / 256, scalar2=None, op0=ALU.mult)
                v.tensor_tensor(out=b["fin"][0:1, 512:1024], in0=b["fin"][0:1, 0:512], in1=b["fin"][0:1, 0:512], op=ALU.mult)
                v.tensor_scalar(out=b["fin"][0:1, 1024:1536], in0=strow[0:1, 512:1024], scalar1=1.0 / 256, scalar2=None, op0=ALU.mult)
                v.tensor_tensor(out=b["fin"][0:1, 512:1024], in0=b["fin"][0:1, 1024:1536], in1=b["fin"][0:1, 512:1024], op=ALU.subtract)
                cv = k.inc("cv", v.tensor_scalar(out=b["fin"][0:1, 1024:1536], in0=strow[0:1, 1024:1536], scalar1=1.0 / 256, scalar2=None, op0=ALU.mult))
                k.wait(s, "cv", cv)
                s.activation(out=strow[0:1, 1536:2048], in_=strow[0:1, 512:1024], func=AF.Ln)
                cs = k.inc("cs", s.activation(out=b["fin"][0:1, 1536:2048], in_=strow[0:1, 1536:2048], func=AF.Exp, scale=0.5))
                k.wait(g, "cs", cs)
                k.wait(g, "cv", cv)
                k.inc("d_st2", g.dma_start(out=statsb[h, ri * 4:(ri + 1) * 4, :], in_=b["fin"][0:1, 0:2048]), 16)
        for sl in range(NSLAB):
            cv = k.inc("cv", v.tensor_copy(out=vhist[:, sl * 63:(sl + 1) * 63],
                                           in_=v_fm[:, sl * 512 + 449: sl * 512 + 512]))
        k.wait(t, "cv", cv)
        k.barrier()
        for c in range(NSLAB):
            slot = stream_w("w1at", c)
            pp = pA if c % 2 == 0 else pB
            for sl in range(NSLAB):
                mmx = t.matmul(pp[:, 0:512], wst[:, slot * 1024 + sl * 128: slot * 1024 + (sl + 1) * 128],
                               XT[:, sl * T + tok0: sl * T + tok0 + WIN],
                               start=(sl == 0), stop=(sl == NSLAB - 1))
            ct = k.inc("ct", mmx)
            wslot_ct[c % 2] = ct
            k.wait(v, "ct", ct)
            cv = k.inc("cv", v.tensor_scalar(out=xw_bf[:, c * 512:(c + 1) * 512], in0=pp[:, 0:512],
                                             scalar1=b["gb1c"][:, c:c + 1], scalar2=None, op0=ALU.add))
            k.wait(t, "cv", cv)
        k.wait(g, "ct", k.val("ct"))
        dw1b = k.inc("d_w1b", g.dma_start(out=w1bT[:], in_=W["w1bT"]), 16)
        k.wait(t, "d_w1b", dw1b)
        for h in range(4):
            k.wait(g, "d_st2", k.val("d_st2"))
            k.wait(g, "ct", k.val("ct"))
            dst2 = k.inc("d_sb", g.dma_start(out=stats_bf[:], in_=statsb[h]), 16)
            k.wait(t, "d_sb", dst2)
            tgb = b["vx"][:, 0:4096]
            t2b = b["w_fm2"][:, 0:4096]
            for c in range(NSLAB):
                ct = k.inc("ct", t.matmul(pA[:, 0:512] if c % 2 == 0 else pB[:, 0:512],
                                          w1bT[:, c * 128:(c + 1) * 128], stats_bf[:],
                                          start=True, stop=True))
                pp = pA if c % 2 == 0 else pB
                k.wait(v, "ct", ct)
                cv = k.inc("cv", v.tensor_tensor(out=tgb[:, c * 512:(c + 1) * 512], in0=pp[:, 0:512],
                                                 in1=xw_bf[:, c * 512:(c + 1) * 512], op=ALU.add))
                k.wait(t, "cv", cv)
            k.wait(s, "cv", cv)
            cs = k.inc("cs", s.activation(out=t2b, in_=tgb, func=AF.Square))
            k.wait(v, "cs", cs)
            v.tensor_tensor(out=t2b, in0=t2b, in1=tgb, op=ALU.mult)
            cv = k.inc("cv", v.scalar_tensor_tensor(out=t2b, in0=t2b, scalar=0.044715, in1=tgb, op0=ALU.mult, op1=ALU.add))
            k.wait(s, "cv", cv)
            cs = k.inc("cs", s.activation(out=t2b, in_=t2b, func=AF.Tanh, scale=float(math.sqrt(2.0 / math.pi))))
            k.wait(v, "cs", cs)
            v.tensor_scalar(out=t2b, in0=t2b, scalar1=0.5, scalar2=0.5, op0=ALU.mult, op1=ALU.add)
            cv = k.inc("cv", v.tensor_tensor(out=h1b[:, 0:4096], in0=t2b, in1=tgb, op=ALU.mult))
            k.wait(t, "cv", cv)
            for sl in range(NSLAB):
                mlg = t.matmul(pM[0:4, 0:512], w2gT[:, sl * 4:(sl + 1) * 4], h1b[:, sl * 512:(sl + 1) * 512],
                               start=(sl == 0), stop=(sl == NSLAB - 1))
            ct = k.inc("ct", mlg)
            k.wait(v, "ct", ct)
            cv = k.inc("cv", v.tensor_scalar(out=b["lgbf"][0:4, 0:512], in0=pM[0:4, 0:512],
                                             scalar1=gsm[0:4, 6:7], scalar2=None, op0=ALU.add))
            k.wait(t, "cv", cv)
            for tl in range(NTL):
                ct = k.inc("ct", t.transpose(pT[0:128, 0:4], b["lgbf"][0:4, tl * 128:(tl + 1) * 128], id_bf[0:4, 0:4]))
                k.wait(v, "ct", ct)
                lt = b["lgtm"]
                cv = k.inc("cv", v.tensor_copy(out=lt[:, 0:4], in_=pT[0:128, 0:4]))
                k.wait(t, "cv", cv)
                v.tensor_scalar(out=lt[:, 3:4], in0=lt[:, 3:4], scalar1=gsm[:, cbcol[h]:cbcol[h] + 1],
                                scalar2=None, op0=ALU.add)
                v.tensor_scalar(out=lt[:, 4:8], in0=lt[:, 0:4], scalar1=gsm[:, itcol[h]:itcol[h] + 1],
                                scalar2=None, op0=ALU.mult)
                v.reduce_max(out=sm2[:, 0:1], in_=lt[:, 4:8], axis=AX.X)
                cv = k.inc("cv", v.tensor_scalar(out=sm2[:, 1:2], in0=sm2[:, 0:1], scalar1=-1.0, scalar2=None, op0=ALU.mult))
                k.wait(s, "cv", cv)
                cs = k.inc("cs", s.activation(out=lt[:, 8:12], in_=lt[:, 4:8], func=AF.Exp, bias=sm2[:, 1:2],
                                              accum_out=sm2[:, 2:3]))
                k.wait(v, "cs", cs)
                v.reciprocal(out=sm2[:, 3:4], in_=sm2[:, 2:3])
                v.tensor_scalar(out=lt[:, 4:8], in0=lt[:, 8:12], scalar1=sm2[:, 3:4], scalar2=None, op0=ALU.mult)
                cv = k.inc("cv", v.tensor_scalar(out=lt[:, 8:12], in0=lt[:, 4:8], scalar1=1.0 - 4.0 * EPSF,
                                                 scalar2=EPSF, op0=ALU.mult, op1=ALU.add))
                tc0 = tl * 1024 + h * 256
                v.tensor_scalar(out=mixs[:], in0=fs_tm[:, tc0:tc0 + 256], scalar1=lt[:, 8:9], scalar2=None, op0=ALU.mult)
                v.scalar_tensor_tensor(out=mixs[:], in0=fl_tm[:, tc0:tc0 + 256], scalar=lt[:, 9:10], in1=mixs[:],
                                       op0=ALU.mult, op1=ALU.add)
                v.scalar_tensor_tensor(out=mixs[:], in0=od_tm[:, tc0:tc0 + 256], scalar=lt[:, 10:11], in1=mixs[:],
                                       op0=ALU.mult, op1=ALU.add)
                cv = k.inc("cv", v.scalar_tensor_tensor(out=mixs[:], in0=v_tm[:, tc0:tc0 + 256], scalar=lt[:, 11:12],
                                                        in1=mixs[:], op0=ALU.mult, op1=ALU.add))
                k.wait(s, "cv", cv)
                cs = k.inc("cs", s.activation(out=b["kst"][:, 0:256], in_=mixs[:], func=AF.Square, accum_out=sm2[:, 4:5]))
                k.wait(v, "cs", cs)
                cv = k.inc("cv", v.tensor_scalar(out=sm2[:, 5:6], in0=sm2[:, 4:5], scalar1=1.0 / 256, scalar2=1e-5,
                                                 op0=ALU.mult, op1=ALU.add))
                k.wait(s, "cv", cv)
                s.activation(out=sm2[:, 6:7], in_=sm2[:, 5:6], func=AF.Ln)
                cs = k.inc("cs", s.activation(out=sm2[:, 7:8], in_=sm2[:, 6:7], func=AF.Exp, scale=-0.5))
                k.wait(v, "cs", cs)
                v.tensor_scalar(out=mixs[:], in0=mixs[:], scalar1=sm2[:, 7:8], scalar2=None, op0=ALU.mult)
                cv = k.inc("cv", v.tensor_tensor(out=fs_tm[:, tc0:tc0 + 256], in0=mixs[:], in1=onwb[:], op=ALU.mult))
                k.wait(t, "cv", cv)
        k.barrier()
        jobs = []
        for sl in range(NSLAB):
            for tl in range(NTL):
                jobs.append((fs_tm[:, tl * 1024 + sl * 128: tl * 1024 + (sl + 1) * 128],
                             b["o_fm"][:, sl * 512 + tl * 128: sl * 512 + (tl + 1) * 128]))
        tseq(k, jobs)
        for c in range(NSLAB):
            slot = stream_w("wot", c)
            for tt in range(NTL):
                pp = pA if tt % 2 == 0 else pB
                for sl in range(NSLAB):
                    mmy = t.matmul(pp[:, 0:128], b["o_fm"][:, sl * 512 + tt * 128: sl * 512 + (tt + 1) * 128],
                                   wst[:, slot * 1024 + sl * 128: slot * 1024 + (sl + 1) * 128],
                                   start=(sl == 0), stop=(sl == NSLAB - 1))
                ct = k.inc("ct", mmy)
                k.wait(v, "ct", ct)
                cv = k.inc("cv", v.tensor_copy(out=b["ybuf"][:, tt * 1024 + c * 128: tt * 1024 + (c + 1) * 128],
                                               in_=pp[:, 0:128]))
                k.wait(t, "cv", cv)
            wslot_ct[slot] = k.val("ct")
        for tt in range(NTL):
            gt_ = (tok0 // 128) + tt
            ldr = k.inc("d_x", sp.dma_start(out=lnin[:], in_=src_ap[gt_ * 128:(gt_ + 1) * 128, :]), 16)
            k.wait(v, "d_x", ldr)
            cv = k.inc("cv", v.tensor_tensor(out=lnin[:], in0=lnin[:], in1=b["ybuf"][:, tt * 1024:(tt + 1) * 1024], op=ALU.add))
            cv = emit_ln_tile(k, lnin[:], lnin[:])
            k.wait(g, "cv", cv)
            k.inc("d_st", g.dma_start(out=dst_ap[gt_ * 128:(gt_ + 1) * 128, :], in_=lnin[:]), 16)
            k.wait(s, "cv", cv)
            cs = k.inc("cs", k.S.copy(out=b["xbf3"][:], in_=lnin[:]))
            k.wait(t, "cs", cs)
            tseq(k, [(b["xbf3"][:, sl * 128:(sl + 1) * 128], XT[:, sl * T + gt_ * 128:sl * T + (gt_ + 1) * 128])
                     for sl in range(NSLAB)])
            k.wait(sp, "cv", k.val("cv"))
            k.wait(sp, "cs", k.val("cs"))
            k.wait(sp, "d_st", k.val("d_st"))



def emit_l1_ffn(k, src_ap, dst_ap, W):
    t, v, s, g, sp = k.t, k.V, k.S, k.g, k.sp
    b = k.buf
    XT = b["XT"]
    id_bf = b["id_bf"]; idf = b["id_f32"]
    pA, pB, pC, pD = b["pA"], b["pB"], b["pC"], b["pD"]
    pM = b["pMISC"]; pT32 = b["pT32"]; pT, pT2 = b["pT"], b["pT2"]
    he = b["he"]; s2cW = b["s2cW"]; c2sW = b["c2sW"]; we1 = b["we1"]; smo = b["smo"]
    contb = b["contb"]; outfm = b["outfm"]; outst = b["outst"]; we2 = b["we2"]
    spk = b["spk"]; decW = b["decW"]; wst2 = b["wst2"]; sqb = b["sqb"]
    contf = b["contf"]; r1f = b["r1f"]; rlogf = b["rlogf"]; acc2 = b["acc2"]
    w_tm2 = b["w_tm2"]; dect = b["dect"]; lnin2 = b["lnin2"]
    sm = b["sm"]; sm2 = b["sm5"]
    b1e = b["b1e"]; be2b = b["be2b"]; encb = b["encb"]; s2cb = b["s2cb"]
    rb1c = b["rb1c"]; rb2c = b["rb2c"]; c2sb = b["c2sb"]; decbb = b["decbb"]
    lng = b["lng"]; lnb = b["lnb"]
    rw1f = b["rw1f"]; rw2f = b["rw2f"]

    if getattr(k, "_wconv_done", None):
        for eng in (g, sp, v, t, s):
            k.wait(eng, "d_cvo0", k._wconv_done[0])
            k.wait(eng, "d_cvo1", k._wconv_done[1])
        k._wconv_done = None
    k.inc("d_cst", sp.dma_start(out=rw1f[0:64, :], in_=W["rW1"]), 16)
    k.inc("d_cst", sp.dma_start(out=rw2f[0:64, :], in_=W["rW2"]), 16)
    k.inc("d_cst", sp.dma_start(out=b1e[:], in_=W["b1e"]), 16)
    k.inc("d_cst", sp.dma_start(out=be2b[:], in_=W["be2b"]), 16)
    k.inc("d_cst", sp.dma_start(out=encb[:], in_=W["encbc"]), 16)
    k.inc("d_cst", sp.dma_start(out=s2cb[:], in_=W["s2cbc"]), 16)
    k.inc("d_cst", sp.dma_start(out=rb1c[:], in_=W["rb1c"]), 16)
    k.inc("d_cst", sp.dma_start(out=rb2c[:], in_=W["rb2c"]), 16)
    k.inc("d_cst", sp.dma_start(out=c2sb[:], in_=W["c2sbc"]), 16)
    k.inc("d_cst", sp.dma_start(out=decbb[:], in_=W["r_decb"]), 16)
    k.inc("d_cst", sp.dma_start(out=lng[:], in_=W["r_lng"]), 16)
    k.inc("d_cst", sp.dma_start(out=lnb[:], in_=W["r_lnb"]), 16)
    dcst2 = k.val("d_cst")
    k.inc("d_wg", g.dma_start(out=s2cW[:], in_=W["s2ct"]), 16)
    k.inc("d_wg", g.dma_start(out=c2sW[0:64, :], in_=W["c2s_W"]), 16)
    k.inc("d_wg", g.dma_start(out=we2[:], in_=W["we2t"]), 16)
    dwg2 = k.val("d_wg")
    k.wait(t, "d_cst", dcst2)
    k.wait(t, "d_wg", dwg2)

    wslot = [0, 0]
    enc_cs = {0: 0, 1: 0}
    for w in range(NWIN):
        tok0 = w * WIN
        k.barrier()
        k.wait(g, "ct", k.val("ct"))
        dwe1 = k.inc("d_we1", g.dma_start(out=we1[0:64, :], in_=W["we1r"]), 16)
        for c in range(16):
            slot = c % 2
            k.wait(sp, "ct", wslot[slot])
            sem = f"d_ws{slot}"
            k.inc(sem, sp.dma_start(out=wst2[:, slot * 1024:(slot + 1) * 1024],
                                    in_=W["encbf"][c]), 16)
            k.wait(t, sem, k.val(sem))
            pp = (pA, pB, pC, pD)[c % 4]
            if enc_cs.get(c % 4):
                k.wait(t, "cs", enc_cs[c % 4])
            for sl in range(NSLAB):
                mme = t.matmul(pp[:, 0:512], wst2[:, slot * 1024 + sl * 128: slot * 1024 + (sl + 1) * 128],
                               XT[:, sl * T + tok0: sl * T + tok0 + WIN],
                               start=(sl == 0), stop=(sl == NSLAB - 1))
            ct = k.inc("ct", mme)
            wslot[slot] = ct
            k.wait(s, "ct", ct)
            cs = k.inc("cs", s.activation(out=spk[:, c * 512:(c + 1) * 512], in_=pp[:, 0:512],
                                          func=AF.Sigmoid, bias=encb[:, c:c + 1]))
            enc_cs[c % 4] = cs
        k.wait(t, "cs", k.val("cs"))
        for sl16 in range(16):
            mms = t.matmul(pM[0:64, 0:512], s2cW[:, sl16 * 64:(sl16 + 1) * 64],
                           spk[:, sl16 * 512:(sl16 + 1) * 512],
                           start=(sl16 == 0), stop=(sl16 == 15))
        ct = k.inc("ct", mms)
        k.wait(v, "ct", ct)
        v.tensor_scalar(out=contf[0:64, :], in0=pM[0:64, 0:512], scalar1=s2cb[0:64, 0:1],
                        scalar2=None, op0=ALU.add)
        cv = k.inc("cv", v.tensor_copy(out=contb[0:64, :], in_=contf[0:64, :]))
        k.wait(t, "cv", cv)
        ct = k.inc("ct", t.matmul(pM[0:64, 0:512], rw1f[0:64, :], contf[0:64, :], start=True, stop=True))
        k.wait(s, "ct", ct)
        cs = k.inc("cs", s.activation(out=r1f[0:64, :], in_=pM[0:64, 0:512], func=AF.Tanh, bias=rb1c[0:64, 0:1]))
        k.wait(t, "cs", cs)
        ct = k.inc("ct", t.matmul(pM[0:8, 0:512], rw2f[0:64, :], r1f[0:64, :], start=True, stop=True))
        k.wait(v, "ct", ct)
        cv = k.inc("cv", v.tensor_scalar(out=rlogf[0:8, :], in0=pM[0:8, 0:512], scalar1=rb2c[0:8, 0:1],
                                         scalar2=None, op0=ALU.add))
        k.wait(t, "cv", cv)
        for tl in range(NTL):
            ct = k.inc("ct", t.transpose(pT32[0:128, 0:8], rlogf[0:8, tl * 128:(tl + 1) * 128], idf[0:8, 0:8]))
            k.wait(v, "ct", ct)
            rl8 = sm2[:, 16:24]
            v.tensor_copy(out=rl8, in_=pT32[0:128, 0:8])
            v.reduce_max(out=sm2[:, 0:1], in_=rl8, axis=AX.X)
            v.tensor_scalar(out=sm2[:, 24:32], in0=rl8, scalar1=sm2[:, 0:1], scalar2=None, op0=ALU.is_ge)
            v.scalar_tensor_tensor(out=sm2[:, 32:40], in0=sm2[:, 24:32], scalar=-1e30, in1=rl8,
                                   op0=ALU.mult, op1=ALU.add)
            v.reduce_max(out=sm2[:, 1:2], in_=sm2[:, 32:40], axis=AX.X)
            v.tensor_scalar(out=sm2[:, 24:32], in0=rl8, scalar1=sm2[:, 1:2], scalar2=None, op0=ALU.is_ge)
            cv = k.inc("cv", v.tensor_scalar(out=sm2[:, 2:3], in0=sm2[:, 0:1], scalar1=-1.0, scalar2=None, op0=ALU.mult))
            k.wait(s, "cv", cv)
            cs = k.inc("cs", s.activation(out=sm2[:, 32:40], in_=rl8, func=AF.Exp, bias=sm2[:, 2:3]))
            k.wait(v, "cs", cs)
            v.tensor_tensor(out=sm2[:, 32:40], in0=sm2[:, 32:40], in1=sm2[:, 24:32], op=ALU.mult)
            v.reduce_sum(out=sm2[:, 3:4], in_=sm2[:, 32:40], axis=AX.X)
            v.reciprocal(out=sm2[:, 4:5], in_=sm2[:, 3:4])
            cv = k.inc("cv", v.tensor_scalar(out=w_tm2[:, tl * 8:(tl + 1) * 8], in0=sm2[:, 32:40],
                                             scalar1=sm2[:, 4:5], scalar2=None, op0=ALU.mult))
            k.wait(t, "cv", cv)
        cv = k.inc("cv", v.memset(acc2[:], 0.0))
        k.wait(t, "d_we1", dwe1)
        k.wait(v, "cv", cv)
        oe_cv2 = {0: 0, 1: 0}
        for e in range(8):
            for c in range(NSLAB):
                pp = (pA, pB, pC, pD)[c % 4]
                if enc_cs.get(c % 4):
                    k.wait(t, "cs", enc_cs[c % 4])
                if c % 4 >= 2 and oe_cv2.get(c % 4 - 2):
                    k.wait(t, "cv", oe_cv2[c % 4 - 2])
                mmh = t.matmul(pp[:, 0:512], we1[0:64, e * 1024 + c * 128: e * 1024 + (c + 1) * 128],
                               contb[0:64, :], start=True, stop=True)
                ct = k.inc("ct", mmh)
                k.wait(s, "ct", ct)
                cs = k.inc("cs", s.activation(out=he[:, c * 512:(c + 1) * 512], in_=pp[:, 0:512],
                                              func=AF.Relu, bias=b1e[:, e * 8 + c:e * 8 + c + 1]))
                enc_cs[c % 4] = cs
            k.wait(t, "cs", k.val("cs"))
            for tt in range(NTL):
                u = tt % 2
                pp = pC if u == 0 else pD
                if oe_cv2[u]:
                    k.wait(t, "cv", oe_cv2[u])
                for c in range(NSLAB):
                    mmo = t.matmul(pp[:, 0:64], he[:, c * 512 + tt * 128: c * 512 + (tt + 1) * 128],
                                   we2[:, (e * 8 + c) * 64:(e * 8 + c + 1) * 64],
                                   start=(c == 0), stop=(c == NSLAB - 1))
                ct = k.inc("ct", mmo)
                k.wait(v, "ct", ct)
                wcol = w_tm2[:, tt * 8 + e: tt * 8 + e + 1]
                accap = acc2[:, tt * 64:(tt + 1) * 64]
                v.scalar_tensor_tensor(out=accap, in0=pp[:, 0:64], scalar=wcol, in1=accap,
                                       op0=ALU.mult, op1=ALU.add)
                cv = k.inc("cv", v.scalar_tensor_tensor(out=accap, in0=be2b[:, e * 64:(e + 1) * 64],
                                                        scalar=wcol, in1=accap, op0=ALU.mult, op1=ALU.add))
                oe_cv2[u] = cv
        for tt in range(NTL):
            cv = k.inc("cv", v.tensor_copy(out=outst[:], in_=acc2[:, tt * 64:(tt + 1) * 64]))
            k.wait(t, "cv", cv)
            ct = k.inc("ct", t.transpose(pT[0:64, 0:128], outst[:], id_bf[:]))
            k.wait(v, "ct", ct)
            cv = k.inc("cv", v.tensor_copy(out=outfm[0:64, tt * 128:(tt + 1) * 128], in_=pT[0:64, 0:128]))
            k.wait(t, "cv", cv)
        for c in range(16):
            pp = (pA, pB, pC, pD)[c % 4]
            if enc_cs.get(c % 4):
                k.wait(t, "cs", enc_cs[c % 4])
            mmc = t.matmul(pp[:, 0:512], c2sW[0:64, c * 128:(c + 1) * 128], outfm[0:64, :],
                           start=True, stop=True)
            ct = k.inc("ct", mmc)
            k.wait(s, "ct", ct)
            cs = k.inc("cs", s.activation(out=smo[:, c * 512:(c + 1) * 512], in_=pp[:, 0:512],
                                          func=AF.Sigmoid, bias=c2sb[:, c:c + 1]))
            enc_cs[c % 4] = cs
        k.wait(t, "cs", k.val("cs"))
        k.wait(sp, "ct", k.val("ct"))
        for grp in range(2):
            k.inc("d_dec", sp.dma_start(
                out=decW[:, grp * 8192:(grp + 1) * 8192].rearrange("p (f c) -> p f c", f=8),
                in_=W["decbf"][grp * 8:(grp + 1) * 8].rearrange("f p c -> p f c")), 16)
        ddec = k.val("d_dec")
        k.wait(t, "d_dec", ddec)
        for tt in range(NTL):
            gt_ = (tok0 // 128) + tt
            for half in range(2):
                pp = pC if half == 0 else pD
                for sl16 in range(16):
                    mmd = t.matmul(pp[:, 0:512], smo[:, sl16 * 512 + tt * 128: sl16 * 512 + (tt + 1) * 128],
                                   decW[:, sl16 * 1024 + half * 512: sl16 * 1024 + half * 512 + 512],
                                   start=(sl16 == 0), stop=(sl16 == 15))
                ct = k.inc("ct", mmd)
                k.wait(v, "ct", ct)
                cv = k.inc("cv", v.tensor_tensor(out=dect[:, half * 512:(half + 1) * 512], in0=pp[:, 0:512],
                                                 in1=decbb[:, half * 512:(half + 1) * 512], op=ALU.add))
                k.wait(t, "cv", cv)
                k.wait(s, "cv", cv)
                cs = k.inc("cs", s.activation(out=dect[:, half * 512:(half + 1) * 512],
                                              in_=dect[:, half * 512:(half + 1) * 512], func=AF.Sigmoid))
            k.wait(v, "cs", k.val("cs"))
            cv = emit_ln_tile(k, dect[:], dect[:])
            v.tensor_tensor(out=dect[:], in0=dect[:], in1=lng[:], op=ALU.mult)
            cv = k.inc("cv", v.tensor_tensor(out=dect[:], in0=dect[:], in1=lnb[:], op=ALU.add))
            ldr = k.inc("d_x", sp.dma_start(out=lnin2[:], in_=src_ap[gt_ * 128:(gt_ + 1) * 128, :]), 16)
            k.wait(v, "d_x", ldr)
            cv = k.inc("cv", v.tensor_tensor(out=lnin2[:], in0=lnin2[:], in1=dect[:], op=ALU.add))
            cv = emit_ln_tile(k, lnin2[:], lnin2[:])
            k.wait(g, "cv", cv)
            k.inc("d_st", g.dma_start(out=dst_ap[gt_ * 128:(gt_ + 1) * 128, :], in_=lnin2[:]), 16)
            k.wait(sp, "cv", k.val("cv"))
            k.wait(sp, "d_st", k.val("d_st"))
            k.wait(v, "d_st", k.val("d_st"))



def build_program(blocks=(0,), final_out=True):
    nc = bass.Bass("TRN2", target_bir_lowering=False)
    x_in = nc.dram_tensor("x", [T, D], F32, kind="ExternalInput").ap()
    out = nc.dram_tensor("out", [T, D], F32, kind="ExternalOutput").ap()
    xres = nc.dram_tensor("xres", [T, D], F32, kind="Internal").ap()

    W = {}

    def inp(name, shape, dtype=F32):
        W[name] = nc.dram_tensor(name, list(shape), dtype, kind="ExternalInput").ap()
        return W[name]

    inp("wvT", (D, D)); inp("woT", (D, D)); inp("wshT", (D, 64)); inp("wchT", (D, 1))
    inp("omgT", (64, 32)); inp("wcmT", (320, 64))
    inp("r_bv", (128, D)); inp("r_bout", (128, D)); inp("r_bsh", (128, 64))
    inp("r_rffb", (128, 32)); inp("r_bch", (128, 1)); inp("r_lt", (128, 1))
    inp("c_brow", (128, 512)); inp("c_colb", (128, 1)); inp("c_mags", (128, 3))
    inp("c_B4", (128, 2048), BF16)
    inp("c_id", (128, 128), BF16)
    inp("c_idf", (128, 128), F32)
    inp("rwT32", (D, 8)); inp("w1t", (8, 32, 128, 8, 128)); inp("w2T", (8, T, D))
    inp("b1c", (128, 256)); inp("rb2", (8, 128, D)); inp("b2r", (8, D))
    for nm in ("wqt", "wkt", "wvt", "w1at", "wot"):
        inp(nm, (8, 128, 8, 128))
    inp("wbTs", (128, 32)); inp("w2gT", (128, 32)); inp("w1bT", (16, 1024))
    inp("c_conv", (128, 96)); inp("c_firs", (128, 40)); inp("c_firl", (128, 512))
    inp("r_onw", (128, 256)); inp("r_glt", (128, 4)); inp("r_gcb", (128, 4))
    inp("gb2col", (128, 1)); inp("gb1c", (128, 8))
    inp("c_bd", (128, 128), BF16); inp("c_triu", (128, 64), BF16)
    inp("c_fds", (8, 5, 128, 128), BF16); inp("c_fdl", (8, 64, 128, 128), BF16)
    W["statsb"] = nc.dram_tensor("statsb", [4, 16, 512], F32, kind="Internal").ap()
    W["w1bf"] = nc.dram_tensor("w1bf", [8, 32, 128, 1024], BF16, kind="Internal").ap()
    W["w2bf"] = nc.dram_tensor("w2bf", [8, 32, 128, 1024], BF16, kind="Internal").ap()
    for nm in ("wqt", "wkt", "wvt", "w1at", "wot"):
        W[nm + "bf"] = nc.dram_tensor(nm + "bf", [8, 128, 1024], BF16, kind="Internal").ap()
    W["encbf"] = nc.dram_tensor("encbf", [16, 128, 1024], BF16, kind="Internal").ap()
    W["decbf"] = nc.dram_tensor("decbf", [16, 128, 1024], BF16, kind="Internal").ap()
    inp("enct", (16, 128, 8, 128)); inp("s2ct", (128, 1024)); inp("c2s_W", (64, 2048))
    inp("we1r", (64, 8192)); inp("we2t", (128, 4096)); inp("dec_W", (2048, 1024))
    inp("rW1", (64, 64)); inp("rW2", (64, 8))
    inp("b1e", (128, 64)); inp("be2b", (128, 512)); inp("encbc", (128, 16))
    inp("s2cbc", (128, 1)); inp("rb1c", (128, 1)); inp("rb2c", (128, 1))
    inp("c2sbc", (128, 16)); inp("r_decb", (128, D)); inp("r_lng", (128, D)); inp("r_lnb", (128, D))

    with ExitStack() as ctx:
        k = KB(nc, ctx)
        k.sbuf("XT", [128, NSLAB * T], BF16)
        AA = k.sbuf("AA", [128, 32768], BF16)
        AB = k.sbuf("AB", [128, 22528], BF16)
        AF = k.sbuf("AF", [128, 8520], F32)
        k.sbuf("id_bf", [128, 128], BF16)
        k.sbuf("id_f32", [128, 128], F32)
        k.psum("pA", [128, 512]); k.psum("pB", [128, 512])
        k.psum("pC", [128, 512]); k.psum("pD", [128, 512])
        pM = k.psum("pMISC", [128, 512])
        pTB = k.psum("pTB", [128, 1024], BF16)
        k.psum("pTB2", [128, 1024], BF16)
        k.buf["pS"] = pM[:, 0:256]
        k.buf["pPhi"] = pM[:, 256:384]
        k.buf["pT32"] = pM[:, 384:512]
        k.buf["pT"] = pTB[:, 0:128]
        k.buf["pT2"] = pTB[:, 128:256]

        def seg(arena, start, n):
            return arena[:, start:start + n]

        b = k.buf
        b["BIG"] = seg(AA, 0, 32768)
        b["rowbuf"] = seg(AB, 0, 4096)
        b["aofm"] = seg(AB, 0, 1024)
        b["xbf"] = seg(AB, 1024, 1024)
        b["wsmall"] = seg(AB, 2048, 520)
        b["wcm_bf"] = seg(AB, 2568, 192)
        b["omg_bf"] = seg(AB, 2760, 32)
        b["shn_bf"] = seg(AB, 2792, 64)
        b["shnT_bf"] = seg(AB, 2856, 128)
        b["phis_bf"] = seg(AB, 2984, 64)
        b["tp_bf"] = seg(AB, 3048, 384)
        b["tpT_bf"] = seg(AB, 3432, 384)
        b["Pbf"] = seg(AB, 4096, 4096)
        b["wbig"] = seg(AB, 8192, 8192)
        b["phi"] = seg(AB, 16384, 4096)
        b["B4"] = seg(AB, 20480, 2048)
        b["biasb"] = seg(AF, 0, 2304)
        b["cst"] = seg(AF, 2304, 520)
        b["wsmall_f32"] = seg(AF, 2824, 192)
        b["xld"] = seg(AF, 3016, 1024)
        b["tfull"] = seg(AF, 4040, 1024)
        b["sq"] = seg(AF, 5064, 1024)
        b["sh64"] = seg(AF, 5064, 192)
        b["pj"] = seg(AF, 5256, 192)
        b["sm"] = seg(AF, 6088, 8)
        b["sm2"] = seg(AF, 6096, 64)

        k.inc("d_cst", k.sp.dma_start(out=k.buf["id_bf"][:], in_=W["c_id"]), 16)
        k.inc("d_cst", k.sp.dma_start(out=k.buf["id_f32"][:], in_=W["c_idf"]), 16)

        if 1 in blocks or 2 in blocks or 3 in blocks:
            emit_w_bf16_stage(k, W, blocks)
        emit_load_xt(k, x_in)

        last = blocks[-1]
        prev = x_in
        for bi in blocks:
            src_b = prev
            dst_b = out if bi == last else xres
            prev = dst_b
            if bi == 0:
                emit_l0_attn(k, src_b, dst_b, W)
            elif bi == 1:
                b["w2buf"] = seg(AA, 0, 32768)
                b["h1buf"] = seg(AB, 0, 16384)
                b["w1buf"] = seg(AB, 16384, 4096)
                b["b2buf"] = seg(AB, 20480, 1024)
                b["b2all"] = seg(AB, 20480, 1024)
                b["w8bf"] = seg(AB, 21760, 136)
                b["b1c"] = seg(AB, 21504, 256)
                b["xbf2"] = seg(AB, 0, 1024)
                b["acc"] = seg(AF, 0, 8192)
                b["sm3"] = seg(AF, 8192, 8)
                b["sm4"] = seg(AF, 8200, 64)
                b["w_tm"] = seg(AF, 8264, 64)
                b["rwT_f32"] = seg(AF, 8328, 64)
                b["xT32"] = seg(AF, 8392, 128)
                b["sm"] = b["sm3"]
                b["sq"] = seg(AB, 1024, 1024)
                emit_l0_ffn(k, src_b, dst_b, W)
            elif bi == 2:
                ABS = k.buf.get("ABS_t")
                if ABS is None:
                    ABS = k.sbuf("ABS_t", [128, 897], BF16)
                    b["ABS"] = ABS
                b["v_fm"] = seg(AA, 0, 4096)
                b["fs_tm"] = seg(AA, 4096, 4096)
                b["fl_tm"] = seg(AA, 8192, 4096)
                b["v_tm"] = seg(AA, 12288, 4096)
                b["od_tm"] = seg(AA, 16384, 4096)
                b["u4"] = seg(AA, 20480, 4096)
                b["w4"] = seg(AA, 24576, 4096)
                b["xw_bf"] = seg(AA, 28672, 4096)
                b["o_fm"] = seg(AA, 0, 4096)
                b["ybuf"] = seg(AA, 4096, 4096)
                b["xbf3"] = seg(AA, 8192, 1024)
                b["q_fm"] = seg(AB, 0, 4096)
                b["k_fm"] = seg(AB, 4096, 4096)
                b["kn_tm"] = seg(AB, 8192, 4096)
                b["w_fm2"] = seg(AB, 12288, 4096)
                b["wst"] = seg(AB, 16384, 2048)
                b["h1b"] = seg(AB, 18432, 4096)
                b["Sbf"] = seg(AB, 18432, 2048)
                b["convst"] = seg(AB, 18432, 520)
                b["sqst"] = seg(AB, 18432, 1024)
                b["kst"] = seg(AB, 18432, 1024)
                b["odst"] = seg(AB, 19456, 1024)
                b["A4"] = seg(AB, 18432, 128)
                b["Pc"] = seg(AB, 18560, 128)
                b["PTc"] = seg(AB, 18688, 128)
                b["a4f"] = seg(AB, 18816, 128)
                b["c_bd"] = seg(AB, 18944, 128)
                b["w1bT"] = AB[0:16, 0:1024]
                b["stats_bf"] = AB[0:16, 1024:1536]
                b["lgbf"] = AB[0:4, 4096:4608]
                b["mixs"] = seg(AB, 4608, 256)
                b["beta_bf"] = AB[0:4, 8192:8704]
                b["rstd_bf"] = AB[0:1, 8704:9216]
                b["siltmp"] = seg(AB, 19456, 512)
                b["btm"] = seg(AF, 7672, 16)
                b["nbt"] = seg(AF, 7688, 16)
                b["A4b"] = seg(AB, 19072, 128)
                b["Pc2"] = seg(AB, 19200, 128)
                b["PTc2"] = seg(AB, 19328, 128)
                b["vx"] = seg(AB, 8192, 4600)
                b["dstage"] = seg(AB, 12800, 2048)
                b["Sf"] = seg(AF, 0, 2048)
                b["beta_f"] = AF[0:4, 2048:2560]
                b["strow"] = AF[0:1, 2560:4608]
                b["fin"] = AF[0:1, 4608:6656]
                b["sq"] = seg(AF, 4608, 1024)
                b["lnin"] = seg(AF, 5632, 1024)
                b["sm"] = seg(AF, 6656, 8)
                b["sm2"] = seg(AF, 6664, 64)
                b["onwb"] = seg(AF, 6728, 256)
                b["convw"] = seg(AF, 6984, 96)
                b["firws"] = seg(AF, 7080, 40)
                b["firwl"] = seg(AF, 7120, 512)
                b["gsm"] = seg(AF, 7632, 16)
                b["gb1c"] = seg(AF, 7648, 8)
                b["lgtm"] = seg(AF, 7656, 16)
                emit_l1_attn(k, src_b, dst_b, W)
            elif bi == 3:
                b["he"] = seg(AA, 0, 4096)
                b["s2cW"] = seg(AA, 4096, 1024)
                b["c2sW"] = seg(AA, 5120, 2048)
                b["we1"] = seg(AA, 8192, 8192)
                b["smo"] = seg(AA, 8192, 8192)
                b["we2"] = seg(AA, 16384, 4096)
                b["contb"] = seg(AA, 20480, 512)
                b["outfm"] = seg(AA, 20992, 512)
                b["outst"] = seg(AA, 21504, 64)
                b["spk"] = seg(AB, 0, 8192)
                b["decW"] = seg(AB, 0, 16384)
                b["wst2"] = seg(AB, 16384, 2048)
                b["sqb"] = seg(AB, 18432, 1024)
                b["contf"] = seg(AF, 0, 512)
                b["r1f"] = seg(AF, 512, 512)
                b["rlogf"] = seg(AF, 1024, 512)
                b["acc2"] = seg(AF, 1536, 256)
                b["w_tm2"] = seg(AF, 1792, 32)
                b["dect"] = seg(AF, 1824, 1024)
                b["lnin2"] = seg(AF, 2848, 1024)
                b["sm"] = seg(AF, 3872, 8)
                b["sm5"] = seg(AF, 3880, 64)
                b["b1e"] = seg(AF, 3944, 64)
                b["be2b"] = seg(AF, 4008, 512)
                b["encb"] = seg(AF, 4520, 16)
                b["s2cb"] = seg(AF, 4536, 1)
                b["rb1c"] = seg(AF, 4537, 1)
                b["rb2c"] = seg(AF, 4538, 1)
                b["c2sb"] = seg(AF, 4539, 16)
                b["decbb"] = seg(AF, 4555, 1024)
                b["lng"] = seg(AF, 5579, 1024)
                b["lnb"] = seg(AF, 6603, 1024)
                b["rw1f"] = seg(AF, 7627, 64)
                b["rw2f"] = seg(AF, 7691, 8)
                b["sq"] = b["sqb"]
                emit_l1_ffn(k, src_b, dst_b, W)

        for snm in ("d_st", "d_x", "d_cst", "d_wg", "d_wo", "d_w1s0", "d_w1s1", "d_w1s2", "d_w1s3", "d_w2", "d_ws0", "d_ws1", "d_st2", "d_w1b", "d_we1", "d_dec", "d_cvi0", "d_cvi1", "d_cvo0", "d_cvo1", "d_fd0", "d_fd1", "d_sb"):
            if snm in k._sems:
                k.wait(k.sp, snm, k.val(snm))
    return nc, W



def _rep(vec, n=128):
    return np.ascontiguousarray(np.broadcast_to(np.asarray(vec, np.float32).reshape(1, -1), (n, np.asarray(vec).size)))


def prep_l0a(inputs):
    f = np.float32
    w = {}
    w["wvT"] = np.ascontiguousarray(inputs["L0_attn_W_v"].T.astype(f, copy=False))
    w["woT"] = np.ascontiguousarray(inputs["L0_attn_W_out"].T.astype(f, copy=False))
    w["wshT"] = np.ascontiguousarray(inputs["L0_attn_W_shell"].T.astype(f, copy=False))
    w["wchT"] = np.ascontiguousarray(inputs["L0_attn_W_charge"].T.astype(f, copy=False))
    w["omgT"] = np.ascontiguousarray(inputs["L0_attn_omega"].T.astype(f, copy=False))
    w["wcmT"] = np.ascontiguousarray(inputs["L0_attn_W_combine"].T.astype(f, copy=False))
    w["r_bv"] = _rep(inputs["L0_attn_b_v"])
    w["r_bout"] = _rep(inputs["L0_attn_b_out"])
    w["r_bsh"] = _rep(inputs["L0_attn_b_shell"])
    w["r_rffb"] = _rep(inputs["L0_attn_rff_bias"])
    w["r_bch"] = _rep(inputs["L0_attn_b_charge"])
    w["r_lt"] = _rep([inputs["L0_attn_log_temp"]])
    dm = np.arange(512, dtype=np.float64)
    w["c_brow"] = _rep(np.exp(0.03 * dm).astype(f))
    dn = np.arange(128, dtype=np.float64)
    w["c_colb"] = np.ascontiguousarray(np.exp(-0.03 * dn).astype(f).reshape(128, 1) *
                                       np.ones((1, 1), f))
    coeffs = np.asarray([(-1.0) ** m / math.factorial(m) for m in range(5)], np.float64)
    mags = np.sqrt(np.abs(coeffs) + 1e-10)
    rff_scale = math.sqrt(2.0 / 64.0) * math.sqrt(math.exp(-0.5))
    magcol = np.zeros((320,), np.float64)
    for c in range(5):
        magcol[c * 64:(c + 1) * 64] = mags[c] * rff_scale
    mags3 = np.ones((128, 3), f)
    mags3[:, 0] = magcol[0:128].astype(f)
    mags3[:, 1] = magcol[128:256].astype(f)
    mags3[0:64, 2] = magcol[256:320].astype(f)
    w["c_mags"] = np.ascontiguousarray(mags3)
    B4 = np.zeros((128, 2048), np.float32)
    for dvar in range(4):
        delta = dvar * 128
        dnn = np.arange(128).reshape(128, 1)
        dmm = np.arange(512).reshape(1, 512)
        B4[:, dvar * 512:(dvar + 1) * 512] = np.where(dmm > delta + dnn, -1e9, 0.0)
    w["c_B4"] = B4.astype(BF)
    w["c_id"] = np.eye(128, dtype=BF)
    return w


def prep_l0f(inputs):
    f = np.float32
    w = {}
    w["rwT32"] = np.ascontiguousarray(inputs["L0_ffn_router_w"].T.astype(f, copy=False))
    w1T = inputs["L0_ffn_w1"].transpose(0, 2, 1)
    w["w1t"] = np.ascontiguousarray(
        w1T.reshape(8, 8, 128, 32, 128).transpose(0, 3, 2, 1, 4).astype(f, copy=False))
    w["w2T"] = np.ascontiguousarray(inputs["L0_ffn_w2"].transpose(0, 2, 1).astype(f, copy=False))
    b1 = inputs["L0_ffn_b1"].astype(f, copy=False)
    b1c = np.zeros((128, 256), f)
    for e in range(8):
        for fc in range(32):
            b1c[:, e * 32 + fc] = b1[e, fc * 128:(fc + 1) * 128]
    w["b1c"] = b1c
    w["rb2"] = np.ascontiguousarray(
        np.broadcast_to(inputs["L0_ffn_b2"].astype(f, copy=False)[:, None, :], (8, 128, 1024)))
    w["b2r"] = np.ascontiguousarray(inputs["L0_ffn_b2"].astype(f, copy=False))
    return w


def prep_l1a(inputs):
    f = np.float32

    def tile8(wt):
        return np.ascontiguousarray(wt.reshape(8, 128, 8, 128).transpose(2, 1, 0, 3).astype(f, copy=False))

    w = {}
    w["wqt"] = tile8(inputs["L1_attn_Wq"].T)
    w["wkt"] = tile8(inputs["L1_attn_Wk"].T)
    w["wvt"] = tile8(inputs["L1_attn_Wv"].T)
    w["w1at"] = tile8(np.ascontiguousarray(inputs["L1_attn_gate_W1"][:, 0:1024].T))
    w["wot"] = tile8(inputs["L1_attn_Wo"].T)
    wb = inputs["L1_attn_Wb"].astype(f)
    a = np.zeros((128, 32), f)
    for sl in range(8):
        for m in range(4):
            a[:, sl * 4 + m] = wb[m, sl * 128:(sl + 1) * 128]
    w["wbTs"] = a
    g2 = inputs["L1_attn_gate_W2"].astype(f)
    a = np.zeros((128, 32), f)
    for sl in range(8):
        for m in range(4):
            a[:, sl * 4 + m] = g2[m, sl * 128:(sl + 1) * 128]
    w["w2gT"] = a
    w["w1bT"] = np.ascontiguousarray(inputs["L1_attn_gate_W1"][:, 1024:1040].T.astype(f))
    cv_ = np.zeros((128, 96), f)
    for ti, nm in enumerate(("conv_q_w", "conv_k_w", "conv_v_w")):
        cw = inputs[f"L1_attn_{nm}"].astype(f)
        for c in range(8):
            for j in range(4):
                cv_[:, ti * 32 + c * 4 + j] = cw[c * 128:(c + 1) * 128, j]
    w["c_conv"] = cv_
    fs = inputs["L1_attn_fir_short_filt"].reshape(1024, 5).astype(f)
    a = np.zeros((128, 40), f)
    for sl in range(8):
        for j in range(5):
            a[:, sl * 5 + j] = fs[sl * 128:(sl + 1) * 128, j]
    w["c_firs"] = a
    fl = inputs["L1_attn_fir_long_filt"].reshape(1024, 64).astype(f)
    a = np.zeros((128, 512), f)
    for sl in range(8):
        for j in range(64):
            a[:, sl * 64 + j] = fl[sl * 128:(sl + 1) * 128, j]
    w["c_firl"] = a
    w["r_onw"] = _rep(inputs["L1_attn_o_norm_w"])
    w["r_glt"] = _rep(inputs["L1_attn_gate_log_temp"])
    w["r_gcb"] = _rep(inputs["L1_attn_gate_copy_bias"])
    gb2 = np.zeros((128, 1), f)
    gb2[0:4, 0] = inputs["L1_attn_gate_b2"].astype(f)
    w["gb2col"] = gb2
    gb1 = inputs["L1_attn_gate_b1"].astype(f)
    a = np.zeros((128, 8), f)
    for c in range(8):
        a[:, c] = gb1[c * 128:(c + 1) * 128]
    w["gb1c"] = a
    bd = np.zeros((128, 128), f)
    for blk in range(2):
        bd[blk * 64:(blk + 1) * 64, blk * 64:(blk + 1) * 64] = np.tril(np.ones((64, 64), f), -1)
    w["c_bd"] = bd.astype(BF)
    tri = np.triu(np.ones((64, 64), f))
    w["c_triu"] = np.tile(tri, (2, 1)).astype(BF)
    idx = np.arange(128)
    fds = np.zeros((8, 5, 128, 128), f)
    fdl = np.zeros((8, 64, 128, 128), f)
    for sl in range(8):
        for j in range(5):
            fds[sl, j, idx, idx] = fs[sl * 128:(sl + 1) * 128, j]
        for j in range(64):
            fdl[sl, j, idx, idx] = fl[sl * 128:(sl + 1) * 128, j]
    w["c_fds"] = fds.astype(BF)
    w["c_fdl"] = fdl.astype(BF)
    return w


def prep_l1f(inputs):
    f = np.float32
    w = {}
    encW = inputs["L1_ffn_enc_W"].astype(f)
    w["enct"] = np.ascontiguousarray(encW.reshape(8, 128, 16, 128).transpose(2, 1, 0, 3))
    s2c = inputs["L1_ffn_s2c_W"].astype(f)
    a = np.zeros((128, 1024), f)
    for sl in range(16):
        a[:, sl * 64:(sl + 1) * 64] = s2c[sl * 128:(sl + 1) * 128, :]
    w["s2ct"] = a
    w["c2s_W"] = np.ascontiguousarray(inputs["L1_ffn_c2s_W"].astype(f))
    w["we1r"] = np.ascontiguousarray(inputs["L1_ffn_We1"].transpose(1, 0, 2).reshape(64, 8192).astype(f))
    we2 = inputs["L1_ffn_We2"].astype(f)
    a = np.zeros((128, 4096), f)
    for e in range(8):
        for c in range(8):
            a[:, (e * 8 + c) * 64:(e * 8 + c + 1) * 64] = we2[e, c * 128:(c + 1) * 128, :]
    w["we2t"] = a
    w["dec_W"] = np.ascontiguousarray(inputs["L1_ffn_dec_W"].astype(f))
    w["rW1"] = np.ascontiguousarray(inputs["L1_ffn_rW1"].astype(f))
    w["rW2"] = np.ascontiguousarray(inputs["L1_ffn_rW2"].astype(f))
    be1 = inputs["L1_ffn_be1"].astype(f)
    a = np.zeros((128, 64), f)
    for e in range(8):
        for c in range(8):
            a[:, e * 8 + c] = be1[e, c * 128:(c + 1) * 128]
    w["b1e"] = a
    be2 = inputs["L1_ffn_be2"].astype(f)
    a = np.zeros((128, 512), f)
    for e in range(8):
        a[:, e * 64:(e + 1) * 64] = np.broadcast_to(be2[e], (128, 64))
    w["be2b"] = a
    encb = inputs["L1_ffn_enc_b"].astype(f)
    a = np.zeros((128, 16), f)
    for c in range(16):
        a[:, c] = encb[c * 128:(c + 1) * 128]
    w["encbc"] = a
    a = np.zeros((128, 1), f); a[0:64, 0] = inputs["L1_ffn_s2c_b"].astype(f)
    w["s2cbc"] = a
    a = np.zeros((128, 1), f); a[0:64, 0] = inputs["L1_ffn_rb1"].astype(f)
    w["rb1c"] = a
    a = np.zeros((128, 1), f); a[0:8, 0] = inputs["L1_ffn_rb2"].astype(f)
    w["rb2c"] = a
    c2sb = inputs["L1_ffn_c2s_b"].astype(f)
    a = np.zeros((128, 16), f)
    for c in range(16):
        a[:, c] = c2sb[c * 128:(c + 1) * 128]
    w["c2sbc"] = a
    w["r_decb"] = _rep(inputs["L1_ffn_dec_b"])
    w["r_lng"] = _rep(inputs["L1_ffn_ln_g"])
    w["r_lnb"] = _rep(inputs["L1_ffn_ln_b"])
    return w


def prep_all(inputs):
    w = prep_l0a(inputs)
    w["c_idf"] = np.eye(128, dtype=np.float32)
    w.update(prep_l0f(inputs))
    w.update(prep_l1a(inputs))
    w.update(prep_l1f(inputs))
    return w


_CACHE = {}
_BLOCKS = (0, 1, 2, 3)


def kernel(**inputs):
    key = ("prog", _BLOCKS)
    if key not in _CACHE:
        _CACHE[key] = build_program(blocks=_BLOCKS)
    nc, _ = _CACHE[key]
    shared = prep_all(inputs)
    X = np.ascontiguousarray(inputs["L0_attn_x"].astype(np.float32, copy=False))
    in_maps = []
    for c in range(8):
        m = dict(shared)
        m["x"] = np.ascontiguousarray(X[c])
        in_maps.append(m)
    res = run_bass_kernel_spmd(nc, in_maps, core_ids=list(range(8)))
    return np.stack([r["out"] for r in res.results])


# version 16 (final)
# speedup vs baseline: 1.1662x; 1.1662x over previous
import math
from contextlib import ExitStack

import numpy as np
import ml_dtypes

import concourse.bass as bass
import concourse.mybir as mybir
from concourse.bass_utils import run_bass_kernel_spmd

F32 = mybir.dt.float32
BF16 = mybir.dt.bfloat16
AF = mybir.ActivationFunctionType
ALU = mybir.AluOpType
AX = mybir.AxisListType

T = 4096
D = 1024
NSLAB = 8
NTILE = 32
PI = math.pi

BF = np.dtype(ml_dtypes.bfloat16)


CHAIN_SAME_ENGINE = True


class _EngProxy:

    def __init__(self, kb, eng, sem_name):
        self._kb = kb
        self._eng = eng
        self._sem = sem_name

    def wait_ge(self, semh, val):
        return self._eng.wait_ge(semh, val)

    def __getattr__(self, m):
        f = getattr(self._eng, m)
        kb = self._kb
        sem = self._sem

        def call(*a, **kw):
            ent = kb.sem(sem)
            if CHAIN_SAME_ENGINE and ent[1] > 0:
                self._eng.wait_ge(ent[0], ent[1])
            ins = f(*a, **kw)
            kb.inc(sem, ins)
            kb._last_instr = ins
            kb._last_sem = sem
            return ins
        return call


class KB:
    def __init__(self, nc, ctx):
        self.nc = nc
        self.ctx = ctx
        self.t = nc.tensor
        self.v = nc.vector
        self.s = nc.scalar
        self.g = nc.gpsimd
        self.sp = nc.sync
        self._sems = {}
        self.buf = {}
        self.V = _EngProxy(self, nc.vector, "cv")
        self.S = _EngProxy(self, nc.scalar, "cs")
        self.G = _EngProxy(self, nc.gpsimd, "cg")

    def sem(self, name):
        if name not in self._sems:
            h = self.ctx.enter_context(self.nc.semaphore(name))
            self._sems[name] = [h, 0]
        return self._sems[name]

    _last_instr = None
    _last_sem = None

    def inc(self, name, instr, k=1):
        ent = self.sem(name)
        if instr is self._last_instr and name == self._last_sem:
            return ent[1]
        instr.then_inc(ent[0], k)
        ent[1] += k
        return ent[1]

    def val(self, name):
        return self.sem(name)[1]

    def wait(self, eng, name, val=None):
        ent = self.sem(name)
        eng.wait_ge(ent[0], ent[1] if val is None else val)

    def barrier(self):
        for eng in (self.t, self.v, self.s, self.g, self.sp):
            for snm in ("ct", "cv", "cs", "cg"):
                if snm in self._sems and self._sems[snm][1] > 0:
                    eng.wait_ge(self._sems[snm][0], self._sems[snm][1])

    def sbuf(self, name, shape, dtype):
        h = self.ctx.enter_context(self.nc.sbuf_tensor(name, shape, dtype))
        self.buf[name] = h
        return h

    def psum(self, name, shape, dtype=F32):
        h = self.ctx.enter_context(self.nc.psum_tensor(name, shape, dtype))
        self.buf[name] = h
        return h



def emit_load_xt(k, src_ap):
    nc = k.nc
    XT, xld, xbf, id_bf = k.buf["XT"], k.buf["xld"], k.buf["xbf"], k.buf["id_bf"]
    pT = k.buf["pT"]
    k.wait(k.t, "d_cst", k.val("d_cst"))
    for i in range(NTILE):
        ld = k.inc("d_x", k.sp.dma_start(out=xld[:], in_=src_ap[i * 128:(i + 1) * 128, :]), 16)
        k.wait(k.S, "d_x", ld)
        k.wait(k.S, "ct")
        cs = k.inc("cs", k.S.copy(out=xbf[:], in_=xld[:]))
        k.wait(k.t, "cs", cs)
        tseq(k, [(xbf[:, s * 128:(s + 1) * 128], XT[:, s * T + i * 128: s * T + (i + 1) * 128])
                 for s in range(NSLAB)])
        k.wait(k.sp, "cs", cs)


def emit_ln_tile(k, x_f32_ap, out_f32_ap, eps=1e-5, gain_ap=None, bias_ap=None):
    v, s = k.V, k.S
    W = x_f32_ap.shape[-1]
    sm = k.buf["sm"]
    sq = k.buf["sq"]
    cv = k.inc("cv", v.reduce_sum(out=sm[:, 0:1], in_=x_f32_ap, axis=AX.X))
    k.wait(s, "cv", cv)
    cs = k.inc("cs", s.activation(out=sq[:, 0:W], in_=x_f32_ap, func=AF.Square, accum_out=sm[:, 1:2]))
    k.wait(v, "cs", cs)
    v.tensor_scalar(out=sm[:, 2:3], in0=sm[:, 0:1], scalar1=-1.0 / W, scalar2=None, op0=ALU.mult)
    v.tensor_tensor(out=sm[:, 3:4], in0=sm[:, 2:3], in1=sm[:, 2:3], op=ALU.mult)
    v.tensor_scalar(out=sm[:, 4:5], in0=sm[:, 1:2], scalar1=1.0 / W, scalar2=float(eps), op0=ALU.mult, op1=ALU.add)
    cv = k.inc("cv", v.tensor_tensor(out=sm[:, 5:6], in0=sm[:, 4:5], in1=sm[:, 3:4], op=ALU.subtract))
    k.wait(s, "cv", cv)
    s.activation(out=sm[:, 6:7], in_=sm[:, 5:6], func=AF.Ln)
    cs = k.inc("cs", s.activation(out=sm[:, 7:8], in_=sm[:, 6:7], func=AF.Exp, scale=-0.5))
    k.wait(v, "cs", cs)
    cv = k.inc("cv", v.tensor_scalar(out=out_f32_ap, in0=x_f32_ap, scalar1=sm[:, 2:3],
                                     scalar2=sm[:, 7:8], op0=ALU.add, op1=ALU.mult))
    return cv


def tseq(k, jobs, f32=False):
    t, v = k.t, k.V
    hist = []
    for i, (sap, dap) in enumerate(jobs):
        sp_, sf = sap.shape[-2], sap.shape[-1]
        bank = k.buf["pTB"] if i % 2 == 0 else k.buf["pTB2"]
        ps = bank[0:sf, 0:sp_]
        idn = k.buf["id_bf"][0:sp_, 0:sp_]
        if len(hist) >= 2:
            k.wait(t, "cv", hist[-2])
        ct = k.inc("ct", t.transpose(ps, sap, idn))
        k.wait(v, "ct", ct)
        cv = k.inc("cv", v.tensor_copy(out=dap, in_=ps))
        hist.append(cv)
    k.wait(t, "cv", hist[-1])


def emit_w_bf16_stage(k, W, blocks):
    g, sp = k.g, k.sp
    AA = k.buf["AA"]
    stages = [AA[:, 0:8192], AA[:, 8192:16384]]
    units = []
    if 1 in blocks:
        for e in range(8):
            for grp in range(4):
                units.append(("w1t", "w1bf", e * 4 + grp))
        for e in range(8):
            for grp in range(4):
                units.append(("w2T", "w2bf", e * 4 + grp))
    if 2 in blocks:
        for nm in ("wqt", "wkt", "wvt", "w1at", "wot"):
            units.append((nm, nm + "bf", 0))
    if 3 in blocks:
        units.append(("enct", "encbf", 0))
        units.append(("enct", "encbf", 1))
        units.append(("dec_W", "decbf", 0))
        units.append(("dec_W", "decbf", 1))
    out_done = [0, 0]
    for i, (kind, dstname, grp) in enumerate(units):
        sl_ = i % 2
        sem_in = f"d_cvi{sl_}"
        if out_done[sl_]:
            k.wait(g, f"d_cvo{sl_}", out_done[sl_])
        if kind == "w1t":
            srcap = W["w1t"][grp // 4, (grp % 4) * 8:(grp % 4) * 8 + 8].rearrange("f p s c -> p f (s c)")
            dstap = W[dstname][grp // 4, (grp % 4) * 8:(grp % 4) * 8 + 8].rearrange("f p c -> p f c")
        elif kind == "w2T":
            srcap = W["w2T"][grp // 4, (grp % 4) * 1024:(grp % 4) * 1024 + 1024, :].rearrange("(f p) c -> p f c", p=128)
            dstap = W[dstname][grp // 4, (grp % 4) * 8:(grp % 4) * 8 + 8].rearrange("f p c -> p f c")
        elif kind == "enct":
            srcap = W["enct"][grp * 8:(grp + 1) * 8].rearrange("f p s c -> p f (s c)")
            dstap = W[dstname][grp * 8:(grp + 1) * 8].rearrange("f p c -> p f c")
        elif kind == "dec_W":
            srcap = W["dec_W"][grp * 1024:(grp + 1) * 1024, :].rearrange("(f p) c -> p f c", p=128)
            dstap = W[dstname][grp * 8:(grp + 1) * 8].rearrange("f p c -> p f c")
        else:
            srcap = W[kind].rearrange("f p s c -> p f (s c)")
            dstap = W[dstname].rearrange("f p c -> p f c")
        st3 = stages[sl_].rearrange("p (f c) -> p f c", f=8)
        di = k.inc(sem_in, g.dma_start(out=st3, in_=srcap), 16)
        k.wait(sp, sem_in, di)
        out_done[sl_] = k.inc(f"d_cvo{sl_}", sp.dma_start(out=dstap, in_=st3), 16)
    k._wconv_done = (k.val("d_cvo0"), k.val("d_cvo1"))



def emit_l0_attn(k, src_ap, dst_ap, W):
    nc = k.nc
    t, v, s, g, sp = k.t, k.V, k.S, k.g, k.sp
    XT = k.buf["XT"]
    BIG = k.buf["BIG"]
    id_bf = k.buf["id_bf"]
    pT = k.buf["pT"]
    pT2 = k.buf["pT2"]
    pA, pB, pC, pD = k.buf["pA"], k.buf["pB"], k.buf["pC"], k.buf["pD"]
    pS = k.buf["pS"]
    tmp = k.buf["sq"][:, 0:512]
    tmp2 = k.buf["sq"][:, 512:1024]
    tfull = k.buf["tfull"]
    xld = k.buf["xld"]
    xbf = k.buf["xbf"]
    sm = k.buf["sm"]
    sm2 = k.buf["sm2"]
    rowbuf = k.buf["rowbuf"]
    Pbf = k.buf["Pbf"]
    PT = Pbf
    wbig = k.buf["wbig"]
    wsmall = k.buf["wsmall"]
    omg_bf = k.buf["omg_bf"]
    biasb = k.buf["biasb"]
    cst = k.buf["cst"]
    B4 = k.buf["B4"]

    wsm_f = k.buf["wsmall_f32"]
    if getattr(k, "_wconv_done", None):
        for eng in (g, sp, v, t, s):
            k.wait(eng, "d_cvo0", k._wconv_done[0])
            k.wait(eng, "d_cvo1", k._wconv_done[1])
        k._wconv_done = None
    for sl in range(NSLAB):
        k.inc("d_wg", g.dma_start(out=wbig[:, sl * 1024:(sl + 1) * 1024], in_=W["wvT"][sl * 128:(sl + 1) * 128, :]), 16)
        k.inc("d_wg", g.dma_start(out=wsmall[:, sl * 64:(sl + 1) * 64], in_=W["wshT"][sl * 128:(sl + 1) * 128, :]), 16)
        k.inc("d_wg", g.dma_start(out=wsmall[:, 512 + sl:513 + sl], in_=W["wchT"][sl * 128:(sl + 1) * 128, :]), 16)
    k.inc("d_wg", g.dma_start(out=omg_bf[0:64, :], in_=W["omgT"]), 16)
    k.inc("d_cst", sp.dma_start(out=wsm_f[:, 0:64], in_=W["wcmT"][0:128, :]), 16)
    k.inc("d_cst", sp.dma_start(out=wsm_f[:, 64:128], in_=W["wcmT"][128:256, :]), 16)
    k.inc("d_cst", sp.dma_start(out=wsm_f[0:64, 128:192], in_=W["wcmT"][256:320, :]), 16)
    k.inc("d_cst", sp.dma_start(out=biasb[:, 0:1024], in_=W["r_bv"]), 16)
    k.inc("d_cst", sp.dma_start(out=biasb[:, 1024:2048], in_=W["r_bout"]), 16)
    k.inc("d_cst", sp.dma_start(out=biasb[:, 2048:2112], in_=W["r_bsh"]), 16)
    k.inc("d_cst", sp.dma_start(out=biasb[:, 2112:2144], in_=W["r_rffb"]), 16)
    k.inc("d_cst", sp.dma_start(out=biasb[:, 2144:2145], in_=W["r_bch"]), 16)
    k.inc("d_cst", sp.dma_start(out=biasb[:, 2176:2177], in_=W["r_lt"]), 16)
    k.inc("d_cst", sp.dma_start(out=cst[:, 0:512], in_=W["c_brow"]), 16)
    k.inc("d_cst", sp.dma_start(out=cst[:, 512:513], in_=W["c_colb"]), 16)
    k.inc("d_cst", sp.dma_start(out=cst[:, 513:516], in_=W["c_mags"]), 16)
    k.inc("d_wg", g.dma_start(out=B4[:], in_=W["c_B4"]), 16)
    dg0, dsp0 = k.val("d_wg"), k.val("d_cst")

    k.wait(v, "d_wg", dg0)
    k.wait(v, "d_cst", dsp0)
    v.memset(wsm_f[64:128, 128:192], 0.0)
    v.memset(k.buf["tp_bf"][:, 320:384], 0.0)
    for c3 in range(3):
        k.inc("cv", v.tensor_scalar(out=k.buf["wcm_bf"][:, c3 * 64:(c3 + 1) * 64],
                                    in0=wsm_f[:, c3 * 64:(c3 + 1) * 64],
                                    scalar1=cst[:, 513 + c3:514 + c3], scalar2=None, op0=ALU.mult))
    k.wait(s, "d_cst", dsp0)
    k.wait(s, "d_wg", dg0)
    lt = biasb[:, 2176:2177]
    cv0 = k.inc("cv", v.scalar_tensor_tensor(out=sm2[:, 0:1], in0=lt, scalar=-1.0, in1=lt,
                                             op0=ALU.mult, op1=ALU.max))
    k.wait(s, "cv", cv0)
    s.activation(out=sm2[:, 1:2], in_=sm2[:, 0:1], func=AF.Exp, scale=-1.0)
    cs = k.inc("cs", s.activation(out=sm2[:, 2:3], in_=sm2[:, 1:2], func=AF.Ln, bias=1.0))
    k.wait(v, "cs", cs)
    v.tensor_scalar(out=sm2[:, 3:4], in0=lt, scalar1=0.0, scalar2=None, op0=ALU.max)
    v.tensor_tensor(out=sm2[:, 4:5], in0=sm2[:, 3:4], in1=sm2[:, 2:3], op=ALU.add)
    v.tensor_scalar(out=sm2[:, 5:6], in0=sm2[:, 4:5], scalar1=0.1, scalar2=None, op0=ALU.add)
    v.reciprocal(out=sm2[:, 6:7], in_=sm2[:, 5:6])
    cv = k.inc("cv", v.tensor_tensor(out=sm2[:, 7:8], in0=cst[:, 512:513], in1=sm2[:, 6:7], op=ALU.mult))

    wv = wbig
    k.wait(t, "cv", cv)
    for i in range(NTILE):
        for half in range(2):
            pp = pA if half == 0 else pB
            for sl in range(NSLAB):
                mm = t.matmul(pp[:], XT[:, sl * T + i * 128:sl * T + (i + 1) * 128],
                              wv[:, sl * 1024 + half * 512: sl * 1024 + (half + 1) * 512],
                              start=(sl == 0), stop=(sl == NSLAB - 1))
            ct = k.inc("ct", mm)
            k.wait(v, "ct", ct)
            cv = k.inc("cv", v.tensor_tensor(out=BIG[:, i * 1024 + half * 512: i * 1024 + (half + 1) * 512],
                                             in0=pp[:], in1=biasb[:, half * 512:(half + 1) * 512], op=ALU.add))
        k.wait(t, "cv", cv)

    wsh = _bfview(k, "wsh")
    wch = _bfview(k, "wch")
    wcm = k.buf["wcm_bf"]
    for i in range(NTILE):
        for sl in range(NSLAB):
            mmq = t.matmul(pD[:, 0:1], XT[:, sl * T + i * 128:sl * T + (i + 1) * 128],
                           wch[:, sl:sl + 1], start=(sl == 0), stop=(sl == NSLAB - 1))
        ct_q = k.inc("ct", mmq)
        for sl in range(NSLAB):
            mms = t.matmul(pS[:, 1:65], XT[:, sl * T + i * 128:sl * T + (i + 1) * 128],
                           wsh[:, sl * 64:(sl + 1) * 64], start=(sl == 0), stop=(sl == NSLAB - 1))
        ct_sh = k.inc("ct", mms)
        k.wait(s, "ct", ct_q)
        cs = k.inc("cs", s.activation(out=sm2[:, 8:9], in_=pD[:, 0:1], func=AF.Tanh, bias=biasb[:, 2144:2145]))
        k.wait(v, "cs", cs)
        q = sm2[:, 8:9]
        v.memset(sm2[:, 9:10], 1.0)
        v.tensor_tensor(out=sm2[:, 10:11], in0=q, in1=q, op=ALU.mult)
        v.tensor_tensor(out=sm2[:, 11:12], in0=sm2[:, 10:11], in1=q, op=ALU.mult)
        cv = k.inc("cv", v.tensor_tensor(out=sm2[:, 12:13], in0=sm2[:, 10:11], in1=sm2[:, 10:11], op=ALU.mult))
        k.wait(v, "ct", ct_sh)
        sh = k.buf["sh64"]
        cv = k.inc("cv", v.tensor_tensor(out=sh[:, 0:64], in0=pS[:, 1:65],
                                         in1=biasb[:, 2048:2112], op=ALU.add))
        k.wait(s, "cv", cv)
        cs = k.inc("cs", s.activation(out=sh[:, 128:192], in_=sh[:, 0:64], func=AF.Square, accum_out=sm2[:, 14:15]))
        k.wait(v, "cs", cs)
        cv = k.inc("cv", v.tensor_scalar(out=sm2[:, 15:16], in0=sm2[:, 14:15], scalar1=1e-24, scalar2=None, op0=ALU.max))
        k.wait(s, "cv", cv)
        s.activation(out=sm2[:, 17:18], in_=sm2[:, 15:16], func=AF.Ln)
        cs = k.inc("cs", s.activation(out=sm2[:, 16:17], in_=sm2[:, 17:18], func=AF.Exp, scale=-0.5))
        k.wait(v, "cs", cs)
        shn = k.buf["shn_bf"]
        cv = k.inc("cv", v.tensor_scalar(out=shn[:], in0=sh[:, 0:64], scalar1=sm2[:, 16:17],
                                         scalar2=None, op0=ALU.mult))
        k.wait(t, "cv", cv)
        ct = k.inc("ct", t.transpose(pT[0:64, 0:128], shn[:], id_bf[:]))
        k.wait(v, "ct", ct)
        shnT = k.buf["shnT_bf"]
        cv = k.inc("cv", v.tensor_copy(out=shnT[0:64, :], in_=pT[0:64, 0:128]))
        k.wait(t, "cv", cv)
        ct = k.inc("ct", t.matmul(pS[:, 1:33], shnT[0:64, :], omg_bf[0:64, :], start=True, stop=True))
        k.wait(v, "ct", ct)
        pj = k.buf["pj"]
        v.tensor_tensor(out=pj[:, 0:32], in0=pS[:, 1:33], in1=biasb[:, 2112:2144], op=ALU.add)
        cv = k.inc("cv", v.tensor_scalar(out=pj[:, 32:64], in0=pj[:, 0:32], scalar1=PI / 2, scalar2=None, op0=ALU.add))
        for it in range(2):
            for colofs in (0, 32):
                w_ap = pj[:, colofs:colofs + 32]
                v.tensor_scalar(out=pj[:, 64 + colofs:96 + colofs], in0=w_ap, scalar1=PI, scalar2=-2 * PI,
                                op0=ALU.is_gt, op1=ALU.mult)
                v.tensor_tensor(out=w_ap, in0=w_ap, in1=pj[:, 64 + colofs:96 + colofs], op=ALU.add)
                v.tensor_scalar(out=pj[:, 64 + colofs:96 + colofs], in0=w_ap, scalar1=-PI, scalar2=2 * PI,
                                op0=ALU.is_lt, op1=ALU.mult)
                cv = k.inc("cv", v.tensor_tensor(out=w_ap, in0=w_ap, in1=pj[:, 64 + colofs:96 + colofs], op=ALU.add))
        k.wait(s, "cv", cv)
        phis = k.buf["phis_bf"]
        s.activation(out=phis[:, 32:64], in_=pj[:, 0:32], func=AF.Sin)
        cs = k.inc("cs", s.activation(out=phis[:, 0:32], in_=pj[:, 32:64], func=AF.Sin))
        k.wait(v, "cs", cs)
        tp_bf = k.buf["tp_bf"]
        phic_cols = [sm2[:, 9:10], sm2[:, 8:9], sm2[:, 10:11], sm2[:, 11:12], sm2[:, 12:13]]
        for c in range(5):
            cv = k.inc("cv", v.tensor_scalar(out=tp_bf[:, c * 64:(c + 1) * 64], in0=phis[:],
                                             scalar1=phic_cols[c], scalar2=None, op0=ALU.mult))
        k.wait(t, "cv", cv)
        tpT = k.buf["tpT_bf"]
        for c3 in range(3):
            ct = k.inc("ct", t.transpose(pT[:], tp_bf[:, c3 * 128:(c3 + 1) * 128], id_bf[:]))
            k.wait(v, "ct", ct)
            cv = k.inc("cv", v.tensor_copy(out=tpT[:, c3 * 128:(c3 + 1) * 128], in_=pT[:]))
            k.wait(t, "cv", cv)
        k.wait(t, "cv", cv)
        for c3 in range(3):
            kk = 128 if c3 < 2 else 64
            mmp = t.matmul(k.buf["pPhi"][0:64, 0:128],
                           wcm[0:kk, c3 * 64:(c3 + 1) * 64],
                           tpT[0:kk, c3 * 128:(c3 + 1) * 128],
                           start=(c3 == 0), stop=(c3 == 2))
        ct = k.inc("ct", mmp)
        k.wait(v, "ct", ct)
        cv = k.inc("cv", v.tensor_copy(out=phiap(k, i * 128, 128), in_=k.buf["pPhi"][0:64, 0:128]))
        k.wait(t, "cv", cv)

    k.wait(g, "ct", k.val("ct"))
    k.wait(g, "cv", k.val("cv"))
    for sl in range(NSLAB):
        dg = k.inc("d_wo", g.dma_start(out=wbig[:, sl * 1024:(sl + 1) * 1024], in_=W["woT"][sl * 128:(sl + 1) * 128, :]), 16)

    brow = cst[:, 0:512]
    for i in range(NTILE):
        jmax = (i * 128 + 127) // 512
        pa_cv = getattr(k, "_pa_cv", {0: 0, 1: 0})
        k._pa_cv = pa_cv
        for j in range(jmax + 1):
            pSb = pC if j % 2 == 0 else pD
            if pa_cv[j % 2]:
                k.wait(t, "cv", pa_cv[j % 2])
            ct = k.inc("ct", t.matmul(pSb[:], phiap(k, i * 128, 128),
                                      phiap(k, j * 512, 512), start=True, stop=True))
            k.wait(v, "ct", ct)
            v.tensor_tensor(out=tmp[:], in0=pSb[:], in1=brow, op=ALU.mult)
            delta = i * 128 - j * 512
            sc = math.exp(-0.03 * delta) if delta < 500 else 0.0
            v.tensor_scalar(out=sm2[:, 20:21], in0=sm2[:, 7:8], scalar1=float(sc), scalar2=None, op0=ALU.mult)
            if j == jmax and delta < 512:
                v.tensor_scalar(out=tmp2[:], in0=tmp[:], scalar1=sm2[:, 20:21], scalar2=None, op0=ALU.mult)
                cv = k.inc("cv", v.tensor_tensor(out=rowbuf[:, j * 512:(j + 1) * 512], in0=tmp2[:],
                                                 in1=B4[:, (delta // 128) * 512:(delta // 128 + 1) * 512], op=ALU.add))
            else:
                cv = k.inc("cv", v.tensor_scalar(out=rowbuf[:, j * 512:(j + 1) * 512], in0=tmp[:],
                                                 scalar1=sm2[:, 20:21], scalar2=None, op0=ALU.mult))
            pa_cv[j % 2] = k.val("cv")
        wrow = (jmax + 1) * 512
        v.reduce_max(out=sm2[:, 22:23], in_=rowbuf[:, 0:wrow], axis=AX.X)
        cv = k.inc("cv", v.tensor_scalar(out=sm2[:, 23:24], in0=sm2[:, 22:23], scalar1=-1.0, scalar2=None, op0=ALU.mult))
        k.wait(s, "cv", cv)
        cs = k.inc("cs", s.activation(out=Pbf[:, 0:wrow], in_=rowbuf[:, 0:wrow],
                                      func=AF.Exp, bias=sm2[:, 23:24], accum_out=sm2[:, 33:34]))
        k.wait(v, "cs", cs)
        cv = k.inc("cv", v.reciprocal(out=sm2[:, 34:35], in_=sm2[:, 33:34]))
        k.wait(t, "cv", k.val("cv"))
        k.wait(t, "cs", k.val("cs"))
        tseq(k, [(Pbf[:, mb * 128:(mb + 1) * 128], PT[:, mb * 128:(mb + 1) * 128])
                 for mb in range(i + 1)])
        for half in range(2):
            pp = pA if half == 0 else pB
            for mb in range(i + 1):
                mm = t.matmul(pp[:], PT[:, mb * 128:(mb + 1) * 128],
                              BIG[:, mb * 1024 + half * 512: mb * 1024 + (half + 1) * 512],
                              start=(mb == 0), stop=(mb == i))
            k.inc("ct", mm)
        ct = k.val("ct")
        ao = xbf
        k.wait(v, "ct", ct)
        for half in range(2):
            pp = pA if half == 0 else pB
            cv = k.inc("cv", v.tensor_scalar(out=ao[:, half * 512:(half + 1) * 512], in0=pp[:],
                                             scalar1=sm2[:, 34:35], scalar2=None, op0=ALU.mult))
        if i == 0:
            k.wait(t, "d_wo", dg)
        aofm = k.buf["aofm"]
        k.wait(t, "cv", cv)
        tseq(k, [(ao[:, sl * 128:(sl + 1) * 128], aofm[:, sl * 128:(sl + 1) * 128])
                 for sl in range(NSLAB)])
        ld = k.inc("d_x", sp.dma_start(out=xld[:], in_=src_ap[i * 128:(i + 1) * 128, :]), 16)
        for half in range(2):
            pp = pC if half == 0 else pD
            for sl in range(NSLAB):
                mm = t.matmul(pp[:], aofm[:, sl * 128:(sl + 1) * 128],
                              wbig[:, sl * 1024 + half * 512: sl * 1024 + (half + 1) * 512],
                              start=(sl == 0), stop=(sl == NSLAB - 1))
            k.inc("ct", mm)
        ct = k.val("ct")
        k.wait(v, "ct", ct)
        k.wait(v, "d_x", k.val("d_x"))
        for half in range(2):
            pp = pC if half == 0 else pD
            v.tensor_tensor(out=tmp[:], in0=pp[:], in1=biasb[:, 1024 + half * 512:1024 + (half + 1) * 512], op=ALU.add)
            cv = k.inc("cv", v.tensor_tensor(out=tfull[:, half * 512:(half + 1) * 512], in0=tmp[:],
                                             in1=xld[:, half * 512:(half + 1) * 512], op=ALU.add))
        cv = emit_ln_tile(k, tfull[:, 0:1024], xld[:, 0:1024])
        k.wait(g, "cv", cv)
        k.inc("d_st", g.dma_start(out=dst_ap[i * 128:(i + 1) * 128, :], in_=xld[:]), 16)
        k.wait(s, "cv", cv)
        cs = k.inc("cs", s.copy(out=xbf[:], in_=xld[:]))
        k.wait(t, "cs", cs)
        tseq(k, [(xbf[:, sl * 128:(sl + 1) * 128], XT[:, sl * T + i * 128:sl * T + (i + 1) * 128])
                 for sl in range(NSLAB)])
        k.wait(sp, "d_st", k.val("d_st"))
        k.wait(sp, "cv", k.val("cv"))
        k.wait(sp, "cs", k.val("cs"))


def phiap(k, col, width):
    return k.buf["phi"][0:64, col:col + width]


def _wcm_f32_view(k, c3):
    wsm = k.buf["wsmall_f32"]
    return wsm[:, c3 * 64:(c3 + 1) * 64]


def _bfview(k, which):
    if which == "wsh":
        return k.buf["wsmall"][:, 0:512]
    if which == "wch":
        return k.buf["wsmall"][:, 512:520]
    raise KeyError(which)



def emit_l0_ffn(k, src_ap, dst_ap, W):
    t, v, s, g, sp = k.t, k.V, k.S, k.g, k.sp
    b = k.buf
    XT = k.buf["XT"]
    w2buf, h1buf, w1buf = k.buf["w2buf"], k.buf["h1buf"], k.buf["w1buf"]
    b2buf, b1c = k.buf["b2buf"], k.buf["b1c"]
    acc = k.buf["acc"]
    smA, smB, w_tm, rwT = k.buf["sm3"], k.buf["sm4"], k.buf["w_tm"], k.buf["rwT_f32"]
    xT32 = k.buf["xT32"]
    xbf2 = k.buf["xbf2"]
    idf = k.buf["id_f32"]
    id_bf = k.buf["id_bf"]
    pA, pB, pC, pD = k.buf["pA"], k.buf["pB"], k.buf["pC"], k.buf["pD"]
    pT32, pT, pT2 = k.buf["pT32"], k.buf["pT"], k.buf["pT2"]

    for sl in range(NSLAB):
        k.inc("d_cst", sp.dma_start(out=rwT[:, sl * 8:(sl + 1) * 8],
                                    in_=W["rwT32"][sl * 128:(sl + 1) * 128, :]), 16)
    k.inc("d_wg", g.dma_start(out=b1c[:], in_=W["b1c"]), 16)
    k.inc("d_wg", g.dma_start(out=b["b2all"][0:8, :], in_=W["b2r"]), 16)
    dcst0 = k.val("d_cst")
    dwg0 = k.val("d_wg")

    w1slot_ct = [0, 0, 0, 0]
    ga_cs = {0: 0, 1: 0}
    oe_cv = {0: 0, 1: 0}
    rw_bf = k.buf["rw_bf"]
    k.wait(v, "d_cst", dcst0)
    cv_rw = k.inc("cv", v.tensor_copy(out=rw_bf[:], in_=rwT[:]))
    rl_cv = 0
    for tc in range(4):
        k.wait(t, "d_cst", dcst0)
        k.wait(t, "cv", k.val("cv"))
        for tt in range(8):
            gt_ = tc * 8 + tt
            accap = acc[:, tt * 1024:(tt + 1) * 1024]
            ldx = k.inc("d_x", sp.dma_start(out=accap, in_=src_ap[gt_ * 128:(gt_ + 1) * 128, :]), 16)
            rl8 = smB[:, 0:8]
            if rl_cv:
                k.wait(t, "cv", rl_cv)
            for sl in range(NSLAB):
                mmr = t.matmul(pD[:, 0:8], XT[:, sl * T + gt_ * 128: sl * T + (gt_ + 1) * 128],
                               rw_bf[:, sl * 8:(sl + 1) * 8],
                               start=(sl == 0), stop=(sl == NSLAB - 1))
            ct = k.inc("ct", mmr)
            k.wait(v, "ct", ct)
            rl_cv = k.inc("cv", v.tensor_copy(out=rl8, in_=pD[:, 0:8]))
            v.reduce_max(out=smA[:, 0:1], in_=rl8, axis=AX.X)
            v.tensor_scalar(out=smB[:, 8:16], in0=rl8, scalar1=smA[:, 0:1], scalar2=None, op0=ALU.is_ge)
            v.scalar_tensor_tensor(out=smB[:, 16:24], in0=smB[:, 8:16], scalar=-1e30, in1=rl8,
                                   op0=ALU.mult, op1=ALU.add)
            v.reduce_max(out=smA[:, 1:2], in_=smB[:, 16:24], axis=AX.X)
            v.tensor_scalar(out=smB[:, 8:16], in0=rl8, scalar1=smA[:, 1:2], scalar2=None, op0=ALU.is_ge)
            cv = k.inc("cv", v.tensor_scalar(out=smA[:, 2:3], in0=smA[:, 0:1], scalar1=-1.0, scalar2=None, op0=ALU.mult))
            k.wait(s, "cv", cv)
            cs = k.inc("cs", s.activation(out=smB[:, 16:24], in_=rl8, func=AF.Exp, bias=smA[:, 2:3]))
            k.wait(v, "cs", cs)
            v.tensor_tensor(out=smB[:, 24:32], in0=smB[:, 16:24], in1=smB[:, 8:16], op=ALU.mult)
            v.reduce_sum(out=smA[:, 3:4], in_=smB[:, 24:32], axis=AX.X)
            v.reciprocal(out=smA[:, 4:5], in_=smA[:, 3:4])
            cv = k.inc("cv", v.tensor_scalar(out=w_tm[:, tt * 8:(tt + 1) * 8], in0=smB[:, 24:32],
                                             scalar1=smA[:, 4:5], scalar2=None, op0=ALU.mult))
            k.wait(sp, "cv", cv)
        k.wait(s, "d_wg", dwg0)
        k.wait(v, "d_x", k.val("d_x"))
        for e in range(8):
            k.wait(sp, "ct", k.val("ct"))
            k.wait(g, "ct", k.val("ct"))
            k.wait(g, "cv", k.val("cv"))
            for grp in range(4):
                k.inc("d_w2", sp.dma_start(
                    out=w2buf[:, grp * 8192:(grp + 1) * 8192].rearrange("p (f c) -> p f c", f=8),
                    in_=W["w2bf"][e, grp * 8:(grp + 1) * 8].rearrange("f p c -> p f c")), 16)
            dw2 = k.val("d_w2")
            for half in range(2):
                for fc in range(32):
                    pslot = (fc // 2) % 2
                    sname = f"d_w1s{pslot}"
                    if fc % 2 == 0:
                        k.wait(sp, "ct", w1slot_ct[pslot])
                        k.inc(sname, sp.dma_start(
                            out=w1buf[:, pslot * 2048:(pslot + 1) * 2048].rearrange("p (f c) -> p f c", f=2),
                            in_=W["w1bf"][e, fc:fc + 2].rearrange("f p c -> p f c")), 16)
                    k.wait(t, sname, k.val(sname))
                    slot = fc % 4
                    pp = (pA, pB, pC, pD)[fc % 4]
                    if ga_cs.get(fc % 4):
                        k.wait(t, "cs", ga_cs[fc % 4])
                    if oe_cv.get(fc % 4):
                        k.wait(t, "cv", oe_cv[fc % 4])
                    for sl in range(NSLAB):
                        mmh = t.matmul(pp[:, 0:512],
                                       w1buf[:, slot * 1024 + sl * 128: slot * 1024 + (sl + 1) * 128],
                                       XT[:, sl * T + tc * 1024 + half * 512: sl * T + tc * 1024 + half * 512 + 512],
                                       start=(sl == 0), stop=(sl == NSLAB - 1))
                    ct_h = k.inc("ct", mmh)
                    if fc % 2 == 1:
                        w1slot_ct[pslot] = ct_h
                    k.wait(s, "ct", ct_h)
                    cs = k.inc("cs", s.activation(out=h1buf[:, fc * 512:(fc + 1) * 512], in_=pp[:, 0:512],
                                                  func=AF.Gelu, bias=b1c[:, e * 32 + fc:e * 32 + fc + 1]))
                    ga_cs[fc % 4] = cs
                k.wait(t, "d_w2", dw2)
                k.wait(t, "cs", k.val("cs"))
                for tt in range(4):
                    for dh in range(2):
                        u = tt * 2 + dh
                        pp = (pA, pB, pC, pD)[u % 4]
                        if oe_cv.get(u % 4):
                            k.wait(t, "cv", oe_cv[u % 4])
                        if ga_cs.get(u % 4):
                            k.wait(t, "cs", ga_cs[u % 4])
                        for fs in range(32):
                            mmo = t.matmul(pp[:, 0:512],
                                           h1buf[:, fs * 512 + tt * 128: fs * 512 + (tt + 1) * 128],
                                           w2buf[:, fs * 1024 + dh * 512: fs * 1024 + dh * 512 + 512],
                                           start=(fs == 0), stop=(fs == 31))
                        ct_o = k.inc("ct", mmo)
                        k.wait(v, "ct", ct_o)
                        k.wait(v, "d_w2", dw2)
                        gtt = half * 4 + tt
                        wcol = w_tm[:, gtt * 8 + e: gtt * 8 + e + 1]
                        accap = acc[:, gtt * 1024 + dh * 512: gtt * 1024 + dh * 512 + 512]
                        cv = k.inc("cv", v.scalar_tensor_tensor(out=accap, in0=pp[:, 0:512], scalar=wcol,
                                                                in1=accap, op0=ALU.mult, op1=ALU.add))
                        oe_cv[u % 4] = cv
        for tt8 in range(8):
            k.wait(s, "ct", k.val("ct"))
            cs = k.inc("cs", k.S.copy(out=b["w8bf"][:, 0:8], in_=w_tm[:, tt8 * 8:(tt8 + 1) * 8]))
            k.wait(t, "cs", cs)
            ct = k.inc("ct", t.transpose(k.buf["pTB"][0:8, 0:128], b["w8bf"][:, 0:8], id_bf[:]))
            k.wait(v, "ct", ct)
            cv = k.inc("cv", v.tensor_copy(out=b["w8bf"][0:8, 8:136], in_=k.buf["pTB"][0:8, 0:128]))
            k.wait(t, "cv", cv)
            for dh in range(2):
                pp = pC if dh == 0 else pD
                ct = k.inc("ct", t.matmul(pp[:, 0:512], b["w8bf"][0:8, 8:136],
                                          b["b2all"][0:8, dh * 512:(dh + 1) * 512], start=True, stop=True))
                k.wait(v, "ct", ct)
                accap = acc[:, tt8 * 1024 + dh * 512: tt8 * 1024 + dh * 512 + 512]
                cv = k.inc("cv", v.tensor_tensor(out=accap, in0=accap, in1=pp[:, 0:512], op=ALU.add))
            k.wait(t, "cv", cv)
        for tt8 in range(8):
            gt_ = tc * 8 + tt8
            accap = acc[:, tt8 * 1024:(tt8 + 1) * 1024]
            cv = emit_ln_tile(k, accap, accap)
            k.wait(g, "cv", cv)
            k.inc("d_st", g.dma_start(out=dst_ap[gt_ * 128:(gt_ + 1) * 128, :], in_=accap), 16)
            k.wait(s, "cv", cv)
            cs = k.inc("cs", k.S.copy(out=xbf2[:], in_=accap))
            k.wait(t, "cs", cs)
            tseq(k, [(xbf2[:, sl * 128:(sl + 1) * 128], XT[:, sl * T + gt_ * 128:sl * T + (gt_ + 1) * 128])
                     for sl in range(NSLAB)])
        k.wait(sp, "cv", k.val("cv"))
        k.wait(sp, "cs", k.val("cs"))
        k.wait(sp, "d_st", k.val("d_st"))



WIN = 512
NWIN = T // WIN
NCH = WIN // 32
NTL = WIN // 128


def emit_l1_attn(k, src_ap, dst_ap, W):
    t, v, s, g, sp = k.t, k.V, k.S, k.g, k.sp
    b = k.buf
    XT = b["XT"]
    id_bf = b["id_bf"]
    pA, pB, pC, pD = b["pA"], b["pB"], b["pC"], b["pD"]
    pM = b["pMISC"]
    pT, pT2 = b["pT"], b["pT2"]
    v_fm = b["v_fm"]; fs_tm = b["fs_tm"]; fl_tm = b["fl_tm"]; v_tm = b["v_tm"]
    od_tm = b["od_tm"]; u4 = b["u4"]; w4 = b["w4"]; xw_bf = b["xw_bf"]
    q_fm = b["q_fm"]; k_fm = b["k_fm"]; kn_tm = b["kn_tm"]; w_fm2 = b["w_fm2"]
    wst = b["wst"]; h1b = b["h1b"]
    Sbf = b["Sbf"]
    ABS = b["ABS"]
    atn = ABS[:, 0:64]
    hist = ABS[:, 64:136]
    vhist = ABS[:, 136:640]
    btm = b["btm"]
    w2gT = ABS[:, 640:672]
    ones1 = ABS[0:1, 672:800]
    onesc = ABS[:, 800:801]
    wbT = ABS[:, 801:833]
    c_triu = ABS[:, 833:897]

    b["c_triu"] = c_triu
    A4 = b["A4"]; Pc = b["Pc"]; PTc = b["PTc"]
    mixs = b["mixs"]
    w1bT = b["w1bT"]
    rstd_bf = b["rstd_bf"]
    stats_bf = b["stats_bf"]
    Sf = b["Sf"]
    beta_f = b["beta_f"]
    strow = b["strow"]
    sq = b["sq"]
    lnin = b["lnin"]
    sm = b["sm"]; sm2 = b["sm2"]
    onwb = b["onwb"]
    convw = b["convw"]
    firws = b["firws"]; firwl = b["firwl"]
    gsm = b["gsm"]
    statsb = W["statsb"]

    if getattr(k, "_wconv_done", None):
        for eng in (g, sp, v, t, s):
            k.wait(eng, "d_cvo0", k._wconv_done[0])
            k.wait(eng, "d_cvo1", k._wconv_done[1])
        k._wconv_done = None
    k.inc("d_cst", sp.dma_start(out=convw[:], in_=W["c_conv"]), 16)
    k.inc("d_cst", sp.dma_start(out=firws[:], in_=W["c_firs"]), 16)
    k.inc("d_cst", sp.dma_start(out=firwl[:], in_=W["c_firl"]), 16)
    k.inc("d_cst", sp.dma_start(out=onwb[:], in_=W["r_onw"]), 16)
    k.inc("d_cst", sp.dma_start(out=gsm[:, 8:12], in_=W["r_glt"]), 16)
    k.inc("d_cst", sp.dma_start(out=gsm[:, 12:16], in_=W["r_gcb"]), 16)
    k.inc("d_cst", sp.dma_start(out=gsm[:, 6:7], in_=W["gb2col"]), 16)
    k.inc("d_cst", sp.dma_start(out=b["gb1c"][:], in_=W["gb1c"]), 16)
    dcst1 = k.val("d_cst")
    k.inc("d_wg", g.dma_start(out=w2gT[:], in_=W["w2gT"]), 16)
    k.inc("d_wg", g.dma_start(out=wbT[:], in_=W["wbTs"]), 16)
    k.inc("d_wg", g.dma_start(out=c_triu[:], in_=W["c_triu"]), 16)
    dwg1 = k.val("d_wg")
    DECAY = 1.0 - 1.0 / 3000.0
    EPSF = 0.08 * DECAY
    k.wait(s, "d_cst", dcst1)
    s.activation(out=gsm[:, 0:4], in_=gsm[:, 8:12], func=AF.Exp)
    cs = k.val("cs")
    k.wait(v, "cs", cs)
    v.reciprocal(out=gsm[:, 4:5], in_=gsm[:, 0:1])
    v.reciprocal(out=gsm[:, 5:6], in_=gsm[:, 1:2])
    v.reciprocal(out=gsm[:, 7:8], in_=gsm[:, 2:3])
    cv = k.inc("cv", v.reciprocal(out=gsm[:, 0:1], in_=gsm[:, 3:4]))
    itcol = {0: 4, 1: 5, 2: 7, 3: 0}
    v.tensor_scalar(out=gsm[:, 1:2], in0=gsm[:, 12:13], scalar1=DECAY, scalar2=None, op0=ALU.mult)
    v.tensor_scalar(out=gsm[:, 2:3], in0=gsm[:, 13:14], scalar1=DECAY, scalar2=None, op0=ALU.mult)
    v.tensor_scalar(out=gsm[:, 3:4], in0=gsm[:, 14:15], scalar1=DECAY, scalar2=None, op0=ALU.mult)
    cv = k.inc("cv", v.tensor_scalar(out=gsm[:, 8:9], in0=gsm[:, 15:16], scalar1=DECAY, scalar2=None, op0=ALU.mult))
    cbcol = {0: 1, 1: 2, 2: 3, 3: 8}
    v.memset(ones1[:], 1.0)
    v.memset(onesc[:], 1.0)
    v.memset(hist[:], 0.0)
    v.memset(vhist[:], 0.0)
    for h in range(4):
        v.memset(Sf[:, h * 512:(h + 1) * 512], 0.0)
    cv = k.inc("cv", v.memset(Sbf[:], 0.0))
    k.wait(t, "cv", cv)
    k.wait(t, "d_wg", dwg1)

    wslot_ct = [0, 0]

    def stream_w(name, c, eng_gate_ct=None):
        slot = c % 2
        k.wait(sp, "ct", wslot_ct[slot])
        sem = f"d_ws{slot}"
        k.inc(sem, sp.dma_start(out=wst[:, slot * 1024:(slot + 1) * 1024],
                                in_=W[name + "bf"][c]), 16)
        k.wait(t, sem, k.val(sem))
        return slot

    for w in range(NWIN):
        tok0 = w * WIN
        k.barrier()
        for ti, (wname, fmbuf, cwofs) in enumerate(
                (("wqt", q_fm, 0), ("wkt", k_fm, 32), ("wvt", v_fm, 64))):
            for c in range(NSLAB):
                slot = stream_w(wname, c)
                pp = pA if c % 2 == 0 else pB
                for sl in range(NSLAB):
                    mmp = t.matmul(pp[:, 0:512], wst[:, slot * 1024 + sl * 128: slot * 1024 + (sl + 1) * 128],
                                   XT[:, sl * T + tok0: sl * T + tok0 + WIN],
                                   start=(sl == 0), stop=(sl == NSLAB - 1))
                ct = k.inc("ct", mmp)
                wslot_ct[slot] = ct
                k.wait(v, "ct", ct)
                st_ = b["convst"]
                G = k.G
                v.tensor_copy(out=st_[:, 0:3], in_=hist[:, (ti * 8 + c) * 3:(ti * 8 + c) * 3 + 3])
                cv = k.inc("cv", v.tensor_copy(out=st_[:, 3:515], in_=pp[:, 0:512]))
                k.wait(t, "cv", cv)
                v.tensor_copy(out=hist[:, (ti * 8 + c) * 3:(ti * 8 + c) * 3 + 3], in_=st_[:, 512:515])
                v.tensor_scalar(out=fmbuf[:, c * 512:(c + 1) * 512], in0=st_[:, 0:512],
                                scalar1=convw[:, cwofs + c * 4:cwofs + c * 4 + 1], scalar2=None, op0=ALU.mult)
                for j in range(1, 4):
                    v.scalar_tensor_tensor(out=fmbuf[:, c * 512:(c + 1) * 512], in0=st_[:, j:512 + j],
                                           scalar=convw[:, cwofs + c * 4 + j:cwofs + c * 4 + j + 1],
                                           in1=fmbuf[:, c * 512:(c + 1) * 512], op0=ALU.mult, op1=ALU.add)
                cg = k.val("cv")
                k.wait(s, "cv", cg)
                cs = k.inc("cs", s.activation(out=b["siltmp"][:], in_=fmbuf[:, c * 512:(c + 1) * 512],
                                              func=AF.Sigmoid))
                k.wait(v, "cs", cs)
                cv = k.inc("cv", v.tensor_tensor(out=fmbuf[:, c * 512:(c + 1) * 512],
                                                 in0=fmbuf[:, c * 512:(c + 1) * 512],
                                                 in1=b["siltmp"][:], op=ALU.mult))
        for sl in range(NSLAB):
            mmb = t.matmul(pM[0:4, 0:512], wbT[:, sl * 4:(sl + 1) * 4],
                           XT[:, sl * T + tok0: sl * T + tok0 + WIN],
                           start=(sl == 0), stop=(sl == NSLAB - 1))
        ct = k.inc("ct", mmb)
        k.wait(s, "ct", ct)
        cs = k.inc("cs", s.activation(out=b["beta_bf"][0:4, :], in_=pM[0:4, 0:512], func=AF.Sigmoid))
        k.wait(t, "cs", cs)
        for tl in range(NTL):
            ct = k.inc("ct", t.transpose(pT[0:128, 0:4], b["beta_bf"][0:4, tl * 128:(tl + 1) * 128], id_bf[0:4, 0:4]))
            k.wait(v, "ct", ct)
            cv = k.inc("cv", v.tensor_copy(out=btm[:, tl * 4:(tl + 1) * 4], in_=pT[0:128, 0:4]))
            k.wait(t, "cv", cv)
        cv = k.inc("cv", v.tensor_scalar(out=b["nbt"][:], in0=btm[:], scalar1=-1.0, scalar2=None, op0=ALU.mult))
        k.barrier()
        for fmbuf in (q_fm, k_fm):
            for h in range(4):
                for s2 in range(2):
                    sl = 2 * h + s2
                    cs = k.inc("cs", s.activation(out=b["sqst"][:, s2 * 512:(s2 + 1) * 512],
                                                  in_=fmbuf[:, sl * 512:(sl + 1) * 512], func=AF.Square))
                    k.wait(t, "cs", cs)
                    mmn = t.matmul(pM[0:1, 0:512], onesc[:], b["sqst"][:, s2 * 512:(s2 + 1) * 512],
                                   start=(s2 == 0), stop=(s2 == 1))
                ct = k.inc("ct", mmn)
                k.wait(v, "ct", ct)
                v.tensor_scalar(out=strow[0:1, 0:512], in0=pM[0:1, 0:512], scalar1=1e-6, scalar2=None, op0=ALU.add)
                cv = k.val("cv")
                k.wait(s, "cv", cv)
                s.activation(out=strow[0:1, 512:1024], in_=strow[0:1, 0:512], func=AF.Ln)
                cs = k.inc("cs", s.activation(out=strow[0:1, 0:512], in_=strow[0:1, 512:1024], func=AF.Exp, scale=-0.5))
                k.wait(v, "cs", cs)
                cv = k.inc("cv", v.tensor_copy(out=rstd_bf[:], in_=strow[0:1, 0:512]))
                k.wait(t, "cv", cv)
                ct = k.inc("ct", t.matmul(pC[:, 0:512], ones1[:], rstd_bf[:], start=True, stop=True))
                k.wait(v, "ct", ct)
                for s2 in range(2):
                    sl = 2 * h + s2
                    cv = k.inc("cv", v.tensor_tensor(out=fmbuf[:, sl * 512:(sl + 1) * 512],
                                                     in0=fmbuf[:, sl * 512:(sl + 1) * 512],
                                                     in1=pC[:, 0:512], op=ALU.mult))
                k.wait(t, "cv", cv)
        k.barrier()
        for (srcb, dstb, perhead) in ((k_fm, kn_tm, True), (v_fm, v_tm, False)):
            jobs = []
            for sl in range(NSLAB):
                for tl in range(NTL):
                    if perhead:
                        h = sl // 2
                        dcol = h * 1024 + tl * 256 + (sl % 2) * 128
                    else:
                        dcol = tl * 1024 + sl * 128
                    jobs.append((srcb[:, sl * 512 + tl * 128: sl * 512 + (tl + 1) * 128],
                                 dstb[:, dcol:dcol + 128]))
            tseq(k, jobs)
        for h in range(4):
            for tl in range(NTL):
                bc = btm[:, tl * 4 + h:tl * 4 + h + 1]
                v.tensor_scalar(out=u4[:, h * 1024 + tl * 256: h * 1024 + (tl + 1) * 256],
                                in0=v_tm[:, tl * 1024 + h * 256: tl * 1024 + (h + 1) * 256],
                                scalar1=bc, scalar2=None, op0=ALU.mult)
                cv = k.inc("cv", v.tensor_scalar(out=w4[:, h * 1024 + tl * 256: h * 1024 + (tl + 1) * 256],
                                                 in0=kn_tm[:, h * 1024 + tl * 256: h * 1024 + (tl + 1) * 256],
                                                 scalar1=bc, scalar2=None, op0=ALU.mult))
        k.wait(t, "cv", cv)
        k.barrier()
        k.wait(g, "ct", k.val("ct"))
        k.wait(g, "cv", k.val("cv"))
        dbd = k.inc("d_wg", g.dma_start(out=b["c_bd"][:], in_=W["c_bd"]), 16)
        k.wait(v, "d_wg", dbd)
        k.wait(t, "d_wg", dbd)
        p4cv = getattr(k, "_p4cv", {0: 0, 1: 0})
        k._p4cv = p4cv
        ui = 0
        for h in range(4):
            for gr in range(NTL):
                par = ui % 2
                ui += 1
                pap = pA if par == 0 else pC
                psq = pB if par == 0 else pD
                A4p = A4 if par == 0 else b["A4b"]
                Pcp = Pc if par == 0 else b["Pc2"]
                PTp = PTc if par == 0 else b["PTc2"]
                if p4cv[par]:
                    k.wait(t, "cv", p4cv[par])
                for s2 in range(2):
                    sl = 2 * h + s2
                    mma = t.matmul(psq[:, 0:128], k_fm[:, sl * 512 + gr * 128: sl * 512 + (gr + 1) * 128],
                                   k_fm[:, sl * 512 + gr * 128: sl * 512 + (gr + 1) * 128],
                                   start=(s2 == 0), stop=(s2 == 1))
                ct = k.inc("ct", mma)
                k.wait(v, "ct", ct)
                v.tensor_scalar(out=b["a4f"][:], in0=psq[:, 0:128],
                                scalar1=b["nbt"][:, gr * 4 + h:gr * 4 + h + 1], scalar2=None, op0=ALU.mult)
                cv = k.inc("cv", v.tensor_tensor(out=A4p[:], in0=b["a4f"][:], in1=b["c_bd"][:], op=ALU.mult))
                k.wait(t, "cv", cv)
                cv = k.inc("cv", v.tensor_copy(out=Pcp[:], in_=A4p[:]))
                k.wait(t, "cv", cv)
                ct = k.inc("ct", t.transpose(psq.bitcast(BF16)[:, 0:128] if False else k.buf["pTB" if par == 0 else "pTB2"][0:128, 0:128], Pcp[:], id_bf[:]))
                k.wait(v, "ct", ct)
                cv = k.inc("cv", v.tensor_copy(out=PTp[:], in_=k.buf["pTB" if par == 0 else "pTB2"][0:128, 0:128]))
                for lev in range(6):
                    k.wait(t, "cv", cv)
                    mmu = t.matmul(pap[:, 0:256], PTp[:], u4[:, h * 1024 + gr * 256: h * 1024 + (gr + 1) * 256],
                                   start=True, stop=True)
                    k.inc("ct", mmu)
                    mmw = t.matmul(pap[:, 256:512], PTp[:], w4[:, h * 1024 + gr * 256: h * 1024 + (gr + 1) * 256],
                                   start=True, stop=True)
                    k.inc("ct", mmw)
                    if lev < 5:
                        mmp2 = t.matmul(psq[:, 0:128], PTp[:], Pcp[:], start=True, stop=True)
                        k.inc("ct", mmp2)
                        mmpT = t.matmul(psq[:, 128:256], Pcp[:], PTp[:], start=True, stop=True)
                        ct = k.inc("ct", mmpT)
                    else:
                        ct = k.val("ct")
                    k.wait(v, "ct", ct)
                    v.tensor_tensor(out=u4[:, h * 1024 + gr * 256: h * 1024 + (gr + 1) * 256],
                                    in0=u4[:, h * 1024 + gr * 256: h * 1024 + (gr + 1) * 256],
                                    in1=pap[:, 0:256], op=ALU.add)
                    cv = k.inc("cv", v.tensor_tensor(out=w4[:, h * 1024 + gr * 256: h * 1024 + (gr + 1) * 256],
                                                     in0=w4[:, h * 1024 + gr * 256: h * 1024 + (gr + 1) * 256],
                                                     in1=pap[:, 256:512], op=ALU.add))
                    if lev < 5:
                        v.tensor_copy(out=Pcp[:], in_=psq[:, 0:128])
                        cv = k.inc("cv", v.tensor_copy(out=PTp[:], in_=psq[:, 128:256]))
                p4cv[par] = cv
        k.wait(t, "cv", k.val("cv"))
        jobs = []
        for h in range(4):
            for tl in range(NTL):
                for s2 in range(2):
                    jobs.append((w4[:, h * 1024 + tl * 256 + s2 * 128: h * 1024 + tl * 256 + (s2 + 1) * 128],
                                 w_fm2[:, h * 1024 + s2 * 512 + tl * 128: h * 1024 + s2 * 512 + (tl + 1) * 128]))
        tseq(k, jobs)
        for h in range(4):
            cv = k.inc("cv", v.tensor_copy(out=Sbf[:, h * 512:(h + 1) * 512], in_=Sf[:, h * 512:(h + 1) * 512]))
        k.wait(t, "cv", cv)
        k.barrier()
        for ci in range(NCH // 2):
            tl = ci // 2
            bo = (ci % 2) * 64
            for h in range(4):
                hu = h * 1024 + tl * 256
                for s2 in range(2):
                    mmws = t.matmul(pA[bo:bo + 64, 0:256],
                                    w_fm2[:, h * 1024 + s2 * 512 + tl * 128 + bo: h * 1024 + s2 * 512 + tl * 128 + bo + 64],
                                    Sbf[:, h * 512 + s2 * 256: h * 512 + (s2 + 1) * 256],
                                    start=(s2 == 0), stop=(s2 == 1), tile_position=(0, bo))
                ct = k.inc("ct", mmws)
                k.wait(v, "ct", ct)
                cv = k.inc("cv", v.tensor_tensor(out=u4[bo:bo + 64, hu:hu + 256],
                                                 in0=u4[bo:bo + 64, hu:hu + 256],
                                                 in1=pA[bo:bo + 64, 0:256], op=ALU.subtract))
                k.wait(t, "cv", cv)
                for s2 in range(2):
                    sl = 2 * h + s2
                    mat = t.matmul(pM[bo:bo + 64, 0:64],
                                   k_fm[:, sl * 512 + tl * 128 + bo: sl * 512 + tl * 128 + bo + 64],
                                   q_fm[:, sl * 512 + tl * 128 + bo: sl * 512 + tl * 128 + bo + 64],
                                   start=(s2 == 0), stop=(s2 == 1), tile_position=(0, bo))
                ct = k.inc("ct", mat)
                k.wait(v, "ct", ct)
                cv = k.inc("cv", v.tensor_tensor(out=atn[bo:bo + 64, 0:64], in0=pM[bo:bo + 64, 0:64],
                                                 in1=b["c_triu"][bo:bo + 64, 0:64], op=ALU.mult))
                k.wait(t, "cv", cv)
                for s2 in range(2):
                    sl = 2 * h + s2
                    t.matmul(pB[bo:bo + 64, 0:256],
                             q_fm[:, sl * 512 + tl * 128 + bo: sl * 512 + tl * 128 + bo + 64],
                             Sbf[:, h * 512 + s2 * 256: h * 512 + (s2 + 1) * 256],
                             start=(s2 == 0), stop=False, tile_position=(0, bo))
                mo = t.matmul(pB[bo:bo + 64, 0:256], atn[bo:bo + 64, 0:64], u4[bo:bo + 64, hu:hu + 256],
                              start=False, stop=True, tile_position=(bo, bo))
                ct = k.inc("ct", mo)
                k.wait(v, "ct", ct)
                cv = k.inc("cv", v.tensor_copy(out=od_tm[bo:bo + 64, tl * 1024 + h * 256: tl * 1024 + (h + 1) * 256],
                                               in_=pB[bo:bo + 64, 0:256]))
                k.wait(t, "cv", cv)
                for s2 in range(2):
                    msu = t.matmul(pC[:, s2 * 256:(s2 + 1) * 256],
                                   kn_tm[bo:bo + 64, hu + s2 * 128: hu + (s2 + 1) * 128],
                                   u4[bo:bo + 64, hu:hu + 256],
                                   start=True, stop=True, tile_position=(bo, 0))
                ct = k.inc("ct", msu)
                k.wait(v, "ct", ct)
                v.tensor_tensor(out=Sf[:, h * 512: h * 512 + 512],
                                in0=Sf[:, h * 512: h * 512 + 512],
                                in1=pC[:, 0:512], op=ALU.add)
                cv = k.inc("cv", v.tensor_copy(out=Sbf[:, h * 512:(h + 1) * 512],
                                               in_=Sf[:, h * 512:(h + 1) * 512]))
                k.wait(t, "cv", cv)
        k.barrier()
        k._fd_ct = getattr(k, "_fd_ct", [0, 0])
        vxb = b["vx"]
        for sl in range(NSLAB):
            v.tensor_copy(out=vxb[:, sl * 575: sl * 575 + 63], in_=vhist[:, sl * 63:(sl + 1) * 63])
            cv = k.inc("cv", v.tensor_copy(out=vxb[:, sl * 575 + 63: sl * 575 + 575],
                                           in_=v_fm[:, sl * 512:(sl + 1) * 512]))
        k.wait(t, "cv", cv)
        for h in range(4):
            for s2 in range(2):
                sl = 2 * h + s2
                sts = wst[:, s2 * 512: s2 * 512 + 512]
                stl = wst[:, 1024 + s2 * 512: 1024 + (s2 + 1) * 512]
                vx = b["vx"]
                dstage = b["dstage"]
                for (stg, ntap, fdname, pbank) in ((sts, 5, "c_fds", pC), (stl, 64, "c_fdl", pD)):
                    slot = 0
                    for j in range(ntap):
                        if j % 8 == 0:
                            slot = (j // 8) % 2
                            sname = f"d_fd{slot}"
                            k.wait(sp, "ct", k._fd_ct[slot])
                            ng = min(8, ntap - j)
                            k.inc(sname, sp.dma_start(
                                out=dstage[:, slot * 1024:slot * 1024 + ng * 128].rearrange(
                                    "p (f c) -> p f c", f=ng),
                                in_=W[fdname][sl, j:j + ng].rearrange("f p c -> p f c")), 16)
                            k.wait(t, sname, k.val(sname))
                        sh = ntap - 1 - j
                        mmf = t.matmul(pbank[:, 0:512],
                                       dstage[:, slot * 1024 + (j % 8) * 128: slot * 1024 + (j % 8 + 1) * 128],
                                       vx[:, sl * 575 + 63 - sh: sl * 575 + 63 - sh + 512],
                                       start=(j == 0), stop=(j == ntap - 1))
                        if j % 8 == 7 or j == ntap - 1:
                            ct = k.inc("ct", mmf)
                            k._fd_ct[slot] = ct
                    k.wait(v, "ct", ct)
                    cv = k.inc("cv", v.tensor_copy(out=stg[:], in_=pbank[:, 0:512]))
                    k.wait(t, "cv", cv)
            k.wait(t, "cv", k.val("cv"))
            k.wait(s, "cv", k.val("cv"))
            jobs = []
            for (stgbase, dstb2) in ((0, fs_tm), (1024, fl_tm)):
                for s2 in range(2):
                    for tl in range(NTL):
                        jobs.append((wst[:, stgbase + s2 * 512 + tl * 128: stgbase + s2 * 512 + (tl + 1) * 128],
                                     dstb2[:, tl * 1024 + (2 * h + s2) * 128: tl * 1024 + (2 * h + s2 + 1) * 128]))
            tseq(k, jobs)
            k.wait(t, "cv", cv)
            jobs = []
            for s2 in range(2):
                for tl in range(NTL):
                    jobs.append((od_tm[:, tl * 1024 + h * 256 + s2 * 128: tl * 1024 + h * 256 + (s2 + 1) * 128],
                                 b["odst"][:, s2 * 512 + tl * 128: s2 * 512 + (tl + 1) * 128]))
            tseq(k, jobs)
            srcs = [wst[:, 0:1024], wst[:, 1024:2048], b["odst"][:, 0:1024], None]
            for ri, sb_ in enumerate(srcs):
                def chunkap(s2):
                    if sb_ is None:
                        return v_fm[:, (2 * h + s2) * 512:(2 * h + s2 + 1) * 512]
                    return sb_[:, s2 * 512:(s2 + 1) * 512]
                for kind in range(3):
                    for s2 in range(2):
                        xap = chunkap(s2)
                        if kind == 0:
                            rh = xap
                        elif kind == 1:
                            k.wait(s, "cv", k.val("cv"))
                            cs = k.inc("cs", s.activation(out=b["kst"][:, s2 * 512:(s2 + 1) * 512], in_=xap, func=AF.Square))
                            k.wait(t, "cs", cs)
                            rh = b["kst"][:, s2 * 512:(s2 + 1) * 512]
                        else:
                            cv = k.inc("cv", v.scalar_tensor_tensor(out=b["kst"][:, s2 * 512:(s2 + 1) * 512],
                                                                    in0=xap, scalar=-1.0, in1=xap,
                                                                    op0=ALU.mult, op1=ALU.max))
                            k.wait(t, "cv", cv)
                            rh = b["kst"][:, s2 * 512:(s2 + 1) * 512]
                        mst = t.matmul(pM[0:1, 0:512], onesc[:], rh, start=(s2 == 0), stop=(s2 == 1))
                    ct = k.inc("ct", mst)
                    k.wait(v, "ct", ct)
                    cv = k.inc("cv", v.tensor_copy(out=strow[0:1, kind * 512:(kind + 1) * 512], in_=pM[0:1, 0:512]))
                    k.wait(t, "cv", cv)
                k.wait(v, "d_st2", k.val("d_st2"))
                k.wait(v, "d_st", k.val("d_st"))
                v.tensor_scalar(out=b["fin"][0:1, 0:512], in0=strow[0:1, 0:512], scalar1=1.0 / 256, scalar2=None, op0=ALU.mult)
                v.tensor_tensor(out=b["fin"][0:1, 512:1024], in0=b["fin"][0:1, 0:512], in1=b["fin"][0:1, 0:512], op=ALU.mult)
                v.tensor_scalar(out=b["fin"][0:1, 1024:1536], in0=strow[0:1, 512:1024], scalar1=1.0 / 256, scalar2=None, op0=ALU.mult)
                v.tensor_tensor(out=b["fin"][0:1, 512:1024], in0=b["fin"][0:1, 1024:1536], in1=b["fin"][0:1, 512:1024], op=ALU.subtract)
                cv = k.inc("cv", v.tensor_scalar(out=b["fin"][0:1, 1024:1536], in0=strow[0:1, 1024:1536], scalar1=1.0 / 256, scalar2=None, op0=ALU.mult))
                k.wait(s, "cv", cv)
                s.activation(out=strow[0:1, 1536:2048], in_=strow[0:1, 512:1024], func=AF.Ln)
                cs = k.inc("cs", s.activation(out=b["fin"][0:1, 1536:2048], in_=strow[0:1, 1536:2048], func=AF.Exp, scale=0.5))
                k.wait(g, "cs", cs)
                k.wait(g, "cv", cv)
                k.inc("d_st2", g.dma_start(out=statsb[h, ri * 4:(ri + 1) * 4, :], in_=b["fin"][0:1, 0:2048]), 16)
        for sl in range(NSLAB):
            cv = k.inc("cv", v.tensor_copy(out=vhist[:, sl * 63:(sl + 1) * 63],
                                           in_=v_fm[:, sl * 512 + 449: sl * 512 + 512]))
        k.wait(t, "cv", cv)
        k.barrier()
        for c in range(NSLAB):
            slot = stream_w("w1at", c)
            pp = pA if c % 2 == 0 else pB
            for sl in range(NSLAB):
                mmx = t.matmul(pp[:, 0:512], wst[:, slot * 1024 + sl * 128: slot * 1024 + (sl + 1) * 128],
                               XT[:, sl * T + tok0: sl * T + tok0 + WIN],
                               start=(sl == 0), stop=(sl == NSLAB - 1))
            ct = k.inc("ct", mmx)
            wslot_ct[c % 2] = ct
            k.wait(v, "ct", ct)
            cv = k.inc("cv", v.tensor_scalar(out=xw_bf[:, c * 512:(c + 1) * 512], in0=pp[:, 0:512],
                                             scalar1=b["gb1c"][:, c:c + 1], scalar2=None, op0=ALU.add))
            k.wait(t, "cv", cv)
        k.wait(g, "ct", k.val("ct"))
        dw1b = k.inc("d_w1b", g.dma_start(out=w1bT[:], in_=W["w1bT"]), 16)
        k.wait(t, "d_w1b", dw1b)
        for h in range(4):
            k.wait(g, "d_st2", k.val("d_st2"))
            k.wait(g, "ct", k.val("ct"))
            dst2 = k.inc("d_sb", g.dma_start(out=stats_bf[:], in_=statsb[h]), 16)
            k.wait(t, "d_sb", dst2)
            tgb = b["vx"][:, 0:4096]
            t2b = b["w_fm2"][:, 0:4096]
            for c in range(NSLAB):
                ct = k.inc("ct", t.matmul(pA[:, 0:512] if c % 2 == 0 else pB[:, 0:512],
                                          w1bT[:, c * 128:(c + 1) * 128], stats_bf[:],
                                          start=True, stop=True))
                pp = pA if c % 2 == 0 else pB
                k.wait(v, "ct", ct)
                cv = k.inc("cv", v.tensor_tensor(out=tgb[:, c * 512:(c + 1) * 512], in0=pp[:, 0:512],
                                                 in1=xw_bf[:, c * 512:(c + 1) * 512], op=ALU.add))
                k.wait(t, "cv", cv)
            k.wait(s, "cv", cv)
            cs = k.inc("cs", s.activation(out=t2b, in_=tgb, func=AF.Square))
            k.wait(v, "cs", cs)
            v.tensor_tensor(out=t2b, in0=t2b, in1=tgb, op=ALU.mult)
            cv = k.inc("cv", v.scalar_tensor_tensor(out=t2b, in0=t2b, scalar=0.044715, in1=tgb, op0=ALU.mult, op1=ALU.add))
            k.wait(s, "cv", cv)
            cs = k.inc("cs", s.activation(out=t2b, in_=t2b, func=AF.Tanh, scale=float(math.sqrt(2.0 / math.pi))))
            k.wait(v, "cs", cs)
            v.tensor_scalar(out=t2b, in0=t2b, scalar1=0.5, scalar2=0.5, op0=ALU.mult, op1=ALU.add)
            cv = k.inc("cv", v.tensor_tensor(out=h1b[:, 0:4096], in0=t2b, in1=tgb, op=ALU.mult))
            k.wait(t, "cv", cv)
            for sl in range(NSLAB):
                mlg = t.matmul(pM[0:4, 0:512], w2gT[:, sl * 4:(sl + 1) * 4], h1b[:, sl * 512:(sl + 1) * 512],
                               start=(sl == 0), stop=(sl == NSLAB - 1))
            ct = k.inc("ct", mlg)
            k.wait(v, "ct", ct)
            cv = k.inc("cv", v.tensor_scalar(out=b["lgbf"][0:4, 0:512], in0=pM[0:4, 0:512],
                                             scalar1=gsm[0:4, 6:7], scalar2=None, op0=ALU.add))
            k.wait(t, "cv", cv)
            for tl in range(NTL):
                ct = k.inc("ct", t.transpose(pT[0:128, 0:4], b["lgbf"][0:4, tl * 128:(tl + 1) * 128], id_bf[0:4, 0:4]))
                k.wait(v, "ct", ct)
                lt = b["lgtm"]
                cv = k.inc("cv", v.tensor_copy(out=lt[:, 0:4], in_=pT[0:128, 0:4]))
                k.wait(t, "cv", cv)
                v.tensor_scalar(out=lt[:, 3:4], in0=lt[:, 3:4], scalar1=gsm[:, cbcol[h]:cbcol[h] + 1],
                                scalar2=None, op0=ALU.add)
                v.tensor_scalar(out=lt[:, 4:8], in0=lt[:, 0:4], scalar1=gsm[:, itcol[h]:itcol[h] + 1],
                                scalar2=None, op0=ALU.mult)
                v.reduce_max(out=sm2[:, 0:1], in_=lt[:, 4:8], axis=AX.X)
                cv = k.inc("cv", v.tensor_scalar(out=sm2[:, 1:2], in0=sm2[:, 0:1], scalar1=-1.0, scalar2=None, op0=ALU.mult))
                k.wait(s, "cv", cv)
                cs = k.inc("cs", s.activation(out=lt[:, 8:12], in_=lt[:, 4:8], func=AF.Exp, bias=sm2[:, 1:2],
                                              accum_out=sm2[:, 2:3]))
                k.wait(v, "cs", cs)
                v.reciprocal(out=sm2[:, 3:4], in_=sm2[:, 2:3])
                v.tensor_scalar(out=lt[:, 4:8], in0=lt[:, 8:12], scalar1=sm2[:, 3:4], scalar2=None, op0=ALU.mult)
                cv = k.inc("cv", v.tensor_scalar(out=lt[:, 8:12], in0=lt[:, 4:8], scalar1=1.0 - 4.0 * EPSF,
                                                 scalar2=EPSF, op0=ALU.mult, op1=ALU.add))
                tc0 = tl * 1024 + h * 256
                v.tensor_scalar(out=mixs[:], in0=fs_tm[:, tc0:tc0 + 256], scalar1=lt[:, 8:9], scalar2=None, op0=ALU.mult)
                v.scalar_tensor_tensor(out=mixs[:], in0=fl_tm[:, tc0:tc0 + 256], scalar=lt[:, 9:10], in1=mixs[:],
                                       op0=ALU.mult, op1=ALU.add)
                v.scalar_tensor_tensor(out=mixs[:], in0=od_tm[:, tc0:tc0 + 256], scalar=lt[:, 10:11], in1=mixs[:],
                                       op0=ALU.mult, op1=ALU.add)
                cv = k.inc("cv", v.scalar_tensor_tensor(out=mixs[:], in0=v_tm[:, tc0:tc0 + 256], scalar=lt[:, 11:12],
                                                        in1=mixs[:], op0=ALU.mult, op1=ALU.add))
                k.wait(s, "cv", cv)
                cs = k.inc("cs", s.activation(out=b["kst"][:, 0:256], in_=mixs[:], func=AF.Square, accum_out=sm2[:, 4:5]))
                k.wait(v, "cs", cs)
                cv = k.inc("cv", v.tensor_scalar(out=sm2[:, 5:6], in0=sm2[:, 4:5], scalar1=1.0 / 256, scalar2=1e-5,
                                                 op0=ALU.mult, op1=ALU.add))
                k.wait(s, "cv", cv)
                s.activation(out=sm2[:, 6:7], in_=sm2[:, 5:6], func=AF.Ln)
                cs = k.inc("cs", s.activation(out=sm2[:, 7:8], in_=sm2[:, 6:7], func=AF.Exp, scale=-0.5))
                k.wait(v, "cs", cs)
                v.tensor_scalar(out=mixs[:], in0=mixs[:], scalar1=sm2[:, 7:8], scalar2=None, op0=ALU.mult)
                cv = k.inc("cv", v.tensor_tensor(out=fs_tm[:, tc0:tc0 + 256], in0=mixs[:], in1=onwb[:], op=ALU.mult))
                k.wait(t, "cv", cv)
        k.barrier()
        jobs = []
        for sl in range(NSLAB):
            for tl in range(NTL):
                jobs.append((fs_tm[:, tl * 1024 + sl * 128: tl * 1024 + (sl + 1) * 128],
                             b["o_fm"][:, sl * 512 + tl * 128: sl * 512 + (tl + 1) * 128]))
        tseq(k, jobs)
        for c in range(NSLAB):
            slot = stream_w("wot", c)
            for tt in range(NTL):
                pp = pA if tt % 2 == 0 else pB
                for sl in range(NSLAB):
                    mmy = t.matmul(pp[:, 0:128], b["o_fm"][:, sl * 512 + tt * 128: sl * 512 + (tt + 1) * 128],
                                   wst[:, slot * 1024 + sl * 128: slot * 1024 + (sl + 1) * 128],
                                   start=(sl == 0), stop=(sl == NSLAB - 1))
                ct = k.inc("ct", mmy)
                k.wait(v, "ct", ct)
                cv = k.inc("cv", v.tensor_copy(out=b["ybuf"][:, tt * 1024 + c * 128: tt * 1024 + (c + 1) * 128],
                                               in_=pp[:, 0:128]))
                k.wait(t, "cv", cv)
            wslot_ct[slot] = k.val("ct")
        for tt in range(NTL):
            gt_ = (tok0 // 128) + tt
            ldr = k.inc("d_x", sp.dma_start(out=lnin[:], in_=src_ap[gt_ * 128:(gt_ + 1) * 128, :]), 16)
            k.wait(v, "d_x", ldr)
            cv = k.inc("cv", v.tensor_tensor(out=lnin[:], in0=lnin[:], in1=b["ybuf"][:, tt * 1024:(tt + 1) * 1024], op=ALU.add))
            cv = emit_ln_tile(k, lnin[:], lnin[:])
            k.wait(g, "cv", cv)
            k.inc("d_st", g.dma_start(out=dst_ap[gt_ * 128:(gt_ + 1) * 128, :], in_=lnin[:]), 16)
            k.wait(s, "cv", cv)
            cs = k.inc("cs", k.S.copy(out=b["xbf3"][:], in_=lnin[:]))
            k.wait(t, "cs", cs)
            tseq(k, [(b["xbf3"][:, sl * 128:(sl + 1) * 128], XT[:, sl * T + gt_ * 128:sl * T + (gt_ + 1) * 128])
                     for sl in range(NSLAB)])
            k.wait(sp, "cv", k.val("cv"))
            k.wait(sp, "cs", k.val("cs"))
            k.wait(sp, "d_st", k.val("d_st"))



def emit_l1_ffn(k, src_ap, dst_ap, W):
    t, v, s, g, sp = k.t, k.V, k.S, k.g, k.sp
    b = k.buf
    XT = b["XT"]
    id_bf = b["id_bf"]; idf = b["id_f32"]
    pA, pB, pC, pD = b["pA"], b["pB"], b["pC"], b["pD"]
    pM = b["pMISC"]; pT32 = b["pT32"]; pT, pT2 = b["pT"], b["pT2"]
    he = b["he"]; s2cW = b["s2cW"]; c2sW = b["c2sW"]; we1 = b["we1"]; smo = b["smo"]
    contb = b["contb"]; outfm = b["outfm"]; outst = b["outst"]; we2 = b["we2"]
    spk = b["spk"]; decW = b["decW"]; wst2 = b["wst2"]; sqb = b["sqb"]
    contf = b["contf"]; r1f = b["r1f"]; rlogf = b["rlogf"]; acc2 = b["acc2"]
    w_tm2 = b["w_tm2"]; dect = b["dect"]; lnin2 = b["lnin2"]
    sm = b["sm"]; sm2 = b["sm5"]
    b1e = b["b1e"]; be2b = b["be2b"]; encb = b["encb"]; s2cb = b["s2cb"]
    rb1c = b["rb1c"]; rb2c = b["rb2c"]; c2sb = b["c2sb"]; decbb = b["decbb"]
    lng = b["lng"]; lnb = b["lnb"]
    rw1f = b["rw1f"]; rw2f = b["rw2f"]

    if getattr(k, "_wconv_done", None):
        for eng in (g, sp, v, t, s):
            k.wait(eng, "d_cvo0", k._wconv_done[0])
            k.wait(eng, "d_cvo1", k._wconv_done[1])
        k._wconv_done = None
    k.inc("d_cst", sp.dma_start(out=rw1f[0:64, :], in_=W["rW1"]), 16)
    k.inc("d_cst", sp.dma_start(out=rw2f[0:64, :], in_=W["rW2"]), 16)
    k.inc("d_cst", sp.dma_start(out=b1e[:], in_=W["b1e"]), 16)
    k.inc("d_cst", sp.dma_start(out=be2b[:], in_=W["be2b"]), 16)
    k.inc("d_cst", sp.dma_start(out=encb[:], in_=W["encbc"]), 16)
    k.inc("d_cst", sp.dma_start(out=s2cb[:], in_=W["s2cbc"]), 16)
    k.inc("d_cst", sp.dma_start(out=rb1c[:], in_=W["rb1c"]), 16)
    k.inc("d_cst", sp.dma_start(out=rb2c[:], in_=W["rb2c"]), 16)
    k.inc("d_cst", sp.dma_start(out=c2sb[:], in_=W["c2sbc"]), 16)
    k.inc("d_cst", sp.dma_start(out=decbb[:], in_=W["r_decb"]), 16)
    k.inc("d_cst", sp.dma_start(out=lng[:], in_=W["r_lng"]), 16)
    k.inc("d_cst", sp.dma_start(out=lnb[:], in_=W["r_lnb"]), 16)
    dcst2 = k.val("d_cst")
    k.inc("d_wg", g.dma_start(out=s2cW[:], in_=W["s2ct"]), 16)
    k.inc("d_wg", g.dma_start(out=c2sW[0:64, :], in_=W["c2s_W"]), 16)
    k.inc("d_wg", g.dma_start(out=we2[:], in_=W["we2t"]), 16)
    dwg2 = k.val("d_wg")
    k.wait(t, "d_cst", dcst2)
    k.wait(t, "d_wg", dwg2)

    wslot = [0, 0]
    enc_cs = {0: 0, 1: 0}
    for w in range(NWIN):
        tok0 = w * WIN
        k.barrier()
        k.wait(g, "ct", k.val("ct"))
        dwe1 = k.inc("d_we1", g.dma_start(out=we1[0:64, :], in_=W["we1r"]), 16)
        for c in range(16):
            slot = c % 2
            k.wait(sp, "ct", wslot[slot])
            sem = f"d_ws{slot}"
            k.inc(sem, sp.dma_start(out=wst2[:, slot * 1024:(slot + 1) * 1024],
                                    in_=W["encbf"][c]), 16)
            k.wait(t, sem, k.val(sem))
            pp = (pA, pB, pC, pD)[c % 4]
            if enc_cs.get(c % 4):
                k.wait(t, "cs", enc_cs[c % 4])
            for sl in range(NSLAB):
                mme = t.matmul(pp[:, 0:512], wst2[:, slot * 1024 + sl * 128: slot * 1024 + (sl + 1) * 128],
                               XT[:, sl * T + tok0: sl * T + tok0 + WIN],
                               start=(sl == 0), stop=(sl == NSLAB - 1))
            ct = k.inc("ct", mme)
            wslot[slot] = ct
            k.wait(s, "ct", ct)
            cs = k.inc("cs", s.activation(out=spk[:, c * 512:(c + 1) * 512], in_=pp[:, 0:512],
                                          func=AF.Sigmoid, bias=encb[:, c:c + 1]))
            enc_cs[c % 4] = cs
        k.wait(t, "cs", k.val("cs"))
        for sl16 in range(16):
            mms = t.matmul(pM[0:64, 0:512], s2cW[:, sl16 * 64:(sl16 + 1) * 64],
                           spk[:, sl16 * 512:(sl16 + 1) * 512],
                           start=(sl16 == 0), stop=(sl16 == 15))
        ct = k.inc("ct", mms)
        k.wait(v, "ct", ct)
        v.tensor_scalar(out=contf[0:64, :], in0=pM[0:64, 0:512], scalar1=s2cb[0:64, 0:1],
                        scalar2=None, op0=ALU.add)
        cv = k.inc("cv", v.tensor_copy(out=contb[0:64, :], in_=contf[0:64, :]))
        k.wait(t, "cv", cv)
        ct = k.inc("ct", t.matmul(pM[0:64, 0:512], rw1f[0:64, :], contf[0:64, :], start=True, stop=True))
        k.wait(s, "ct", ct)
        cs = k.inc("cs", s.activation(out=r1f[0:64, :], in_=pM[0:64, 0:512], func=AF.Tanh, bias=rb1c[0:64, 0:1]))
        k.wait(t, "cs", cs)
        ct = k.inc("ct", t.matmul(pM[0:8, 0:512], rw2f[0:64, :], r1f[0:64, :], start=True, stop=True))
        k.wait(v, "ct", ct)
        cv = k.inc("cv", v.tensor_scalar(out=rlogf[0:8, :], in0=pM[0:8, 0:512], scalar1=rb2c[0:8, 0:1],
                                         scalar2=None, op0=ALU.add))
        k.wait(t, "cv", cv)
        for tl in range(NTL):
            ct = k.inc("ct", t.transpose(pT32[0:128, 0:8], rlogf[0:8, tl * 128:(tl + 1) * 128], idf[0:8, 0:8]))
            k.wait(v, "ct", ct)
            rl8 = sm2[:, 16:24]
            v.tensor_copy(out=rl8, in_=pT32[0:128, 0:8])
            v.reduce_max(out=sm2[:, 0:1], in_=rl8, axis=AX.X)
            v.tensor_scalar(out=sm2[:, 24:32], in0=rl8, scalar1=sm2[:, 0:1], scalar2=None, op0=ALU.is_ge)
            v.scalar_tensor_tensor(out=sm2[:, 32:40], in0=sm2[:, 24:32], scalar=-1e30, in1=rl8,
                                   op0=ALU.mult, op1=ALU.add)
            v.reduce_max(out=sm2[:, 1:2], in_=sm2[:, 32:40], axis=AX.X)
            v.tensor_scalar(out=sm2[:, 24:32], in0=rl8, scalar1=sm2[:, 1:2], scalar2=None, op0=ALU.is_ge)
            cv = k.inc("cv", v.tensor_scalar(out=sm2[:, 2:3], in0=sm2[:, 0:1], scalar1=-1.0, scalar2=None, op0=ALU.mult))
            k.wait(s, "cv", cv)
            cs = k.inc("cs", s.activation(out=sm2[:, 32:40], in_=rl8, func=AF.Exp, bias=sm2[:, 2:3]))
            k.wait(v, "cs", cs)
            v.tensor_tensor(out=sm2[:, 32:40], in0=sm2[:, 32:40], in1=sm2[:, 24:32], op=ALU.mult)
            v.reduce_sum(out=sm2[:, 3:4], in_=sm2[:, 32:40], axis=AX.X)
            v.reciprocal(out=sm2[:, 4:5], in_=sm2[:, 3:4])
            cv = k.inc("cv", v.tensor_scalar(out=w_tm2[:, tl * 8:(tl + 1) * 8], in0=sm2[:, 32:40],
                                             scalar1=sm2[:, 4:5], scalar2=None, op0=ALU.mult))
            k.wait(t, "cv", cv)
        cv = k.inc("cv", v.memset(acc2[:], 0.0))
        k.wait(t, "d_we1", dwe1)
        k.wait(v, "cv", cv)
        oe_cv2 = {0: 0, 1: 0}
        for e in range(8):
            for c in range(NSLAB):
                pp = (pA, pB, pC, pD)[c % 4]
                if enc_cs.get(c % 4):
                    k.wait(t, "cs", enc_cs[c % 4])
                if c % 4 >= 2 and oe_cv2.get(c % 4 - 2):
                    k.wait(t, "cv", oe_cv2[c % 4 - 2])
                mmh = t.matmul(pp[:, 0:512], we1[0:64, e * 1024 + c * 128: e * 1024 + (c + 1) * 128],
                               contb[0:64, :], start=True, stop=True)
                ct = k.inc("ct", mmh)
                k.wait(s, "ct", ct)
                cs = k.inc("cs", s.activation(out=he[:, c * 512:(c + 1) * 512], in_=pp[:, 0:512],
                                              func=AF.Relu, bias=b1e[:, e * 8 + c:e * 8 + c + 1]))
                enc_cs[c % 4] = cs
            k.wait(t, "cs", k.val("cs"))
            for tt in range(NTL):
                u = tt % 2
                pp = pC if u == 0 else pD
                if oe_cv2[u]:
                    k.wait(t, "cv", oe_cv2[u])
                for c in range(NSLAB):
                    mmo = t.matmul(pp[:, 0:64], he[:, c * 512 + tt * 128: c * 512 + (tt + 1) * 128],
                                   we2[:, (e * 8 + c) * 64:(e * 8 + c + 1) * 64],
                                   start=(c == 0), stop=(c == NSLAB - 1))
                ct = k.inc("ct", mmo)
                k.wait(v, "ct", ct)
                wcol = w_tm2[:, tt * 8 + e: tt * 8 + e + 1]
                accap = acc2[:, tt * 64:(tt + 1) * 64]
                v.scalar_tensor_tensor(out=accap, in0=pp[:, 0:64], scalar=wcol, in1=accap,
                                       op0=ALU.mult, op1=ALU.add)
                cv = k.inc("cv", v.scalar_tensor_tensor(out=accap, in0=be2b[:, e * 64:(e + 1) * 64],
                                                        scalar=wcol, in1=accap, op0=ALU.mult, op1=ALU.add))
                oe_cv2[u] = cv
        for tt in range(NTL):
            cv = k.inc("cv", v.tensor_copy(out=outst[:], in_=acc2[:, tt * 64:(tt + 1) * 64]))
            k.wait(t, "cv", cv)
            ct = k.inc("ct", t.transpose(pT[0:64, 0:128], outst[:], id_bf[:]))
            k.wait(v, "ct", ct)
            cv = k.inc("cv", v.tensor_copy(out=outfm[0:64, tt * 128:(tt + 1) * 128], in_=pT[0:64, 0:128]))
            k.wait(t, "cv", cv)
        for c in range(16):
            pp = (pA, pB, pC, pD)[c % 4]
            if enc_cs.get(c % 4):
                k.wait(t, "cs", enc_cs[c % 4])
            mmc = t.matmul(pp[:, 0:512], c2sW[0:64, c * 128:(c + 1) * 128], outfm[0:64, :],
                           start=True, stop=True)
            ct = k.inc("ct", mmc)
            k.wait(s, "ct", ct)
            cs = k.inc("cs", s.activation(out=smo[:, c * 512:(c + 1) * 512], in_=pp[:, 0:512],
                                          func=AF.Sigmoid, bias=c2sb[:, c:c + 1]))
            enc_cs[c % 4] = cs
        k.wait(t, "cs", k.val("cs"))
        k.wait(sp, "ct", k.val("ct"))
        for grp in range(2):
            k.inc("d_dec", sp.dma_start(
                out=decW[:, grp * 8192:(grp + 1) * 8192].rearrange("p (f c) -> p f c", f=8),
                in_=W["decbf"][grp * 8:(grp + 1) * 8].rearrange("f p c -> p f c")), 16)
        ddec = k.val("d_dec")
        k.wait(t, "d_dec", ddec)
        for tt in range(NTL):
            gt_ = (tok0 // 128) + tt
            for half in range(2):
                pp = pC if half == 0 else pD
                for sl16 in range(16):
                    mmd = t.matmul(pp[:, 0:512], smo[:, sl16 * 512 + tt * 128: sl16 * 512 + (tt + 1) * 128],
                                   decW[:, sl16 * 1024 + half * 512: sl16 * 1024 + half * 512 + 512],
                                   start=(sl16 == 0), stop=(sl16 == 15))
                ct = k.inc("ct", mmd)
                k.wait(v, "ct", ct)
                cv = k.inc("cv", v.tensor_tensor(out=dect[:, half * 512:(half + 1) * 512], in0=pp[:, 0:512],
                                                 in1=decbb[:, half * 512:(half + 1) * 512], op=ALU.add))
                k.wait(t, "cv", cv)
                k.wait(s, "cv", cv)
                cs = k.inc("cs", s.activation(out=dect[:, half * 512:(half + 1) * 512],
                                              in_=dect[:, half * 512:(half + 1) * 512], func=AF.Sigmoid))
            k.wait(v, "cs", k.val("cs"))
            cv = emit_ln_tile(k, dect[:], dect[:])
            v.tensor_tensor(out=dect[:], in0=dect[:], in1=lng[:], op=ALU.mult)
            cv = k.inc("cv", v.tensor_tensor(out=dect[:], in0=dect[:], in1=lnb[:], op=ALU.add))
            ldr = k.inc("d_x", sp.dma_start(out=lnin2[:], in_=src_ap[gt_ * 128:(gt_ + 1) * 128, :]), 16)
            k.wait(v, "d_x", ldr)
            cv = k.inc("cv", v.tensor_tensor(out=lnin2[:], in0=lnin2[:], in1=dect[:], op=ALU.add))
            cv = emit_ln_tile(k, lnin2[:], lnin2[:])
            k.wait(g, "cv", cv)
            k.inc("d_st", g.dma_start(out=dst_ap[gt_ * 128:(gt_ + 1) * 128, :], in_=lnin2[:]), 16)
            k.wait(sp, "cv", k.val("cv"))
            k.wait(sp, "d_st", k.val("d_st"))
            k.wait(v, "d_st", k.val("d_st"))



def build_program(blocks=(0,), final_out=True):
    nc = bass.Bass("TRN2", target_bir_lowering=False)
    x_in = nc.dram_tensor("x", [T, D], F32, kind="ExternalInput").ap()
    out = nc.dram_tensor("out", [T, D], F32, kind="ExternalOutput").ap()
    xres = nc.dram_tensor("xres", [T, D], F32, kind="Internal").ap()

    W = {}

    def inp(name, shape, dtype=F32):
        W[name] = nc.dram_tensor(name, list(shape), dtype, kind="ExternalInput").ap()
        return W[name]

    inp("wvT", (D, D)); inp("woT", (D, D)); inp("wshT", (D, 64)); inp("wchT", (D, 1))
    inp("omgT", (64, 32)); inp("wcmT", (320, 64))
    inp("r_bv", (128, D)); inp("r_bout", (128, D)); inp("r_bsh", (128, 64))
    inp("r_rffb", (128, 32)); inp("r_bch", (128, 1)); inp("r_lt", (128, 1))
    inp("c_brow", (128, 512)); inp("c_colb", (128, 1)); inp("c_mags", (128, 3))
    inp("c_B4", (128, 2048), BF16)
    inp("c_id", (128, 128), BF16)
    inp("c_idf", (128, 128), F32)
    inp("rwT32", (D, 8)); inp("w1t", (8, 32, 128, 8, 128)); inp("w2T", (8, T, D))
    inp("b1c", (128, 256)); inp("rb2", (8, 128, D)); inp("b2r", (8, D))
    for nm in ("wqt", "wkt", "wvt", "w1at", "wot"):
        inp(nm, (8, 128, 8, 128))
    inp("wbTs", (128, 32)); inp("w2gT", (128, 32)); inp("w1bT", (16, 1024))
    inp("c_conv", (128, 96)); inp("c_firs", (128, 40)); inp("c_firl", (128, 512))
    inp("r_onw", (128, 256)); inp("r_glt", (128, 4)); inp("r_gcb", (128, 4))
    inp("gb2col", (128, 1)); inp("gb1c", (128, 8))
    inp("c_bd", (128, 128), BF16); inp("c_triu", (128, 64), BF16)
    inp("c_fds", (8, 5, 128, 128), BF16); inp("c_fdl", (8, 64, 128, 128), BF16)
    W["statsb"] = nc.dram_tensor("statsb", [4, 16, 512], F32, kind="Internal").ap()
    W["w1bf"] = nc.dram_tensor("w1bf", [8, 32, 128, 1024], BF16, kind="Internal").ap()
    W["w2bf"] = nc.dram_tensor("w2bf", [8, 32, 128, 1024], BF16, kind="Internal").ap()
    for nm in ("wqt", "wkt", "wvt", "w1at", "wot"):
        W[nm + "bf"] = nc.dram_tensor(nm + "bf", [8, 128, 1024], BF16, kind="Internal").ap()
    W["encbf"] = nc.dram_tensor("encbf", [16, 128, 1024], BF16, kind="Internal").ap()
    W["decbf"] = nc.dram_tensor("decbf", [16, 128, 1024], BF16, kind="Internal").ap()
    inp("enct", (16, 128, 8, 128)); inp("s2ct", (128, 1024)); inp("c2s_W", (64, 2048))
    inp("we1r", (64, 8192)); inp("we2t", (128, 4096)); inp("dec_W", (2048, 1024))
    inp("rW1", (64, 64)); inp("rW2", (64, 8))
    inp("b1e", (128, 64)); inp("be2b", (128, 512)); inp("encbc", (128, 16))
    inp("s2cbc", (128, 1)); inp("rb1c", (128, 1)); inp("rb2c", (128, 1))
    inp("c2sbc", (128, 16)); inp("r_decb", (128, D)); inp("r_lng", (128, D)); inp("r_lnb", (128, D))

    with ExitStack() as ctx:
        k = KB(nc, ctx)
        k.sbuf("XT", [128, NSLAB * T], BF16)
        AA = k.sbuf("AA", [128, 32768], BF16)
        AB = k.sbuf("AB", [128, 22528], BF16)
        AF = k.sbuf("AF", [128, 8520], F32)
        k.sbuf("id_bf", [128, 128], BF16)
        k.sbuf("id_f32", [128, 128], F32)
        k.psum("pA", [128, 512]); k.psum("pB", [128, 512])
        k.psum("pC", [128, 512]); k.psum("pD", [128, 512])
        pM = k.psum("pMISC", [128, 512])
        pTB = k.psum("pTB", [128, 1024], BF16)
        k.psum("pTB2", [128, 1024], BF16)
        k.buf["pS"] = pM[:, 0:256]
        k.buf["pPhi"] = pM[:, 256:384]
        k.buf["pT32"] = pM[:, 384:512]
        k.buf["pT"] = pTB[:, 0:128]
        k.buf["pT2"] = pTB[:, 128:256]

        def seg(arena, start, n):
            return arena[:, start:start + n]

        b = k.buf
        b["BIG"] = seg(AA, 0, 32768)
        b["rowbuf"] = seg(AB, 0, 4096)
        b["aofm"] = seg(AB, 0, 1024)
        b["xbf"] = seg(AB, 1024, 1024)
        b["wsmall"] = seg(AB, 2048, 520)
        b["wcm_bf"] = seg(AB, 2568, 192)
        b["omg_bf"] = seg(AB, 2760, 32)
        b["shn_bf"] = seg(AB, 2792, 64)
        b["shnT_bf"] = seg(AB, 2856, 128)
        b["phis_bf"] = seg(AB, 2984, 64)
        b["tp_bf"] = seg(AB, 3048, 384)
        b["tpT_bf"] = seg(AB, 3432, 384)
        b["Pbf"] = seg(AB, 4096, 4096)
        b["wbig"] = seg(AB, 8192, 8192)
        b["phi"] = seg(AB, 16384, 4096)
        b["B4"] = seg(AB, 20480, 2048)
        b["biasb"] = seg(AF, 0, 2304)
        b["cst"] = seg(AF, 2304, 520)
        b["wsmall_f32"] = seg(AF, 2824, 192)
        b["xld"] = seg(AF, 3016, 1024)
        b["tfull"] = seg(AF, 4040, 1024)
        b["sq"] = seg(AF, 5064, 1024)
        b["sh64"] = seg(AF, 5064, 192)
        b["pj"] = seg(AF, 5256, 192)
        b["sm"] = seg(AF, 6088, 8)
        b["sm2"] = seg(AF, 6096, 64)

        k.inc("d_cst", k.sp.dma_start(out=k.buf["id_bf"][:], in_=W["c_id"]), 16)
        k.inc("d_cst", k.sp.dma_start(out=k.buf["id_f32"][:], in_=W["c_idf"]), 16)

        if 1 in blocks or 2 in blocks or 3 in blocks:
            emit_w_bf16_stage(k, W, blocks)
        emit_load_xt(k, x_in)

        last = blocks[-1]
        prev = x_in
        for bi in blocks:
            src_b = prev
            dst_b = out if bi == last else xres
            prev = dst_b
            if bi == 0:
                emit_l0_attn(k, src_b, dst_b, W)
            elif bi == 1:
                b["w2buf"] = seg(AA, 0, 32768)
                b["h1buf"] = seg(AB, 0, 16384)
                b["w1buf"] = seg(AB, 16384, 4096)
                b["b2buf"] = seg(AB, 20480, 1024)
                b["b2all"] = seg(AB, 20480, 1024)
                b["w8bf"] = seg(AB, 21760, 136)
                b["rw_bf"] = seg(AB, 21896, 64)
                b["b1c"] = seg(AB, 21504, 256)
                b["xbf2"] = seg(AB, 0, 1024)
                b["acc"] = seg(AF, 0, 8192)
                b["sm3"] = seg(AF, 8192, 8)
                b["sm4"] = seg(AF, 8200, 64)
                b["w_tm"] = seg(AF, 8264, 64)
                b["rwT_f32"] = seg(AF, 8328, 64)
                b["xT32"] = seg(AF, 8392, 128)
                b["sm"] = b["sm3"]
                b["sq"] = seg(AB, 1024, 1024)
                emit_l0_ffn(k, src_b, dst_b, W)
            elif bi == 2:
                ABS = k.buf.get("ABS_t")
                if ABS is None:
                    ABS = k.sbuf("ABS_t", [128, 897], BF16)
                    b["ABS"] = ABS
                b["v_fm"] = seg(AA, 0, 4096)
                b["fs_tm"] = seg(AA, 4096, 4096)
                b["fl_tm"] = seg(AA, 8192, 4096)
                b["v_tm"] = seg(AA, 12288, 4096)
                b["od_tm"] = seg(AA, 16384, 4096)
                b["u4"] = seg(AA, 20480, 4096)
                b["w4"] = seg(AA, 24576, 4096)
                b["xw_bf"] = seg(AA, 28672, 4096)
                b["o_fm"] = seg(AA, 0, 4096)
                b["ybuf"] = seg(AA, 4096, 4096)
                b["xbf3"] = seg(AA, 8192, 1024)
                b["q_fm"] = seg(AB, 0, 4096)
                b["k_fm"] = seg(AB, 4096, 4096)
                b["kn_tm"] = seg(AB, 8192, 4096)
                b["w_fm2"] = seg(AB, 12288, 4096)
                b["wst"] = seg(AB, 16384, 2048)
                b["h1b"] = seg(AB, 18432, 4096)
                b["Sbf"] = seg(AB, 18432, 2048)
                b["convst"] = seg(AB, 18432, 520)
                b["sqst"] = seg(AB, 18432, 1024)
                b["kst"] = seg(AB, 18432, 1024)
                b["odst"] = seg(AB, 19456, 1024)
                b["A4"] = seg(AB, 18432, 128)
                b["Pc"] = seg(AB, 18560, 128)
                b["PTc"] = seg(AB, 18688, 128)
                b["a4f"] = seg(AB, 18816, 128)
                b["c_bd"] = seg(AB, 18944, 128)
                b["w1bT"] = AB[0:16, 0:1024]
                b["stats_bf"] = AB[0:16, 1024:1536]
                b["lgbf"] = AB[0:4, 4096:4608]
                b["mixs"] = seg(AB, 4608, 256)
                b["beta_bf"] = AB[0:4, 8192:8704]
                b["rstd_bf"] = AB[0:1, 8704:9216]
                b["siltmp"] = seg(AB, 19456, 512)
                b["btm"] = seg(AF, 7672, 16)
                b["nbt"] = seg(AF, 7688, 16)
                b["A4b"] = seg(AB, 19072, 128)
                b["Pc2"] = seg(AB, 19200, 128)
                b["PTc2"] = seg(AB, 19328, 128)
                b["vx"] = seg(AB, 8192, 4600)
                b["dstage"] = seg(AB, 12800, 2048)
                b["Sf"] = seg(AF, 0, 2048)
                b["beta_f"] = AF[0:4, 2048:2560]
                b["strow"] = AF[0:1, 2560:4608]
                b["fin"] = AF[0:1, 4608:6656]
                b["sq"] = seg(AF, 4608, 1024)
                b["lnin"] = seg(AF, 5632, 1024)
                b["sm"] = seg(AF, 6656, 8)
                b["sm2"] = seg(AF, 6664, 64)
                b["onwb"] = seg(AF, 6728, 256)
                b["convw"] = seg(AF, 6984, 96)
                b["firws"] = seg(AF, 7080, 40)
                b["firwl"] = seg(AF, 7120, 512)
                b["gsm"] = seg(AF, 7632, 16)
                b["gb1c"] = seg(AF, 7648, 8)
                b["lgtm"] = seg(AF, 7656, 16)
                emit_l1_attn(k, src_b, dst_b, W)
            elif bi == 3:
                b["he"] = seg(AA, 0, 4096)
                b["s2cW"] = seg(AA, 4096, 1024)
                b["c2sW"] = seg(AA, 5120, 2048)
                b["we1"] = seg(AA, 8192, 8192)
                b["smo"] = seg(AA, 8192, 8192)
                b["we2"] = seg(AA, 16384, 4096)
                b["contb"] = seg(AA, 20480, 512)
                b["outfm"] = seg(AA, 20992, 512)
                b["outst"] = seg(AA, 21504, 64)
                b["spk"] = seg(AB, 0, 8192)
                b["decW"] = seg(AB, 0, 16384)
                b["wst2"] = seg(AB, 16384, 2048)
                b["sqb"] = seg(AB, 18432, 1024)
                b["contf"] = seg(AF, 0, 512)
                b["r1f"] = seg(AF, 512, 512)
                b["rlogf"] = seg(AF, 1024, 512)
                b["acc2"] = seg(AF, 1536, 256)
                b["w_tm2"] = seg(AF, 1792, 32)
                b["dect"] = seg(AF, 1824, 1024)
                b["lnin2"] = seg(AF, 2848, 1024)
                b["sm"] = seg(AF, 3872, 8)
                b["sm5"] = seg(AF, 3880, 64)
                b["b1e"] = seg(AF, 3944, 64)
                b["be2b"] = seg(AF, 4008, 512)
                b["encb"] = seg(AF, 4520, 16)
                b["s2cb"] = seg(AF, 4536, 1)
                b["rb1c"] = seg(AF, 4537, 1)
                b["rb2c"] = seg(AF, 4538, 1)
                b["c2sb"] = seg(AF, 4539, 16)
                b["decbb"] = seg(AF, 4555, 1024)
                b["lng"] = seg(AF, 5579, 1024)
                b["lnb"] = seg(AF, 6603, 1024)
                b["rw1f"] = seg(AF, 7627, 64)
                b["rw2f"] = seg(AF, 7691, 8)
                b["sq"] = b["sqb"]
                emit_l1_ffn(k, src_b, dst_b, W)

        for snm in ("d_st", "d_x", "d_cst", "d_wg", "d_wo", "d_w1s0", "d_w1s1", "d_w1s2", "d_w1s3", "d_w2", "d_ws0", "d_ws1", "d_st2", "d_w1b", "d_we1", "d_dec", "d_cvi0", "d_cvi1", "d_cvo0", "d_cvo1", "d_fd0", "d_fd1", "d_sb"):
            if snm in k._sems:
                k.wait(k.sp, snm, k.val(snm))
    return nc, W



def _rep(vec, n=128):
    return np.ascontiguousarray(np.broadcast_to(np.asarray(vec, np.float32).reshape(1, -1), (n, np.asarray(vec).size)))


def prep_l0a(inputs):
    f = np.float32
    w = {}
    w["wvT"] = np.ascontiguousarray(inputs["L0_attn_W_v"].T.astype(f, copy=False))
    w["woT"] = np.ascontiguousarray(inputs["L0_attn_W_out"].T.astype(f, copy=False))
    w["wshT"] = np.ascontiguousarray(inputs["L0_attn_W_shell"].T.astype(f, copy=False))
    w["wchT"] = np.ascontiguousarray(inputs["L0_attn_W_charge"].T.astype(f, copy=False))
    w["omgT"] = np.ascontiguousarray(inputs["L0_attn_omega"].T.astype(f, copy=False))
    w["wcmT"] = np.ascontiguousarray(inputs["L0_attn_W_combine"].T.astype(f, copy=False))
    w["r_bv"] = _rep(inputs["L0_attn_b_v"])
    w["r_bout"] = _rep(inputs["L0_attn_b_out"])
    w["r_bsh"] = _rep(inputs["L0_attn_b_shell"])
    w["r_rffb"] = _rep(inputs["L0_attn_rff_bias"])
    w["r_bch"] = _rep(inputs["L0_attn_b_charge"])
    w["r_lt"] = _rep([inputs["L0_attn_log_temp"]])
    dm = np.arange(512, dtype=np.float64)
    w["c_brow"] = _rep(np.exp(0.03 * dm).astype(f))
    dn = np.arange(128, dtype=np.float64)
    w["c_colb"] = np.ascontiguousarray(np.exp(-0.03 * dn).astype(f).reshape(128, 1) *
                                       np.ones((1, 1), f))
    coeffs = np.asarray([(-1.0) ** m / math.factorial(m) for m in range(5)], np.float64)
    mags = np.sqrt(np.abs(coeffs) + 1e-10)
    rff_scale = math.sqrt(2.0 / 64.0) * math.sqrt(math.exp(-0.5))
    magcol = np.zeros((320,), np.float64)
    for c in range(5):
        magcol[c * 64:(c + 1) * 64] = mags[c] * rff_scale
    mags3 = np.ones((128, 3), f)
    mags3[:, 0] = magcol[0:128].astype(f)
    mags3[:, 1] = magcol[128:256].astype(f)
    mags3[0:64, 2] = magcol[256:320].astype(f)
    w["c_mags"] = np.ascontiguousarray(mags3)
    B4 = np.zeros((128, 2048), np.float32)
    for dvar in range(4):
        delta = dvar * 128
        dnn = np.arange(128).reshape(128, 1)
        dmm = np.arange(512).reshape(1, 512)
        B4[:, dvar * 512:(dvar + 1) * 512] = np.where(dmm > delta + dnn, -1e9, 0.0)
    w["c_B4"] = B4.astype(BF)
    w["c_id"] = np.eye(128, dtype=BF)
    return w


def prep_l0f(inputs):
    f = np.float32
    w = {}
    w["rwT32"] = np.ascontiguousarray(inputs["L0_ffn_router_w"].T.astype(f, copy=False))
    w1T = inputs["L0_ffn_w1"].transpose(0, 2, 1)
    w["w1t"] = np.ascontiguousarray(
        w1T.reshape(8, 8, 128, 32, 128).transpose(0, 3, 2, 1, 4).astype(f, copy=False))
    w["w2T"] = np.ascontiguousarray(inputs["L0_ffn_w2"].transpose(0, 2, 1).astype(f, copy=False))
    b1 = inputs["L0_ffn_b1"].astype(f, copy=False)
    b1c = np.zeros((128, 256), f)
    for e in range(8):
        for fc in range(32):
            b1c[:, e * 32 + fc] = b1[e, fc * 128:(fc + 1) * 128]
    w["b1c"] = b1c
    w["rb2"] = np.ascontiguousarray(
        np.broadcast_to(inputs["L0_ffn_b2"].astype(f, copy=False)[:, None, :], (8, 128, 1024)))
    w["b2r"] = np.ascontiguousarray(inputs["L0_ffn_b2"].astype(f, copy=False))
    return w


def prep_l1a(inputs):
    f = np.float32

    def tile8(wt):
        return np.ascontiguousarray(wt.reshape(8, 128, 8, 128).transpose(2, 1, 0, 3).astype(f, copy=False))

    w = {}
    w["wqt"] = tile8(inputs["L1_attn_Wq"].T)
    w["wkt"] = tile8(inputs["L1_attn_Wk"].T)
    w["wvt"] = tile8(inputs["L1_attn_Wv"].T)
    w["w1at"] = tile8(np.ascontiguousarray(inputs["L1_attn_gate_W1"][:, 0:1024].T))
    w["wot"] = tile8(inputs["L1_attn_Wo"].T)
    wb = inputs["L1_attn_Wb"].astype(f)
    a = np.zeros((128, 32), f)
    for sl in range(8):
        for m in range(4):
            a[:, sl * 4 + m] = wb[m, sl * 128:(sl + 1) * 128]
    w["wbTs"] = a
    g2 = inputs["L1_attn_gate_W2"].astype(f)
    a = np.zeros((128, 32), f)
    for sl in range(8):
        for m in range(4):
            a[:, sl * 4 + m] = g2[m, sl * 128:(sl + 1) * 128]
    w["w2gT"] = a
    w["w1bT"] = np.ascontiguousarray(inputs["L1_attn_gate_W1"][:, 1024:1040].T.astype(f))
    cv_ = np.zeros((128, 96), f)
    for ti, nm in enumerate(("conv_q_w", "conv_k_w", "conv_v_w")):
        cw = inputs[f"L1_attn_{nm}"].astype(f)
        for c in range(8):
            for j in range(4):
                cv_[:, ti * 32 + c * 4 + j] = cw[c * 128:(c + 1) * 128, j]
    w["c_conv"] = cv_
    fs = inputs["L1_attn_fir_short_filt"].reshape(1024, 5).astype(f)
    a = np.zeros((128, 40), f)
    for sl in range(8):
        for j in range(5):
            a[:, sl * 5 + j] = fs[sl * 128:(sl + 1) * 128, j]
    w["c_firs"] = a
    fl = inputs["L1_attn_fir_long_filt"].reshape(1024, 64).astype(f)
    a = np.zeros((128, 512), f)
    for sl in range(8):
        for j in range(64):
            a[:, sl * 64 + j] = fl[sl * 128:(sl + 1) * 128, j]
    w["c_firl"] = a
    w["r_onw"] = _rep(inputs["L1_attn_o_norm_w"])
    w["r_glt"] = _rep(inputs["L1_attn_gate_log_temp"])
    w["r_gcb"] = _rep(inputs["L1_attn_gate_copy_bias"])
    gb2 = np.zeros((128, 1), f)
    gb2[0:4, 0] = inputs["L1_attn_gate_b2"].astype(f)
    w["gb2col"] = gb2
    gb1 = inputs["L1_attn_gate_b1"].astype(f)
    a = np.zeros((128, 8), f)
    for c in range(8):
        a[:, c] = gb1[c * 128:(c + 1) * 128]
    w["gb1c"] = a
    bd = np.zeros((128, 128), f)
    for blk in range(2):
        bd[blk * 64:(blk + 1) * 64, blk * 64:(blk + 1) * 64] = np.tril(np.ones((64, 64), f), -1)
    w["c_bd"] = bd.astype(BF)
    tri = np.triu(np.ones((64, 64), f))
    w["c_triu"] = np.tile(tri, (2, 1)).astype(BF)
    idx = np.arange(128)
    fds = np.zeros((8, 5, 128, 128), f)
    fdl = np.zeros((8, 64, 128, 128), f)
    for sl in range(8):
        for j in range(5):
            fds[sl, j, idx, idx] = fs[sl * 128:(sl + 1) * 128, j]
        for j in range(64):
            fdl[sl, j, idx, idx] = fl[sl * 128:(sl + 1) * 128, j]
    w["c_fds"] = fds.astype(BF)
    w["c_fdl"] = fdl.astype(BF)
    return w


def prep_l1f(inputs):
    f = np.float32
    w = {}
    encW = inputs["L1_ffn_enc_W"].astype(f)
    w["enct"] = np.ascontiguousarray(encW.reshape(8, 128, 16, 128).transpose(2, 1, 0, 3))
    s2c = inputs["L1_ffn_s2c_W"].astype(f)
    a = np.zeros((128, 1024), f)
    for sl in range(16):
        a[:, sl * 64:(sl + 1) * 64] = s2c[sl * 128:(sl + 1) * 128, :]
    w["s2ct"] = a
    w["c2s_W"] = np.ascontiguousarray(inputs["L1_ffn_c2s_W"].astype(f))
    w["we1r"] = np.ascontiguousarray(inputs["L1_ffn_We1"].transpose(1, 0, 2).reshape(64, 8192).astype(f))
    we2 = inputs["L1_ffn_We2"].astype(f)
    a = np.zeros((128, 4096), f)
    for e in range(8):
        for c in range(8):
            a[:, (e * 8 + c) * 64:(e * 8 + c + 1) * 64] = we2[e, c * 128:(c + 1) * 128, :]
    w["we2t"] = a
    w["dec_W"] = np.ascontiguousarray(inputs["L1_ffn_dec_W"].astype(f))
    w["rW1"] = np.ascontiguousarray(inputs["L1_ffn_rW1"].astype(f))
    w["rW2"] = np.ascontiguousarray(inputs["L1_ffn_rW2"].astype(f))
    be1 = inputs["L1_ffn_be1"].astype(f)
    a = np.zeros((128, 64), f)
    for e in range(8):
        for c in range(8):
            a[:, e * 8 + c] = be1[e, c * 128:(c + 1) * 128]
    w["b1e"] = a
    be2 = inputs["L1_ffn_be2"].astype(f)
    a = np.zeros((128, 512), f)
    for e in range(8):
        a[:, e * 64:(e + 1) * 64] = np.broadcast_to(be2[e], (128, 64))
    w["be2b"] = a
    encb = inputs["L1_ffn_enc_b"].astype(f)
    a = np.zeros((128, 16), f)
    for c in range(16):
        a[:, c] = encb[c * 128:(c + 1) * 128]
    w["encbc"] = a
    a = np.zeros((128, 1), f); a[0:64, 0] = inputs["L1_ffn_s2c_b"].astype(f)
    w["s2cbc"] = a
    a = np.zeros((128, 1), f); a[0:64, 0] = inputs["L1_ffn_rb1"].astype(f)
    w["rb1c"] = a
    a = np.zeros((128, 1), f); a[0:8, 0] = inputs["L1_ffn_rb2"].astype(f)
    w["rb2c"] = a
    c2sb = inputs["L1_ffn_c2s_b"].astype(f)
    a = np.zeros((128, 16), f)
    for c in range(16):
        a[:, c] = c2sb[c * 128:(c + 1) * 128]
    w["c2sbc"] = a
    w["r_decb"] = _rep(inputs["L1_ffn_dec_b"])
    w["r_lng"] = _rep(inputs["L1_ffn_ln_g"])
    w["r_lnb"] = _rep(inputs["L1_ffn_ln_b"])
    return w


def prep_all(inputs):
    w = prep_l0a(inputs)
    w["c_idf"] = np.eye(128, dtype=np.float32)
    w.update(prep_l0f(inputs))
    w.update(prep_l1a(inputs))
    w.update(prep_l1f(inputs))
    return w


_CACHE = {}
_BLOCKS = (0, 1, 2, 3)


def kernel(**inputs):
    key = ("prog", _BLOCKS)
    if key not in _CACHE:
        _CACHE[key] = build_program(blocks=_BLOCKS)
    nc, _ = _CACHE[key]
    shared = prep_all(inputs)
    X = np.ascontiguousarray(inputs["L0_attn_x"].astype(np.float32, copy=False))
    in_maps = []
    for c in range(8):
        m = dict(shared)
        m["x"] = np.ascontiguousarray(X[c])
        in_maps.append(m)
    res = run_bass_kernel_spmd(nc, in_maps, core_ids=list(range(8)))
    return np.stack([r["out"] for r in res.results])
